# version 10
# baseline (speedup 1.0000x reference)
import numpy as np
import ml_dtypes
from contextlib import ExitStack

import concourse.bass as bass
import concourse.mybir as mybir
from concourse.bass_utils import run_bass_kernel_spmd

F32 = mybir.dt.float32
BF16 = mybir.dt.bfloat16
I16 = mybir.dt.int16
I32 = mybir.dt.int32
AF = mybir.ActivationFunctionType
ALU = mybir.AluOpType
AX = mybir.AxisListType.X

NC_N = 8
D = 1024
H = 16
DH = 64
DLAT = 256
E = 16
FF = 2048
LN_EPS = 1e-5
NBIS = 46


def make_cfg(SB=8192, B=2, phases=("p0", "p1", "p2")):
    c = {"SB": SB, "B": B}
    c["TC"] = SB * B // NC_N
    c["CAP"] = SB // 8
    c["TILES"] = c["TC"] // 128
    c["phases"] = phases
    return c


def build_nc(cfg, debug=()):
    SB, B, TC, CAP, TILES = (cfg[k] for k in ("SB", "B", "TC", "CAP", "TILES"))
    phases = cfg["phases"]
    TT = B * SB
    GRP = 512
    NG = TT // GRP
    NGS = TC // GRP
    CHB = SB // 128
    RNK = TC

    nc = bass.Bass("TRN2", target_bir_lowering=False, debug=False)

    def din(name, shape, dt=F32):
        return nc.dram_tensor(name, shape, dt, kind="ExternalInput").ap()

    x_in = din("x", [TC, D])
    Wq = din("Wq", [D, D]); bq = din("bq", [1, D])
    Wdn = din("Wdn", [D, DLAT]); bdn = din("bdn", [1, DLAT])
    Wup = din("Wup", [DLAT, 2 * D]); bup = din("bup", [1, 2 * D])
    Wo = din("Wo", [D, D]); bo = din("bo", [1, D])
    gateT = din("gateT", [D, E])
    WsgT = din("WsgT", [D, FF]); WsuT = din("WsuT", [D, FF]); WsdT = din("WsdT", [FF, D])
    WgT = [din(f"Wg{i}T", [D, FF]) for i in range(2)]
    WuT = [din(f"Wu{i}T", [D, FF]) for i in range(2)]
    WdT = [din(f"Wd{i}T", [FF, D]) for i in range(2)]
    eh_in = din("eh_in", [128, 4 * E])
    LT_in = din("LT_in", [128, 128])
    iotar_in = din("iotar_in", [128, 256])
    idm_in = din("idm_in", [128, 128])

    out = nc.dram_tensor("out", [TC, D], F32, kind="ExternalOutput").ap()
    dbg = {}
    def dout(name, shape, dt=F32):
        if name in debug:
            dbg[name] = nc.dram_tensor("d_" + name, shape, dt, kind="ExternalOutput").ap()
        return dbg.get(name)

    d_xp = dout("xp", [TC, D])
    d_yh = dout("yh", [TC, D], BF16)
    d_tau = dout("tau", [1, 24])
    d_wsc = dout("wsc", [128, 4 * CHB])
    d_scj = dout("scj", [128, 4 * CHB])
    d_poss = dout("poss", [4 * 128, 192])
    d_mkb = dout("mkb", [128, 4 * CHB], BF16)

    xp_dram = nc.dram_tensor("xp_dram", [TC, D], F32).ap()
    yTd = nc.dram_tensor("yTd", [D, TC], BF16).ap()
    ag1y_i = nc.dram_tensor("ag1y_i", [TC, D], BF16).ap()
    y_all = nc.dram_tensor("y_all", [TT, D], BF16).ap()
    pcb_d = nc.dram_tensor("pcb_d", [128 * (SB // 128)], F32).ap()
    scd = nc.dram_tensor("scd", [TC, E], F32).ap()
    sc_all = nc.dram_tensor("sc_all", [NC_N * TC, E], F32).ap()
    taux_d = nc.dram_tensor("taux_d", [512], F32).ap()
    acc_d = nc.dram_tensor("acc_d", [TT, D], BF16).ap()
    rs_out = nc.dram_tensor("rs_out", [TC, D], BF16).ap()
    acc_sh = nc.dram_tensor("acc_sh", [TC, D], BF16).ap()

    es = ExitStack()
    sb = lambda n, s, dt: es.enter_context(nc.sbuf_tensor(n, s, dt))

    BUF_A = sb("BUF_A", [128, 49152], BF16)
    BUF_B = sb("BUF_B", [128, 28672], BF16)
    A_, B_ = BUF_A.ap(), BUF_B.ap()
    wq_b = A_[:, 0:8 * D]
    wo_b = A_[:, 8 * D:16 * D]
    wup_b = A_[:, 16 * D:20 * D]
    wdn_b = A_[:, 20 * D:22 * D]
    REG = A_[:, 22528:22528 + 12 * D]
    EXT = A_[:, 34816:49152]
    xs = [REG[:, 0:2 * D].bitcast(F32), REG[:, 2 * D:4 * D].bitcast(F32),
          EXT[:, 0:2 * D].bitcast(F32)]
    aof = REG[:, 4 * D:6 * D].bitcast(F32)
    kvbP = [REG[:, 6 * D:8 * D], EXT[:, 2 * D:4 * D]]
    xhat = REG[:, 8 * D:9 * D]
    xhatT = REG[:, 9 * D:10 * D]
    qbP = [REG[:, 10 * D:11 * D], EXT[:, 4 * D:5 * D]]
    obT = EXT[:, 5 * D:6 * D]
    yhb = REG[:, 11 * D:12 * D]
    latb = B_[:, 0:DLAT]
    latT = B_[:, DLAT:2 * DLAT]
    scf = B_[:, 512:1024].bitcast(F32)
    Eb = B_[:, 1024:1280]
    ob = B_[:, 1280:2304]
    yhT = B_[:, 2304:3328]
    wgt_b = A_[:, 0:16384]
    wut_b = A_[:, 16384:32768]
    wdt_b = A_[:, 32768:49152]
    yt_g = [B_[:, 0:4096], B_[:, 4096:8192]]
    hb_g = B_[:, 8192:16384]
    xTs = B_[:, 0:16384]
    hb2 = B_[:, 16384:24576]
    eoI = B_[:, 24576:28672]
    at = [B_[:, 0:D], B_[:, D:2 * D]]
    ot2 = B_[:, 2 * D:4 * D].bitcast(F32)

    wg8_b = sb("wg8_b", [128, 8 * E], BF16)
    bq_b = sb("bq_b", [1, D], BF16)
    bdn_b = sb("bdn_b", [1, DLAT], BF16)
    bup_b = sb("bup_b", [1, 2 * D], BF16)
    bo_b = sb("bo_b", [1, D], BF16)
    ones = sb("ones", [1, 128], BF16)
    ones_col = sb("ones_col", [128, 1], BF16)
    stg = sb("stg0", [128, 1024], F32)
    stg_f32 = stg.ap()
    gt_b = [sb(f"gt_b{i}", [128, GRP], BF16) for i in range(2)]
    eo_b = sb("eo_b", [128, 4 * D], BF16)
    acb = sb("acb", [128, D], BF16)
    pacc = sb("pacc", [128, D], F32)
    ptmp = sb("ptmp", [128, D], F32)
    ao2 = sb("ao2", [128, D], F32)
    xpf = sb("xpf", [128, D], F32)
    lgf = sb("lgf", [128, 32], F32)
    stats = sb("stats", [128, 16], F32)
    stats2 = sb("stats2", [128, 2 * H], F32)
    epsb = sb("epsb", [128, 1], F32)
    sc_b = [B_[:, 16384 + i * 4096: 16384 + (i + 1) * 4096].bitcast(F32)[:, 0:CHB * E] for i in range(B)]
    scj = sb("scj", [128, 4 * CHB], F32)
    wsc = sb("wsc", [128, 4 * CHB], F32)
    mk_b = sb("mk_b", [128, 4 * CHB], BF16)
    eh_s = sb("eh_s", [128, 4 * E], F32)
    cand128 = sb("cand128", [128, 4], F32)
    iotar_s = sb("iotar_s", [128, 256], F32)
    ltb = sb("ltb", [128, 128], BF16)
    idm_b = sb("idm_b", [128, 128], BF16)
    posc = sb("posc", [128, 64], F32)
    poss = sb("poss", [128, 64], F32)
    cb128 = sb("cb128", [128, 64], F32)
    Pb4 = sb("Pb4", [128, 512], BF16)
    Pb2 = [sb(f"Pb2_{i}", [128, 128], BF16) for i in range(4)]
    tau128 = sb("tau128", [128, 4], F32)
    tt = sb("tt", [1, 24], F32)
    mk16 = sb("mk16", [1, 4], I16)
    bc1 = sb("bc1", [1, 512], F32)

    pss = [es.enter_context(nc.psum_tensor(f"ps{i}", [128, 512], F32, bank=i)) for i in range(6)]
    ptt = es.enter_context(nc.psum_tensor("ptt", [128, 128], BF16, bank=6))
    pcc = es.enter_context(nc.psum_tensor("pcc", [1, 384], F32, bank=7))

    sems = {}
    for s in ("sw", "sx0", "sx1", "sx2", "sst", "slq", "slq2", "syt0", "syt1", "ssc", "stx",
              "sac", "sar", "sfin", "sv", "ss", "stn", "scc"):
        sems[s] = es.enter_context(nc.semaphore(s))
    C = {s: 0 for s in sems}
    ps_free = [None] * 6
    pcc_free = [None]
    ptt_free = [None]
    streams = {k: [] for k in ("sync", "scalar", "vector", "tensor", "gpsimd")}

    def W(eng, sem, val=None):
        v = C[sem] if val is None else val
        streams[eng].append(lambda e: e.wait_ge(sems[sem], v))

    def INC(sem, n=1):
        C[sem] += n
        return sems[sem], C[sem]

    def dma(q, sem, out_ap, in_ap, **kw):
        s, v = INC(sem, 16)
        streams[q].append(lambda e: e.dma_start(out=out_ap, in_=in_ap, **kw).then_inc(s, 16))
        return v

    def ev(eng, fn, sem=None):
        if sem is None:
            streams[eng].append(fn)
            return None
        s, v = INC(sem, 1)
        streams[eng].append(lambda e: fn(e).then_inc(s, 1))
        return v

    def drain(eng):
        streams[eng].append(lambda e: e.drain())

    def ln_norm(src, dst, junk):
        st = lambda i: stats[:, i:i + 1]
        drain("vector")
        ev("vector", lambda e: e.reduce_sum(st(0), src, axis=AX))
        ev("vector", lambda e: e.scalar_tensor_tensor(junk, src, 1.0, src, op0=ALU.mult, op1=ALU.mult, accum_out=st(1)))
        drain("vector")
        ev("vector", lambda e: e.tensor_scalar(st(2), st(0), 1.0 / D, None, op0=ALU.mult))
        ev("vector", lambda e: e.tensor_scalar(st(3), st(1), 1.0 / D, None, op0=ALU.mult))
        drain("vector")
        ev("vector", lambda e: e.tensor_tensor(st(4), st(2), st(2), op=ALU.mult))
        drain("vector")
        vs = ev("vector", lambda e: e.tensor_sub(st(5), st(3), st(4)), sem="sv")
        W("scalar", "sv", vs)
        sq = ev("scalar", lambda e: e.activation(st(6), st(5), AF.Sqrt, bias=epsb[:, :]), sem="ss")
        W("vector", "ss", sq)
        ev("vector", lambda e: e.reciprocal(st(7), st(6)))
        drain("vector")
        ev("vector", lambda e: e.tensor_scalar(dst, src, st(2), st(7), op0=ALU.subtract, op1=ALU.mult))
        drain("vector")

    def free_ps(i):
        if ps_free[i] is not None:
            s, v = ps_free[i]
            W("tensor", s, v)
            ps_free[i] = None

    def mm_acc(slot, pairs, bias_rhs=None, n=512, sem=None):
        free_ps(slot)
        p = pss[slot].ap()[:, 0:n]
        ret = None
        for i, (l, r) in enumerate(pairs):
            first = (i == 0)
            last = (i == len(pairs) - 1) and bias_rhs is None
            fn = (lambda l=l, r=r, f=first, la=last: (lambda e: e.matmul(p, l, r, start=f, stop=la)))()
            if last and sem is not None:
                ret = ev("tensor", fn, sem=sem)
            else:
                ev("tensor", fn)
        if bias_rhs is not None:
            fn = (lambda br=bias_rhs: (lambda e: e.matmul(p, ones.ap()[:, 0:128], br, start=False, stop=True)))()
            if sem is not None:
                ret = ev("tensor", fn, sem=sem)
            else:
                ev("tensor", fn)
        return ret

    ev("vector", lambda e: e.memset(ones.ap(), 1.0))
    ev("vector", lambda e: e.memset(ones_col.ap(), 1.0))
    ev("vector", lambda e: e.memset(epsb.ap(), LN_EPS))
    ev("vector", lambda e: e.memset(tt.ap(), 0.0))
    drain("vector")
    ev("vector", lambda e: e.memset(tt.ap()[0:1, 4:8], 1.0))
    vz = ev("vector", lambda e: e.drain(), sem="sv")
    W("sync", "sv", vz)

    CHW = 1024

    def load_cast(dram, dst):
        R, Ccols = dram.shape
        nchunks = R // 128
        src_r = dram.rearrange("(c p) o -> p c o", p=128)
        for c0 in range(nchunks):
            o0 = 0
            while o0 < Ccols:
                take = min(CHW, Ccols - o0)
                v = dma("sync", "sw", stg_f32[:, 0:take], src_r[:, c0, o0:o0 + take])
                W("scalar", "sw", v)
                off = c0 * Ccols + o0
                dv = ev("scalar", lambda e, take=take, off=off, dst=dst: e.copy(dst[:, off:off + take], stg_f32[:, 0:take]), sem="ss")
                W("sync", "ss", dv)
                o0 += take

    load_cast(Wq, wq_b)
    load_cast(Wdn, wdn_b)
    load_cast(Wup, wup_b)
    load_cast(Wo, wo_b)
    load_cast(gateT, wg8_b.ap())
    for (bsrc, bdst) in ((bq, bq_b), (bdn, bdn_b), (bup, bup_b), (bo, bo_b)):
        BW = bsrc.shape[1]
        for o0 in range(0, BW, 1024):
            tk = min(1024, BW - o0)
            v = dma("sync", "sw", stg_f32[0:1, 0:tk], bsrc[:, o0:o0 + tk])
            W("vector", "sw", v)
            ev("vector", lambda e, bdst=bdst, o0=o0, tk=tk: e.tensor_copy(bdst.ap()[:, o0:o0 + tk], stg_f32[0:1, 0:tk]))
            dv = ev("vector", lambda e: e.drain(), sem="sv")
            W("sync", "sv", dv)
    dma("sync", "sw", eh_s.ap(), eh_in)
    dma("sync", "sw", iotar_s.ap(), iotar_in)
    load_cast(LT_in, ltb.ap())
    load_cast(idm_in, idm_b.ap())
    wcast_s = ev("scalar", lambda e: e.copy(stats[0:1, 14:15], stg_f32[0:1, 0:1]), sem="ss")
    wcast = ev("vector", lambda e: e.drain(), sem="sv")

    SXS = ("sx0", "sx1", "sx2")
    xld = [None, None, None]
    for i in range(min(3, TILES)):
        xld[i] = dma("sync", SXS[i], xs[i][:, :], x_in[i * 128:(i + 1) * 128, :])
    kv_pend = [None]
    q_pend = [None]

    def emit_X(t):
        cur = xs[t % 3]
        qb = qbP[t % 2]
        W("vector", SXS[t % 3], xld[t % 3])
        W("vector", "sst")
        ln_norm(cur, xhat, pacc.ap())
        vx = ev("vector", lambda e: e.engine_nop(), sem="sv")
        W("scalar", "sv", vx)
        W("scalar", "sst")
        W("scalar", "stn", C["stn"])
        for k in range(8):
            tv = dma("scalar", "slq2", xhatT[:, k * 128:(k + 1) * 128],
                     xhat[:, k * 128:(k + 1) * 128], transpose=True)
        W("tensor", "slq2", tv)
        W("tensor", "sv", wcast)
        W("tensor", "ss", wcast_s)
        for hh in range(2):
            tq = mm_acc(hh, [(xhatT[:, k * 128:(k + 1) * 128],
                         wq_b[:, k * D + hh * 512: k * D + hh * 512 + 512]) for k in range(8)],
                   bias_rhs=bq_b.ap()[:, hh * 512: hh * 512 + 512], sem="stn")
        tl = mm_acc(2, [(xhatT[:, k * 128:(k + 1) * 128],
                    wdn_b[:, k * DLAT: (k + 1) * DLAT]) for k in range(8)],
               bias_rhs=bdn_b.ap()[:, :], n=DLAT, sem="stn")
        W("vector", "stn", tl)
        vq = ev("vector", lambda e: e.tensor_copy(latb[:, :], pss[2].ap()[:, 0:DLAT]), sem="sv")
        ps_free[2] = ("sv", vq)
        W("scalar", "sv", vq)
        for k in range(2):
            tv = dma("scalar", "slq2", latT[:, k * 128:(k + 1) * 128],
                     latb[:, k * 128:(k + 1) * 128], transpose=True)
        W("tensor", "slq2", tv)
        kvslot = (2, 3, 4, 5)
        for ch in range(4):
            tk2 = mm_acc(kvslot[ch], [(latT[:, k * 128:(k + 1) * 128],
                             wup_b[:, k * 2 * D + ch * 512: k * 2 * D + ch * 512 + 512]) for k in range(2)],
                   bias_rhs=bup_b.ap()[:, ch * 512: ch * 512 + 512], sem="stn")
        kv_pend[0] = (t, tk2)
        q_pend[0] = (t, tq)

    def emit_qkv_evac():
        if kv_pend[0] is None:
            return
        tn, tk2 = kv_pend[0]
        _, tq = q_pend[0]
        qb = qbP[tn % 2]
        kvb = kvbP[tn % 2]
        kvslot = (2, 3, 4, 5)
        W("vector", "stn", tq)
        ev("vector", lambda e, qb=qb: e.tensor_copy(qb[:, 0:512], pss[0].ap()[:, 0:512]))
        ev("vector", lambda e, qb=qb: e.tensor_copy(qb[:, 512:1024], pss[1].ap()[:, 0:512]))
        W("vector", "stn", tk2)
        for ch in range(4):
            ev("vector", lambda e, ch=ch, kvb=kvb, kvslot=kvslot: e.tensor_copy(
                kvb[:, ch * 512:(ch + 1) * 512], pss[kvslot[ch]].ap()[:, 0:512]))
        vkv = ev("vector", lambda e: e.drain(), sem="sv")
        ps_free[0] = ("sv", vkv); ps_free[1] = ("sv", vkv)
        for ch in range(4):
            ps_free[kvslot[ch]] = ("sv", vkv)
        kv_pend[0] = None

    def emit_Y(t):
        cur = xs[t % 3]
        qb = qbP[t % 2]
        kvb = kvbP[t % 2]
        W("vector", "sst")
        q3 = qb.rearrange("p (h d) -> p h d", d=DH)
        p3 = ptmp.ap().rearrange("p (h d) -> p h d", d=DH)
        a3 = pacc.ap().rearrange("p (h d) -> p h d", d=DH)
        sc3 = scf.rearrange("p (h g) -> p h g", g=H)
        for g in range(H):
            ksl = kvb[:, g * 128: g * 128 + DH].unsqueeze(1).to_broadcast([128, H, DH])
            ev("vector", lambda e, ksl=ksl, q3=q3: e.tensor_tensor(p3, q3, ksl, op=ALU.mult))
            drain("vector")
            ev("vector", lambda e, g=g: e.tensor_reduce(sc3[:, :, g:g + 1], p3, axis=AX, op=ALU.add))
            drain("vector")
        ev("vector", lambda e: e.reduce_max(stats[:, 8:9], scf, axis=AX))
        vmx = ev("vector", lambda e: e.drain(), sem="sv")
        W("scalar", "sv", vmx)
        ev("scalar", lambda e: e.mul(stats[:, 9:10], stats[:, 8:9], -0.125))
        drain("scalar")
        sexp = ev("scalar", lambda e: e.activation(Eb, scf, AF.Exp, bias=stats[:, 9:10], scale=0.125), sem="ss")
        W("vector", "ss", sexp)
        ev("vector", lambda e: e.tensor_reduce(stats2[:, 0:H], Eb.rearrange("p (h g) -> p h g", g=H), axis=AX, op=ALU.add))
        drain("vector")
        ev("vector", lambda e: e.reciprocal(stats2[:, H:2 * H], stats2[:, 0:H]))
        drain("vector")
        E3 = Eb.rearrange("p (h g) -> p h g", g=H)
        for g in range(H):
            esl = E3[:, :, g:g + 1].to_broadcast([128, H, DH])
            vsl = kvb[:, g * 128 + DH: g * 128 + 128].unsqueeze(1).to_broadcast([128, H, DH])
            if g == 0:
                ev("vector", lambda e, esl=esl, vsl=vsl: e.tensor_tensor(a3, esl, vsl, op=ALU.mult))
            else:
                ev("vector", lambda e, esl=esl, vsl=vsl: e.tensor_tensor(p3, esl, vsl, op=ALU.mult))
                drain("vector")
                ev("vector", lambda e: e.tensor_add(pacc.ap(), pacc.ap(), ptmp.ap()))
            drain("vector")
        ev("vector", lambda e: e.tensor_tensor(
            ob.rearrange("p (h d) -> p h d", d=DH), a3,
            stats2[:, H:2 * H].unsqueeze(2).to_broadcast([128, H, DH]), op=ALU.mult))
        vo = ev("vector", lambda e: e.drain(), sem="sv")
        emit_qkv_evac()
        W("scalar", "sv", vo)
        for k in range(8):
            tv = dma("scalar", "slq", obT[:, k * 128:(k + 1) * 128],
                     ob[:, k * 128:(k + 1) * 128], transpose=True)
        W("tensor", "slq", tv)
        for hh in range(2):
            tao = mm_acc(hh, [(obT[:, k * 128:(k + 1) * 128],
                         wo_b[:, k * D + hh * 512: k * D + hh * 512 + 512]) for k in range(8)],
                   bias_rhs=bo_b.ap()[:, hh * 512: hh * 512 + 512], sem="stn")
        W("vector", "stn", tao)
        ev("vector", lambda e: e.tensor_copy(aof[:, 0:512], pss[0].ap()[:, 0:512]))
        vao = ev("vector", lambda e: e.tensor_copy(aof[:, 512:1024], pss[1].ap()[:, 0:512]), sem="sv")
        ps_free[0] = ("sv", vao); ps_free[1] = ("sv", vao)
        ln_norm(aof, ao2.ap(), ptmp.ap())
        ev("vector", lambda e, cur=cur: e.tensor_add(xpf.ap(), cur, ao2.ap()))
        vxp = ev("vector", lambda e: e.drain(), sem="sv")
        W("sync", "sv", vxp)
        dma("sync", "sst", xp_dram[t * 128:(t + 1) * 128, :], xpf.ap())
        if d_xp is not None:
            dma("sync", "sst", d_xp[t * 128:(t + 1) * 128, :], xpf.ap())
        ln_norm(xpf.ap(), yhb, ptmp.ap())
        vyh = ev("vector", lambda e: e.engine_nop(), sem="sv")
        W("sync", "sv", vyh)
        dma("sync", "sst", ag1y_i[t * 128:(t + 1) * 128, :], yhb)
        if d_yh is not None:
            dma("sync", "sst", d_yh[t * 128:(t + 1) * 128, :], yhb)
        W("scalar", "sv", vyh)
        for k in range(8):
            tv = dma("scalar", "slq", yhT[:, k * 128:(k + 1) * 128],
                     yhb[:, k * 128:(k + 1) * 128], transpose=True)
        W("sync", "slq", tv)
        for k in range(8):
            dma("sync", "sst", yTd[k * 128:(k + 1) * 128, t * 128:(t + 1) * 128],
                yhT[:, k * 128:(k + 1) * 128])
        W("tensor", "slq", tv)
        tlg = mm_acc(2, [(yhT[:, k * 128:(k + 1) * 128],
                    wg8_b.ap()[:, k * E:(k + 1) * E]) for k in range(8)], n=E, sem="stn")
        W("vector", "stn", tlg)
        vlg = ev("vector", lambda e: e.tensor_copy(lgf.ap()[:, 0:E], pss[2].ap()[:, 0:E]), sem="sv")
        ps_free[2] = ("sv", vlg)
        drain("vector")
        ev("vector", lambda e: e.reduce_max(stats[:, 10:11], lgf.ap()[:, 0:E], axis=AX, negate=True))
        vng = ev("vector", lambda e: e.drain(), sem="sv")
        W("scalar", "sv", vng)
        sle = ev("scalar", lambda e: e.activation(lgf.ap()[:, 16:16 + E], lgf.ap()[:, 0:E], AF.Exp, bias=stats[:, 10:11]), sem="ss")
        W("vector", "ss", sle)
        ev("vector", lambda e: e.reduce_sum(stats[:, 11:12], lgf.ap()[:, 16:16 + E], axis=AX))
        drain("vector")
        ev("vector", lambda e: e.reciprocal(stats[:, 12:13], stats[:, 11:12]))
        drain("vector")
        ev("vector", lambda e: e.tensor_scalar(lgf.ap()[:, 0:E], lgf.ap()[:, 16:16 + E], stats[:, 12:13], None, op0=ALU.mult))
        vsc = ev("vector", lambda e: e.drain(), sem="sv")
        W("sync", "sv", vsc)
        dma("sync", "sst", scd[t * 128:(t + 1) * 128, :], lgf.ap()[:, 0:E])
        if t + 3 < TILES:
            xld[(t + 3) % 3] = dma("sync", SXS[(t + 3) % 3], xs[(t + 3) % 3][:, :],
                                   x_in[(t + 3) * 128:(t + 4) * 128, :])

    if "p0" in phases:
        emit_X(0)
        emit_qkv_evac()
        for t in range(TILES):
            if t + 1 < TILES:
                emit_X(t + 1)
            emit_Y(t)

    if "p1" in phases:
        W("gpsimd", "sst")
        ev("gpsimd", lambda e: e.collective_compute(
            "AllGather", ALU.bypass, replica_groups=[list(range(NC_N))],
            ins=[scd.opt()], outs=[sc_all.opt()]), sem="scc")
        ev("gpsimd", lambda e: e.collective_compute(
            "AllGather", ALU.bypass, replica_groups=[list(range(NC_N))],
            ins=[ag1y_i.opt()], outs=[y_all.opt()]), sem="scc")
        ag_sc, ag_yt = 1, 2

        last_veo = [None]

        def emit_glu_pass(ngroups, scale_cols, accum, yt_src, out_dram, wait_scc):
            prev_mm = [None, None]
            gtv = [None, None]
            for g in range(ngroups):
                pg = g % 2
                tok0 = g * GRP
                sytk = "syt0" if pg == 0 else "syt1"
                if prev_mm[pg] is not None:
                    W("scalar", "stn", prev_mm[pg])
                rnk = tok0 // RNK
                toff = tok0 % RNK
                if wait_scc and g < 2:
                    W("scalar", "scc", ag_yt)
                vy = dma("scalar", sytk,
                         yt_g[pg].rearrange("p (c t) -> p c t", c=8),
                         yt_src[rnk * D:(rnk + 1) * D, toff:toff + GRP].rearrange("(c p) t -> p c t", p=128))
                W("tensor", sytk, vy)
                for fc in range(16):
                    sA, sB = fc % 2, 2 + fc % 2
                    free_ps(sA); free_ps(sB)
                    for dc in range(8):
                        ev("tensor", (lambda sA=sA, dc=dc, fc=fc, pg=pg: lambda e: e.matmul(
                            pss[sA].ap()[:, 0:GRP],
                            wgt_b[:, dc * FF + fc * 128: dc * FF + (fc + 1) * 128],
                            yt_g[pg][:, dc * GRP:(dc + 1) * GRP],
                            start=(dc == 0), stop=(dc == 7)))())
                    tga = None
                    for dc in range(8):
                        fn = (lambda sB=sB, dc=dc, fc=fc, pg=pg: lambda e: e.matmul(
                            pss[sB].ap()[:, 0:GRP],
                            wut_b[:, dc * FF + fc * 128: dc * FF + (fc + 1) * 128],
                            yt_g[pg][:, dc * GRP:(dc + 1) * GRP],
                            start=(dc == 0), stop=(dc == 7)))()
                        if dc == 7:
                            tga = ev("tensor", fn, sem="stn")
                        else:
                            ev("tensor", fn)
                    if fc == 15:
                        prev_mm[pg] = tga
                    W("scalar", "stn", tga)
                    if gtv[fc % 2] is not None:
                        W("scalar", "sv", gtv[fc % 2])
                    sg1 = ev("scalar", (lambda fc=fc, sA=sA: lambda e: e.activation(
                        gt_b[fc % 2].ap(), pss[sA].ap()[:, 0:GRP], AF.Gelu))(), sem="ss")
                    W("vector", "ss", sg1)
                    vh = ev("vector", (lambda fc=fc, sB=sB: lambda e: e.tensor_tensor(
                        hb_g[:, fc * GRP:(fc + 1) * GRP], gt_b[fc % 2].ap(), pss[sB].ap()[:, 0:GRP], op=ALU.mult))(), sem="sv")
                    gtv[fc % 2] = vh
                    ps_free[sA] = ("sv", C["sv"])
                    ps_free[sB] = ("sv", C["sv"])
                W("tensor", "sv", C["sv"])
                assert not accum
                first_evac = True
                for ch in range(4):
                    marks = {}
                    for half in range(2):
                        slot = (ch * 2 + half) % 6
                        free_ps(slot)
                        for fc in range(16):
                            fn = (lambda slot=slot, fc=fc, ch=ch, half=half: lambda e: e.matmul(
                                pss[slot].ap()[:, 0:512],
                                hb_g[:, fc * GRP + ch * 128: fc * GRP + (ch + 1) * 128],
                                wdt_b[:, fc * D + half * 512: fc * D + (half + 1) * 512],
                                start=(fc == 0), stop=(fc == 15)))()
                            if fc == 15:
                                marks[half] = ev("tensor", fn, sem="stn")
                            else:
                                ev("tensor", fn)
                    W("vector", "stn", marks[1])
                    if first_evac:
                        W("vector", "sac")
                        first_evac = False
                    for half in range(2):
                        slot = (ch * 2 + half) % 6
                        dst = eo_b.ap()[:, ch * D + half * 512: ch * D + (half + 1) * 512]
                        if scale_cols is not None:
                            scol = scale_cols(g, ch)
                            if accum:
                                ev("vector", (lambda dst=dst, slot=slot, scol=scol, ch=ch, half=half: lambda e: e.scalar_tensor_tensor(
                                    dst, pss[slot].ap()[:, 0:512], scol,
                                    acb.ap()[:, ch * D + half * 512: ch * D + (half + 1) * 512],
                                    op0=ALU.mult, op1=ALU.add))())
                            else:
                                ev("vector", (lambda dst=dst, slot=slot, scol=scol: lambda e: e.tensor_scalar(
                                    dst, pss[slot].ap()[:, 0:512], scol, None, op0=ALU.mult))())
                        else:
                            ev("vector", (lambda dst=dst, slot=slot: lambda e: e.tensor_copy(
                                dst, pss[slot].ap()[:, 0:512]))())
                        drain("vector")
                    vch = ev("vector", lambda e: e.drain(), sem="sv")
                    ps_free[(ch * 2) % 6] = ("sv", vch)
                    ps_free[(ch * 2 + 1) % 6] = ("sv", vch)
                veo = vch
                last_veo[0] = veo
                W("sync", "sv", veo)
                dma("sync", "sac", out_dram[tok0:tok0 + GRP, :].rearrange("(c p) d -> p c d", p=128),
                    eo_b.ap().rearrange("p (c d) -> p c d", d=D))

        W("vector", "stn")
        W("vector", "slq")
        W("vector", "sst")
        W("vector", "sx0"); W("vector", "sx1"); W("vector", "sx2")
        W("scalar", "stn"); W("scalar", "sst"); W("scalar", "slq"); W("scalar", "slq2")
        W("scalar", "sx0"); W("scalar", "sx1"); W("scalar", "sx2")
        load_cast(WsgT, wgt_b)
        load_cast(WsuT, wut_b)
        load_cast(WsdT, wdt_b)
        wsh = ev("scalar", lambda e: e.copy(stats[0:1, 14:15], stg_f32[0:1, 0:1]), sem="ss")
        W("tensor", "ss", wsh)
        W("sync", "sst")
        W("scalar", "sst")
        emit_glu_pass(NGS, None, False, yTd, acc_sh, wait_scc=False)

        W("sync", "scc", ag_sc)
        vsl = None
        for bb in range(B):
            vsl = dma("sync", "ssc", sc_b[bb].rearrange("p (c e) -> p c e", e=E),
                      sc_all[bb * SB:(bb + 1) * SB, :].rearrange("(c p) e -> p c e", p=128))
        W("vector", "ssc", vsl)
        W("vector", "sw")
        for j in range(4):
            bb = j // 2
            sc3v = sc_b[bb].rearrange("p (c e) -> p c e", e=E)
            ev("vector", (lambda j=j, sc3v=sc3v: lambda e: e.tensor_tensor(
                stg_f32.rearrange("p (c e) -> p c e", e=E)[:, 0:CHB, :], sc3v,
                eh_s.ap()[:, j * E:(j + 1) * E].unsqueeze(1).to_broadcast([128, CHB, E]), op=ALU.mult))())
            drain("vector")
            ev("vector", (lambda j=j: lambda e: e.tensor_reduce(
                scj.ap()[:, j * CHB:(j + 1) * CHB],
                stg_f32.rearrange("p (c e) -> p c e", e=E)[:, 0:CHB, :], axis=AX, op=ALU.add))())
            drain("vector")
        ev("vector", lambda e: e.drain(), sem="sv")

        tau_sl = [0, 16]
        tcm = None
        for it in range(NBIS):
            cur, nxt = tau_sl
            lo_c = tt.ap()[0:1, cur:cur + 4]
            hi_c = tt.ap()[0:1, cur + 4:cur + 8]
            lo_n = tt.ap()[0:1, nxt:nxt + 4]
            hi_n = tt.ap()[0:1, nxt + 4:nxt + 8]
            mid = tt.ap()[0:1, 8:12]
            cnt_f = tt.ap()[0:1, 12:16]
            ev("vector", (lambda mid=mid, lo_c=lo_c, hi_c=hi_c: lambda e: e.tensor_tensor(
                mid, lo_c, hi_c, op=ALU.add))())
            drain("vector")
            ev("vector", (lambda mid=mid: lambda e: e.tensor_scalar(
                mid, mid, 0.5, None, op0=ALU.mult))())
            drain("vector")
            ev("vector", (lambda mid=mid: lambda e: e.tensor_copy(
                bc1.ap().rearrange("o (p j) -> o p j", j=4),
                mid.unsqueeze(1).to_broadcast([1, 128, 4])))())
            vb = ev("vector", lambda e: e.drain(), sem="sv")
            W("sync", "sv", vb)
            vst = dma("sync", "stx", taux_d, bc1.ap()[0:1, :])
            W("sync", "stx", vst)
            vld = dma("sync", "stx", cand128.ap(), taux_d.rearrange("(p j) -> p j", p=128))
            W("vector", "stx", vld)
            if tcm is not None:
                W("vector", "stn", tcm)
            for j in range(4):
                ev("vector", (lambda j=j: lambda e: e.tensor_single_scalar(
                    mk_b.ap()[:, j * CHB:(j + 1) * CHB], scj.ap()[:, j * CHB:(j + 1) * CHB],
                    cand128.ap()[:, j:j + 1], op=ALU.is_ge))())
            vmk = ev("vector", lambda e: e.drain(), sem="sv")
            W("tensor", "sv", vmk)
            if pcc_free[0] is not None:
                W("tensor", pcc_free[0][0], pcc_free[0][1])
                pcc_free[0] = None
            tcm = ev("tensor", lambda e: e.matmul(pcc.ap()[0:1, 0:4 * CHB], ones_col.ap(),
                                                  mk_b.ap(), start=True, stop=True), sem="stn")
            W("vector", "stn", tcm)
            ev("vector", (lambda cnt_f=cnt_f: lambda e: e.tensor_reduce(
                cnt_f.rearrange("o (j x) -> o j x", x=1),
                pcc.ap()[0:1, 0:4 * CHB].rearrange("o (j c) -> o j c", c=CHB), axis=AX, op=ALU.add))())
            vrd = ev("vector", lambda e: e.drain(), sem="sv")
            pcc_free[0] = ("sv", C["sv"])
            ev("vector", (lambda cnt_f=cnt_f: lambda e: e.tensor_scalar(
                stg_f32[0:1, 0:4], cnt_f, float(CAP), None, op0=ALU.is_ge))())
            drain("vector")
            ev("vector", lambda e: e.tensor_copy(mk16.ap(), stg_f32[0:1, 0:4]))
            drain("vector")
            ev("vector", (lambda lo_n=lo_n, mid=mid, lo_c=lo_c: lambda e: e.select(
                lo_n, mk16.ap(), mid, lo_c, add_drain=True))())
            drain("vector")
            ev("vector", (lambda hi_n=hi_n, hi_c=hi_c, mid=mid: lambda e: e.select(
                hi_n, mk16.ap(), hi_c, mid, add_drain=True))())
            drain("vector")
            tau_sl = [nxt, cur]
        fin = tau_sl[0]
        ev("vector", (lambda fin=fin: lambda e: e.tensor_copy(
            bc1.ap().rearrange("o (p j) -> o p j", j=4),
            tt.ap()[0:1, fin:fin + 4].unsqueeze(1).to_broadcast([1, 128, 4])))())
        vb = ev("vector", lambda e: e.drain(), sem="sv")
        W("sync", "sv", vb)
        vst = dma("sync", "stx", taux_d, bc1.ap()[0:1, :])
        W("sync", "stx", vst)
        vld = dma("sync", "stx", tau128.ap(), taux_d.rearrange("(p j) -> p j", p=128))
        W("vector", "stx", vld)
        if d_tau is not None:
            vtt = ev("vector", lambda e: e.engine_nop(), sem="sv")
            W("sync", "sv", vtt)
            dma("sync", "sst", d_tau[:, :], tt.ap())
        for j in range(4):
            ev("vector", (lambda j=j: lambda e: e.tensor_single_scalar(
                wsc.ap()[:, j * CHB:(j + 1) * CHB], scj.ap()[:, j * CHB:(j + 1) * CHB],
                tau128.ap()[:, j:j + 1], op=ALU.is_ge))())
            drain("vector")
            ev("vector", (lambda j=j: lambda e: e.tensor_tensor(
                wsc.ap()[:, j * CHB:(j + 1) * CHB], wsc.ap()[:, j * CHB:(j + 1) * CHB],
                scj.ap()[:, j * CHB:(j + 1) * CHB], op=ALU.mult))())
            drain("vector")
        for j in range(4):
            ev("vector", (lambda j=j: lambda e: e.tensor_single_scalar(
                mk_b.ap()[:, j * CHB:(j + 1) * CHB], scj.ap()[:, j * CHB:(j + 1) * CHB],
                tau128.ap()[:, j:j + 1], op=ALU.is_ge))())
            drain("vector")
        vwsc = ev("vector", lambda e: e.drain(), sem="sv")
        if d_mkb is not None:
            W("sync", "sv", vwsc)
            dma("sync", "sst", d_mkb[:, :], mk_b.ap())
        if d_wsc is not None:
            W("sync", "sv", vwsc)
            dma("sync", "sst", d_wsc[:, :], wsc.ap())
        if d_scj is not None:
            W("sync", "sv", vwsc)
            dma("sync", "sst", d_scj[:, :], scj.ap())
        W("sync", "sv", vwsc)

        stgbf = stg.ap().bitcast(BF16)

        NCG = SB // 1024
        SLT = NCG * 256
        MGN = SLT // 512

        def emit_routed(le):
            for bb in range(B):
                j = 2 * bb + le
                mkj = mk_b.ap()[:, j * CHB:(j + 1) * CHB]
                free_ps(4)
                if pcc_free[0] is not None:
                    W("tensor", pcc_free[0][0], pcc_free[0][1])
                    pcc_free[0] = None
                W("tensor", "sv", C["sv"])
                t1 = ev("tensor", (lambda mkj=mkj: lambda e: e.matmul(pss[4].ap()[:, 0:CHB], ltb.ap(),
                                                     mkj, start=True, stop=True))(), sem="stn")
                t2 = ev("tensor", (lambda mkj=mkj: lambda e: e.matmul(pcc.ap()[0:1, 0:CHB], ones_col.ap(),
                                                     mkj, start=True, stop=True))(), sem="stn")
                W("vector", "stn", t2)
                if d_poss is not None:
                    W("vector", "sst")
                ev("vector", (lambda mkj=mkj: lambda e: e.tensor_sub(posc.ap()[:, 0:CHB], pss[4].ap()[:, 0:CHB], mkj))())
                drain("vector")
                ra = stg_f32[0:1, 0:CHB]
                rb = stg_f32[0:1, 64:64 + CHB]
                rc = stg_f32[0:1, 128:128 + CHB]
                ev("vector", lambda e: e.tensor_copy(ra, pcc.ap()[0:1, 0:CHB]))
                drain("vector")
                ev("vector", lambda e: e.memset(rb, 0.0))
                drain("vector")
                ev("vector", lambda e: e.tensor_copy(rb[:, 1:CHB], ra[:, 0:CHB - 1]))
                drain("vector")
                cur, oth = rb, ra
                for sh in (1, 2, 4, 8, 16, 32):
                    if sh >= CHB:
                        break
                    ev("vector", (lambda oth=oth, cur=cur, sh=sh: lambda e: e.tensor_copy(oth[:, 0:sh], cur[:, 0:sh]))())
                    drain("vector")
                    ev("vector", (lambda oth=oth, cur=cur, sh=sh: lambda e: e.tensor_add(
                        oth[:, sh:CHB], cur[:, sh:CHB], cur[:, 0:CHB - sh]))())
                    drain("vector")
                    cur, oth = oth, cur
                gv = cur.rearrange("o (g k) -> o g k", k=8)
                gs = gv[:, :, 0:1].to_broadcast([1, CHB // 8, 8])
                ev("vector", (lambda cur=cur, gv=gv, gs=gs: lambda e: e.tensor_sub(
                    rc.rearrange("o (g k) -> o g k", k=8), gv, gs))())
                vps = ev("vector", lambda e: e.drain(), sem="sv")
                ps_free[4] = ("sv", vps)
                pcc_free[0] = ("sv", vps)
                W("sync", "sv", vps)
                vst = dma("sync", "stx", pcb_d.rearrange("(p c) -> p c", c=CHB).unsqueeze(0),
                          rc.unsqueeze(1).to_broadcast([1, 128, CHB]))
                W("sync", "stx", vst)
                vld = dma("sync", "stx", cb128.ap()[:, 0:CHB], pcb_d.rearrange("(p c) -> p c", c=CHB))
                W("vector", "stx", vld)
                ev("vector", lambda e: e.tensor_add(posc.ap()[:, 0:CHB], posc.ap()[:, 0:CHB], cb128.ap()[:, 0:CHB]))
                drain("vector")
                ev("vector", lambda e: e.tensor_scalar(cb128.ap()[:, 0:CHB], posc.ap()[:, 0:CHB], 256.0, 2048.0, op0=ALU.is_ge, op1=ALU.mult))
                drain("vector")
                ev("vector", lambda e: e.tensor_add(posc.ap()[:, 0:CHB], posc.ap()[:, 0:CHB], cb128.ap()[:, 0:CHB]))
                drain("vector")
                ev("vector", (lambda mkj=mkj: lambda e: e.tensor_scalar(cb128.ap()[:, 0:CHB], mkj, -4096.0, 4096.0, op0=ALU.mult, op1=ALU.add))())
                drain("vector")
                ev("vector", lambda e: e.tensor_add(poss.ap()[:, 0:CHB], posc.ap()[:, 0:CHB], cb128.ap()[:, 0:CHB]))
                vposs = ev("vector", lambda e: e.drain(), sem="sv")
                if d_poss is not None:
                    W("sync", "sv", vposs)
                    dma("sync", "sst", d_poss[j * 128:(j + 1) * 128, 0:CHB], poss.ap()[:, 0:CHB])
                    dma("sync", "sst", d_poss[j * 128:(j + 1) * 128, 64:64 + CHB], posc.ap()[:, 0:CHB])
                    dma("sync", "sst", d_poss[j * 128:(j + 1) * 128, 128:128 + CHB], cb128.ap()[:, 0:CHB])

                W("sync", "sv", vposs)
                W("sync", "scc", ag_yt)
                W("sync", "stn", C["stn"])
                W("scalar", "stn", C["stn"])
                W("vector", "sac")
                pstrip = stgbf
                pbrd = [None]
                gytr = [None, None]
                for cg in range(NCG):
                    for ss in range(4):
                        free_ps(ss)
                    if cg > 0 or (bb > 0 or le > 0):
                        W("sync", "stn", C["stn"])
                    vyb = None
                    for sb in range(8):
                        c8 = cg * 8 + sb
                        vyb = dma("sync", "syt0", hb2[:, sb * 1024:(sb + 1) * 1024],
                                  y_all[bb * SB + c8 * 128: bb * SB + (c8 + 1) * 128, :])
                    if pbrd[0] is not None:
                        W("vector", "stn", pbrd[0])
                    W("vector", "sw")
                    for sb in range(8):
                        for ob in range(2):
                            ev("vector", (lambda sb=sb, ob=ob, c=cg * 8 + sb: lambda e: e.tensor_single_scalar(
                                pstrip[:, (sb * 2 + ob) * 128:(sb * 2 + ob + 1) * 128],
                                iotar_s.ap()[:, ob * 128:(ob + 1) * 128],
                                poss.ap()[:, c:c + 1], op=ALU.is_equal))())
                    vpb = ev("vector", lambda e: e.drain(), sem="sv")
                    W("tensor", "sv", vpb)
                    W("tensor", "syt0", vyb)
                    for sb in range(8):
                        for ob in range(2):
                            for half in range(2):
                                fn = (lambda ob=ob, half=half, sb=sb: lambda e: e.matmul(
                                    pss[ob * 2 + half].ap()[:, 0:512],
                                    pstrip[:, (sb * 2 + ob) * 128:(sb * 2 + ob + 1) * 128],
                                    hb2[:, sb * 1024 + half * 512: sb * 1024 + (half + 1) * 512],
                                    start=(sb == 0), stop=(sb == 7)))()
                                if sb == 7 and ob == 1 and half == 1:
                                    pbrd[0] = ev("tensor", fn, sem="stn")
                                else:
                                    ev("tensor", fn)
                    for ob in range(2):
                        po = cg * 2 + ob
                        W("vector", "stn", pbrd[0])
                        if gytr[ob] is not None:
                            W("vector", "slq", C["slq"])
                        ev("vector", (lambda ob=ob: lambda e: e.tensor_copy(
                            eo_b.ap()[:, ob * 1024: ob * 1024 + 512], pss[ob * 2].ap()[:, 0:512]))())
                        drain("vector")
                        vg = ev("vector", (lambda ob=ob: lambda e: e.tensor_copy(
                            eo_b.ap()[:, ob * 1024 + 512: (ob + 1) * 1024], pss[ob * 2 + 1].ap()[:, 0:512]))(), sem="sv")
                        ps_free[ob * 2] = ("sv", vg)
                        ps_free[ob * 2 + 1] = ("sv", vg)
                        W("scalar", "sv", vg)
                        for dc in range(8):
                            tvx = dma("scalar", "slq", xTs[:, dc * SLT + po * 128: dc * SLT + (po + 1) * 128],
                                      eo_b.ap()[:, ob * 1024 + dc * 128: ob * 1024 + (dc + 1) * 128], transpose=True)
                        gytr[ob] = tvx

                W("tensor", "slq", C["slq"])
                gtv2 = [None, None]
                pb2rd = [None]
                pb2rd_d = {}
                ptt_ev = {}
                last_sc_ev = [None]
                for mg in range(MGN):
                    for fc in range(16):
                        sA, sB2 = fc % 2, 2 + fc % 2
                        free_ps(sA); free_ps(sB2)
                        for dc in range(8):
                            ev("tensor", (lambda sA=sA, dc=dc, fc=fc, mg=mg: lambda e: e.matmul(
                                pss[sA].ap()[:, 0:GRP],
                                wgt_b[:, dc * FF + fc * 128: dc * FF + (fc + 1) * 128],
                                xTs[:, dc * SLT + mg * 512: dc * SLT + (mg + 1) * 512],
                                start=(dc == 0), stop=(dc == 7)))())
                        tga = None
                        for dc in range(8):
                            fn = (lambda sB2=sB2, dc=dc, fc=fc, mg=mg: lambda e: e.matmul(
                                pss[sB2].ap()[:, 0:GRP],
                                wut_b[:, dc * FF + fc * 128: dc * FF + (fc + 1) * 128],
                                xTs[:, dc * SLT + mg * 512: dc * SLT + (mg + 1) * 512],
                                start=(dc == 0), stop=(dc == 7)))()
                            if dc == 7:
                                tga = ev("tensor", fn, sem="stn")
                            else:
                                ev("tensor", fn)
                        W("scalar", "stn", tga)
                        if gtv2[fc % 2] is not None:
                            W("scalar", "sv", gtv2[fc % 2])
                        sg1 = ev("scalar", (lambda fc=fc, sA=sA: lambda e: e.activation(
                            gt_b[fc % 2].ap(), pss[sA].ap()[:, 0:GRP], AF.Gelu))(), sem="ss")
                        W("vector", "ss", sg1)
                        vh = ev("vector", (lambda fc=fc, sB2=sB2: lambda e: e.tensor_tensor(
                            hb2[:, fc * GRP:(fc + 1) * GRP], gt_b[fc % 2].ap(), pss[sB2].ap()[:, 0:GRP], op=ALU.mult))(), sem="sv")
                        gtv2[fc % 2] = vh
                        ps_free[sA] = ("sv", C["sv"])
                        ps_free[sB2] = ("sv", C["sv"])
                    W("tensor", "sv", C["sv"])
                    for ch in range(4):
                        marks = {}
                        for half in range(2):
                            slot = (ch * 2 + half) % 6
                            free_ps(slot)
                            for fc in range(16):
                                fn = (lambda slot=slot, fc=fc, ch=ch, half=half: lambda e: e.matmul(
                                    pss[slot].ap()[:, 0:512],
                                    hb2[:, fc * GRP + ch * 128: fc * GRP + (ch + 1) * 128],
                                    wdt_b[:, fc * D + half * 512: fc * D + (half + 1) * 512],
                                    start=(fc == 0), stop=(fc == 15)))()
                                if fc == 15:
                                    marks[half] = ev("tensor", fn, sem="stn")
                                else:
                                    ev("tensor", fn)
                        W("vector", "stn", marks[1])
                        for half in range(2):
                            slot = (ch * 2 + half) % 6
                            ev("vector", (lambda ch=ch, half=half, slot=slot: lambda e: e.tensor_copy(
                                eoI[:, ch * 1024 + half * 512: ch * 1024 + (half + 1) * 512],
                                pss[slot].ap()[:, 0:512]))())
                            drain("vector")
                        vch = ev("vector", lambda e: e.drain(), sem="sv")
                        ps_free[(ch * 2) % 6] = ("sv", vch)
                        ps_free[(ch * 2 + 1) % 6] = ("sv", vch)
                    for cgl in range(2):
                        cg = 2 * mg + cgl
                        if pb2rd[0] is not None:
                            W("vector", "stn", C["stn"])
                        W("vector", "sw")
                        for tbl in range(8):
                            for ob in range(2):
                                ev("vector", (lambda tbl=tbl, ob=ob, c=cg * 8 + tbl: lambda e: e.tensor_single_scalar(
                                    pstrip[:, (tbl * 2 + ob) * 128:(tbl * 2 + ob + 1) * 128],
                                    iotar_s.ap()[:, ob * 128:(ob + 1) * 128],
                                    poss.ap()[:, c:c + 1], op=ALU.is_equal))())
                        vps1 = ev("vector", lambda e: e.drain(), sem="sv")
                        tmark = {}
                        emark = {}
                        for tbl in range(8):
                            c = cg * 8 + tbl
                            free_ps(4); free_ps(5)
                            for ob in range(2):
                                k = tbl * 2 + ob
                                p2 = k % 4
                                if k == 0:
                                    W("tensor", "sv", vps1)
                                if ptt_ev.get(0) is not None:
                                    W("tensor", "sv", ptt_ev[0])
                                tmark[k] = ev("tensor", (lambda k=k: lambda e: e.transpose(
                                    ptt.ap(), pstrip[:, k * 128:(k + 1) * 128], idm_b.ap()))(), sem="stn")
                                W("vector", "stn", tmark[k])
                                if pb2rd_d.get(p2) is not None:
                                    W("vector", "stn", pb2rd_d[p2])
                                emark[k] = ev("vector", (lambda p2=p2: lambda e: e.tensor_copy(
                                    Pb2[p2].ap(), ptt.ap()))(), sem="sv")
                                ptt_ev[0] = emark[k]
                            for ob in range(2):
                                k = tbl * 2 + ob
                                p2 = k % 4
                                W("tensor", "sv", emark[k])
                                for half in range(2):
                                    fn = (lambda p2=p2, half=half, cgl=cgl, ob=ob: lambda e: e.matmul(
                                        pss[4 + half].ap()[:, 0:512],
                                        Pb2[p2].ap(),
                                        eoI[:, (cgl * 2 + ob) * 1024 + half * 512: (cgl * 2 + ob) * 1024 + (half + 1) * 512],
                                        start=(ob == 0), stop=(ob == 1)))()
                                    if half == 1:
                                        smark = ev("tensor", fn, sem="stn")
                                    else:
                                        ev("tensor", fn)
                                pb2rd_d[p2] = smark
                            pb2rd[0] = smark
                            if le == 1:
                                if last_sc_ev[0] is not None:
                                    W("sync", "sv", last_sc_ev[0])
                                va = dma("sync", "sar", acb.ap()[:, 0:D],
                                         acc_d[bb * SB + c * 128: bb * SB + (c + 1) * 128, :])
                            W("vector", "stn", smark)
                            W("vector", "sac")
                            if le == 1:
                                W("vector", "sar", va)
                            scol = wsc.ap()[:, j * CHB + c: j * CHB + c + 1]
                            for half in range(2):
                                dst = eo_b.ap()[:, 2048 + half * 512: 2048 + (half + 1) * 512]
                                if le == 1:
                                    ev("vector", (lambda dst=dst, half=half, scol=scol: lambda e: e.scalar_tensor_tensor(
                                        dst, pss[4 + half].ap()[:, 0:512], scol,
                                        acb.ap()[:, half * 512:(half + 1) * 512],
                                        op0=ALU.mult, op1=ALU.add))())
                                else:
                                    ev("vector", (lambda dst=dst, half=half, scol=scol: lambda e: e.tensor_scalar(
                                        dst, pss[4 + half].ap()[:, 0:512], scol, None, op0=ALU.mult))())
                                drain("vector")
                            veo = ev("vector", lambda e: e.drain(), sem="sv")
                            last_sc_ev[0] = veo
                            ps_free[4] = ("sv", veo)
                            ps_free[5] = ("sv", veo)
                            W("sync", "sv", veo)
                            dma("sync", "sac", acc_d[bb * SB + c * 128: bb * SB + (c + 1) * 128, :],
                                eo_b.ap()[:, 2048:3072])

        for le in range(2):
            W("vector", "stn")
            W("scalar", "stn")
            W("sync", "stn", C["stn"])
            load_cast(WgT[le], wgt_b)
            load_cast(WuT[le], wut_b)
            load_cast(WdT[le], wdt_b)
            wexp = ev("scalar", lambda e: e.copy(stats[0:1, 14:15], stg_f32[0:1, 0:1]), sem="ss")
            W("tensor", "ss", wexp)
            emit_routed(le)

        W("gpsimd", "sac")
        ev("gpsimd", lambda e: e.collective_compute(
            "ReduceScatter", ALU.add, replica_groups=[list(range(NC_N))],
            ins=[acc_d.opt()], outs=[rs_out.opt()]), sem="scc")

    if "p2" in phases:
        W("sync", "scc", C["scc"])
        for t in range(TILES):
            base = t * 128
            W("vector", "sfin")
            v0 = dma("sync", "sar", at[0], rs_out[base:base + 128, :])
            v1 = dma("sync", "sar", at[1], acc_sh[base:base + 128, :])
            vxp2 = dma("sync", "sar", xpf.ap(), xp_dram[base:base + 128, :])
            W("vector", "sar", vxp2)
            W("vector", "sac")
            ev("vector", lambda e: e.tensor_add(ot2, at[0], at[1]))
            ln_norm(ot2, ao2.ap(), pacc.ap())
            ev("vector", lambda e: e.tensor_add(ao2.ap(), ao2.ap(), xpf.ap()))
            vfin = ev("vector", lambda e: e.drain(), sem="sv")
            W("sync", "sv", vfin)
            dma("sync", "sfin", out[base:base + 128, :], ao2.ap())
        W("sync", "sfin")
        W("gpsimd", "sfin")

    blk = es.enter_context(nc.Block())
    for eng in ("sync", "scalar", "vector", "tensor", "gpsimd"):
        fns = streams[eng]
        getattr(blk, eng)(lambda e, fns=fns: [f(e) for f in fns])
    es.close()
    return nc, dbg


def _prep_inputs(cfg, inputs):
    SB, B, TC = cfg["SB"], cfg["B"], cfg["TC"]
    L = "L0_"
    f32 = lambda a: np.ascontiguousarray(np.asarray(a, dtype=np.float32))
    x = f32(inputs[L + "attn_x"]).reshape(B * SB, D)
    Wq = f32(inputs[L + "attn_Wq"]); bq = f32(inputs[L + "attn_bq"]).reshape(1, D)
    Wdn = f32(inputs[L + "attn_Wdown"]); bdn = f32(inputs[L + "attn_bdown"]).reshape(1, DLAT)
    Wup = f32(inputs[L + "attn_Wup"]); bup = f32(inputs[L + "attn_bup"]).reshape(1, 2 * D)
    Wo = f32(inputs[L + "attn_Wout"]); bo = f32(inputs[L + "attn_bout"]).reshape(1, D)
    gateT = np.ascontiguousarray(f32(inputs[L + "ffn_gate_w"]).T)
    WsgT = np.ascontiguousarray(f32(inputs[L + "ffn_Ws_g"]).T)
    WsuT = np.ascontiguousarray(f32(inputs[L + "ffn_Ws_u"]).T)
    WsdT = np.ascontiguousarray(f32(inputs[L + "ffn_Ws_d"]).T)
    Wg = f32(inputs[L + "ffn_Wg"])
    Wu = f32(inputs[L + "ffn_Wu"])
    Wd = f32(inputs[L + "ffn_Wd"])
    LT = np.tril(np.ones((128, 128), np.float32)).T
    iotar = np.tile(np.arange(256, dtype=np.float32), (128, 1))
    idm = np.eye(128, dtype=np.float32)
    maps = []
    for c in range(NC_N):
        eh = np.zeros((128, 4 * E), np.float32)
        for j in range(4):
            eh[:, j * E + 2 * c + (j % 2)] = 1.0
        m = {
            "x": x[c * TC:(c + 1) * TC],
            "Wq": Wq, "bq": bq, "Wdn": Wdn, "bdn": bdn,
            "Wup": Wup, "bup": bup, "Wo": Wo, "bo": bo,
            "gateT": gateT, "WsgT": WsgT, "WsuT": WsuT, "WsdT": WsdT,
            "eh_in": eh, "LT_in": LT, "iotar_in": iotar, "idm_in": idm,
        }
        for le in range(2):
            eg = 2 * c + le
            m[f"Wg{le}T"] = np.ascontiguousarray(Wg[eg].T)
            m[f"Wu{le}T"] = np.ascontiguousarray(Wu[eg].T)
            m[f"Wd{le}T"] = np.ascontiguousarray(Wd[eg].T)
        maps.append(m)
    return maps


def kernel(**inputs):
    cfg = make_cfg()
    nc, _dbg = build_nc(cfg)
    in_maps = _prep_inputs(cfg, inputs)
    res = run_bass_kernel_spmd(nc, in_maps, core_ids=list(range(NC_N)))
    outs = [np.asarray(res.results[c]["out"]) for c in range(NC_N)]
    full = np.concatenate(outs, axis=0)
    return full.reshape(cfg["B"], cfg["SB"], D)


# version 11
# speedup vs baseline: 1.0296x; 1.0296x over previous
import numpy as np
import ml_dtypes
from contextlib import ExitStack

import concourse.bass as bass
import concourse.mybir as mybir
from concourse.bass_utils import run_bass_kernel_spmd

F32 = mybir.dt.float32
BF16 = mybir.dt.bfloat16
I16 = mybir.dt.int16
I32 = mybir.dt.int32
AF = mybir.ActivationFunctionType
ALU = mybir.AluOpType
AX = mybir.AxisListType.X

NC_N = 8
D = 1024
H = 16
DH = 64
DLAT = 256
E = 16
FF = 2048
LN_EPS = 1e-5
NBIS = 46


def make_cfg(SB=8192, B=2, phases=("p0", "p1", "p2")):
    c = {"SB": SB, "B": B}
    c["TC"] = SB * B // NC_N
    c["CAP"] = SB // 8
    c["TILES"] = c["TC"] // 128
    c["phases"] = phases
    return c


def build_nc(cfg, debug=()):
    SB, B, TC, CAP, TILES = (cfg[k] for k in ("SB", "B", "TC", "CAP", "TILES"))
    phases = cfg["phases"]
    TT = B * SB
    GRP = 512
    NG = TT // GRP
    NGS = TC // GRP
    CHB = SB // 128
    RNK = TC

    nc = bass.Bass("TRN2", target_bir_lowering=False, debug=False)

    def din(name, shape, dt=F32):
        return nc.dram_tensor(name, shape, dt, kind="ExternalInput").ap()

    x_in = din("x", [TC, D])
    Wq = din("Wq", [D, D]); bq = din("bq", [1, D])
    Wdn = din("Wdn", [D, DLAT]); bdn = din("bdn", [1, DLAT])
    Wup = din("Wup", [DLAT, 2 * D]); bup = din("bup", [1, 2 * D])
    Wo = din("Wo", [D, D]); bo = din("bo", [1, D])
    gateT = din("gateT", [D, E])
    WsgT = din("WsgT", [D, FF]); WsuT = din("WsuT", [D, FF]); WsdT = din("WsdT", [FF, D])
    WgT = [din(f"Wg{i}T", [D, FF]) for i in range(2)]
    WuT = [din(f"Wu{i}T", [D, FF]) for i in range(2)]
    WdT = [din(f"Wd{i}T", [FF, D]) for i in range(2)]
    eh_in = din("eh_in", [128, 4 * E])
    LT_in = din("LT_in", [128, 128])
    iotar_in = din("iotar_in", [128, 256])
    idm_in = din("idm_in", [128, 128])

    out = nc.dram_tensor("out", [TC, D], F32, kind="ExternalOutput").ap()
    dbg = {}
    def dout(name, shape, dt=F32):
        if name in debug:
            dbg[name] = nc.dram_tensor("d_" + name, shape, dt, kind="ExternalOutput").ap()
        return dbg.get(name)

    d_xp = dout("xp", [TC, D])
    d_yh = dout("yh", [TC, D], BF16)
    d_tau = dout("tau", [1, 24])
    d_wsc = dout("wsc", [128, 4 * CHB])
    d_scj = dout("scj", [128, 4 * CHB])
    d_poss = dout("poss", [4 * 128, 192])
    d_mkb = dout("mkb", [128, 4 * CHB], BF16)

    xp_dram = nc.dram_tensor("xp_dram", [TC, D], F32).ap()
    yTd = nc.dram_tensor("yTd", [D, TC], BF16).ap()
    ag1y_i = nc.dram_tensor("ag1y_i", [TC, D], BF16).ap()
    y_all = nc.dram_tensor("y_all", [TT, D], BF16).ap()
    pcb_d = nc.dram_tensor("pcb_d", [128 * (SB // 128)], F32).ap()
    scd = nc.dram_tensor("scd", [TC, E], F32).ap()
    sc_all = nc.dram_tensor("sc_all", [NC_N * TC, E], F32).ap()
    taux_d = nc.dram_tensor("taux_d", [512], F32).ap()
    acc_d = nc.dram_tensor("acc_d", [TT, D], BF16).ap()
    rs_out = nc.dram_tensor("rs_out", [TC, D], BF16).ap()
    acc_sh = nc.dram_tensor("acc_sh", [TC, D], BF16).ap()

    es = ExitStack()
    sb = lambda n, s, dt: es.enter_context(nc.sbuf_tensor(n, s, dt))

    BUF_A = sb("BUF_A", [128, 49152], BF16)
    BUF_B = sb("BUF_B", [128, 28672], BF16)
    A_, B_ = BUF_A.ap(), BUF_B.ap()
    wq_b = A_[:, 0:8 * D]
    wo_b = A_[:, 8 * D:16 * D]
    wup_b = A_[:, 16 * D:20 * D]
    wdn_b = A_[:, 20 * D:22 * D]
    REG = A_[:, 22528:22528 + 12 * D]
    EXT = A_[:, 34816:49152]
    xs = [REG[:, 0:2 * D].bitcast(F32), REG[:, 2 * D:4 * D].bitcast(F32),
          EXT[:, 0:2 * D].bitcast(F32)]
    aof = REG[:, 4 * D:6 * D].bitcast(F32)
    kvbP = [REG[:, 6 * D:8 * D], EXT[:, 2 * D:4 * D]]
    xhat = REG[:, 8 * D:9 * D]
    xhatT = REG[:, 9 * D:10 * D]
    qbP = [REG[:, 10 * D:11 * D], EXT[:, 4 * D:5 * D]]
    obT = EXT[:, 5 * D:6 * D]
    yhb = REG[:, 11 * D:12 * D]
    latb = B_[:, 0:DLAT]
    latT = B_[:, DLAT:2 * DLAT]
    scf = B_[:, 512:1024].bitcast(F32)
    Eb = B_[:, 1024:1280]
    ob = B_[:, 1280:2304]
    yhT = B_[:, 2304:3328]
    wgt_b = A_[:, 0:16384]
    wut_b = A_[:, 16384:32768]
    wdt_b = A_[:, 32768:49152]
    yt_g = [B_[:, 0:4096], B_[:, 4096:8192]]
    hb_g = B_[:, 8192:16384]
    xTs = B_[:, 0:16384]
    hb2 = B_[:, 16384:24576]
    eoI = B_[:, 24576:28672]
    at = [B_[:, 0:D], B_[:, D:2 * D]]
    ot2 = B_[:, 2 * D:4 * D].bitcast(F32)

    wg8_b = sb("wg8_b", [128, 8 * E], BF16)
    bq_b = sb("bq_b", [1, D], BF16)
    bdn_b = sb("bdn_b", [1, DLAT], BF16)
    bup_b = sb("bup_b", [1, 2 * D], BF16)
    bo_b = sb("bo_b", [1, D], BF16)
    ones = sb("ones", [1, 128], BF16)
    ones_col = sb("ones_col", [128, 1], BF16)
    stg = sb("stg0", [128, 1024], F32)
    stg_f32 = stg.ap()
    gt_b = [sb(f"gt_b{i}", [128, GRP], BF16) for i in range(2)]
    eo_b = sb("eo_b", [128, 4 * D], BF16)
    acb = sb("acb", [128, D], BF16)
    pacc = sb("pacc", [128, D], F32)
    ptmp = sb("ptmp", [128, D], F32)
    ao2 = sb("ao2", [128, D], F32)
    xpf = sb("xpf", [128, D], F32)
    lgf = sb("lgf", [128, 32], F32)
    stats = sb("stats", [128, 16], F32)
    stats2 = sb("stats2", [128, 2 * H], F32)
    epsb = sb("epsb", [128, 1], F32)
    sc_b = [B_[:, 16384 + i * 4096: 16384 + (i + 1) * 4096].bitcast(F32)[:, 0:CHB * E] for i in range(B)]
    scj = sb("scj", [128, 4 * CHB], F32)
    wsc = sb("wsc", [128, 4 * CHB], F32)
    mk_b = sb("mk_b", [128, 4 * CHB], BF16)
    eh_s = sb("eh_s", [128, 4 * E], F32)
    cand128 = sb("cand128", [128, 4], F32)
    iotar_s = sb("iotar_s", [128, 256], F32)
    ltb = sb("ltb", [128, 128], BF16)
    idm_b = sb("idm_b", [128, 128], BF16)
    posc = sb("posc", [128, 64], F32)
    poss = sb("poss", [128, 64], F32)
    cb128 = sb("cb128", [128, 64], F32)
    Pb4 = sb("Pb4", [128, 512], BF16)
    Pb2 = [sb(f"Pb2_{i}", [128, 128], BF16) for i in range(4)]
    tau128 = sb("tau128", [128, 4], F32)
    tt = sb("tt", [1, 24], F32)
    mk16 = sb("mk16", [1, 4], I16)
    bc1 = sb("bc1", [1, 512], F32)

    pss = [es.enter_context(nc.psum_tensor(f"ps{i}", [128, 512], F32, bank=i)) for i in range(6)]
    ptt = es.enter_context(nc.psum_tensor("ptt", [128, 128], BF16, bank=6))
    pcc = es.enter_context(nc.psum_tensor("pcc", [1, 384], F32, bank=7))

    sems = {}
    for s in ("sw", "sw1", "sx0", "sx1", "sx2", "sst", "slq", "slq2", "syt0", "syt1", "ssc", "stx",
              "sac", "sar", "sfin", "sv", "ss", "stn", "scc"):
        sems[s] = es.enter_context(nc.semaphore(s))
    C = {s: 0 for s in sems}
    ps_free = [None] * 6
    pcc_free = [None]
    ptt_free = [None]
    streams = {k: [] for k in ("sync", "scalar", "vector", "tensor", "gpsimd")}

    def W(eng, sem, val=None):
        v = C[sem] if val is None else val
        streams[eng].append(lambda e: e.wait_ge(sems[sem], v))

    def INC(sem, n=1):
        C[sem] += n
        return sems[sem], C[sem]

    def dma(q, sem, out_ap, in_ap, **kw):
        s, v = INC(sem, 16)
        streams[q].append(lambda e: e.dma_start(out=out_ap, in_=in_ap, **kw).then_inc(s, 16))
        return v

    def ev(eng, fn, sem=None):
        if sem is None:
            streams[eng].append(fn)
            return None
        s, v = INC(sem, 1)
        streams[eng].append(lambda e: fn(e).then_inc(s, 1))
        return v

    def drain(eng):
        streams[eng].append(lambda e: e.drain())

    def ln_norm(src, dst, junk):
        st = lambda i: stats[:, i:i + 1]
        drain("vector")
        ev("vector", lambda e: e.reduce_sum(st(0), src, axis=AX))
        ev("vector", lambda e: e.scalar_tensor_tensor(junk, src, 1.0, src, op0=ALU.mult, op1=ALU.mult, accum_out=st(1)))
        drain("vector")
        ev("vector", lambda e: e.tensor_scalar(st(2), st(0), 1.0 / D, None, op0=ALU.mult))
        ev("vector", lambda e: e.tensor_scalar(st(3), st(1), 1.0 / D, None, op0=ALU.mult))
        drain("vector")
        ev("vector", lambda e: e.tensor_tensor(st(4), st(2), st(2), op=ALU.mult))
        drain("vector")
        vs = ev("vector", lambda e: e.tensor_sub(st(5), st(3), st(4)), sem="sv")
        W("scalar", "sv", vs)
        sq = ev("scalar", lambda e: e.activation(st(6), st(5), AF.Sqrt, bias=epsb[:, :]), sem="ss")
        W("vector", "ss", sq)
        ev("vector", lambda e: e.reciprocal(st(7), st(6)))
        drain("vector")
        ev("vector", lambda e: e.tensor_scalar(dst, src, st(2), st(7), op0=ALU.subtract, op1=ALU.mult))
        drain("vector")

    def free_ps(i):
        if ps_free[i] is not None:
            s, v = ps_free[i]
            W("tensor", s, v)
            ps_free[i] = None

    def mm_acc(slot, pairs, bias_rhs=None, n=512, sem=None):
        free_ps(slot)
        p = pss[slot].ap()[:, 0:n]
        ret = None
        for i, (l, r) in enumerate(pairs):
            first = (i == 0)
            last = (i == len(pairs) - 1) and bias_rhs is None
            fn = (lambda l=l, r=r, f=first, la=last: (lambda e: e.matmul(p, l, r, start=f, stop=la)))()
            if last and sem is not None:
                ret = ev("tensor", fn, sem=sem)
            else:
                ev("tensor", fn)
        if bias_rhs is not None:
            fn = (lambda br=bias_rhs: (lambda e: e.matmul(p, ones.ap()[:, 0:128], br, start=False, stop=True)))()
            if sem is not None:
                ret = ev("tensor", fn, sem=sem)
            else:
                ev("tensor", fn)
        return ret

    ev("vector", lambda e: e.memset(ones.ap(), 1.0))
    ev("vector", lambda e: e.memset(ones_col.ap(), 1.0))
    ev("vector", lambda e: e.memset(epsb.ap(), LN_EPS))
    ev("vector", lambda e: e.memset(tt.ap(), 0.0))
    drain("vector")
    ev("vector", lambda e: e.memset(tt.ap()[0:1, 4:8], 1.0))
    vz = ev("vector", lambda e: e.drain(), sem="sv")
    W("sync", "sv", vz)

    CHW = 1024

    lc_state = {"k": 0, "dv": {0: None, 1: None}}

    def load_cast(dram, dst):
        R, Ccols = dram.shape
        nchunks = R // 128
        src_r = dram.rearrange("(c p) o -> p c o", p=128)
        HW2 = 512
        for c0 in range(nchunks):
            o0 = 0
            while o0 < Ccols:
                take = min(HW2, Ccols - o0)
                pb = lc_state["k"] % 2
                lc_state["k"] += 1
                if lc_state["dv"][pb] is not None:
                    W("sync", "ss", lc_state["dv"][pb])
                v = dma("sync", "sw" if pb == 0 else "sw1",
                        stg_f32[:, pb * 512: pb * 512 + take], src_r[:, c0, o0:o0 + take])
                W("scalar", "sw" if pb == 0 else "sw1", v)
                off = c0 * Ccols + o0
                dv = ev("scalar", (lambda take=take, off=off, dst=dst, pb=pb: lambda e: e.copy(
                    dst[:, off:off + take], stg_f32[:, pb * 512: pb * 512 + take]))(), sem="ss")
                lc_state["dv"][pb] = dv
                o0 += take

    load_cast(Wq, wq_b)
    load_cast(Wdn, wdn_b)
    load_cast(Wup, wup_b)
    load_cast(Wo, wo_b)
    load_cast(gateT, wg8_b.ap())
    W("sync", "ss")
    for (bsrc, bdst) in ((bq, bq_b), (bdn, bdn_b), (bup, bup_b), (bo, bo_b)):
        BW = bsrc.shape[1]
        for o0 in range(0, BW, 1024):
            tk = min(1024, BW - o0)
            v = dma("sync", "sw", stg_f32[0:1, 0:tk], bsrc[:, o0:o0 + tk])
            W("vector", "sw", v)
            ev("vector", lambda e, bdst=bdst, o0=o0, tk=tk: e.tensor_copy(bdst.ap()[:, o0:o0 + tk], stg_f32[0:1, 0:tk]))
            dv = ev("vector", lambda e: e.drain(), sem="sv")
            W("sync", "sv", dv)
    dma("sync", "sw", eh_s.ap(), eh_in)
    dma("sync", "sw", iotar_s.ap(), iotar_in)
    load_cast(LT_in, ltb.ap())
    load_cast(idm_in, idm_b.ap())
    wcast_s = ev("scalar", lambda e: e.copy(stats[0:1, 14:15], stg_f32[0:1, 0:1]), sem="ss")
    wcast = ev("vector", lambda e: e.drain(), sem="sv")

    SXS = ("sx0", "sx1", "sx2")
    xld = [None, None, None]
    for i in range(min(3, TILES)):
        xld[i] = dma("sync", SXS[i], xs[i][:, :], x_in[i * 128:(i + 1) * 128, :])
    kv_pend = [None]
    q_pend = [None]

    def emit_X(t):
        cur = xs[t % 3]
        qb = qbP[t % 2]
        W("vector", SXS[t % 3], xld[t % 3])
        W("vector", "sst")
        ln_norm(cur, xhat, pacc.ap())
        vx = ev("vector", lambda e: e.engine_nop(), sem="sv")
        W("scalar", "sv", vx)
        W("scalar", "sst")
        W("scalar", "stn", C["stn"])
        for k in range(8):
            tv = dma("scalar", "slq2", xhatT[:, k * 128:(k + 1) * 128],
                     xhat[:, k * 128:(k + 1) * 128], transpose=True)
        W("tensor", "slq2", tv)
        W("tensor", "sv", wcast)
        W("tensor", "ss", wcast_s)
        for hh in range(2):
            tq = mm_acc(hh, [(xhatT[:, k * 128:(k + 1) * 128],
                         wq_b[:, k * D + hh * 512: k * D + hh * 512 + 512]) for k in range(8)],
                   bias_rhs=bq_b.ap()[:, hh * 512: hh * 512 + 512], sem="stn")
        tl = mm_acc(2, [(xhatT[:, k * 128:(k + 1) * 128],
                    wdn_b[:, k * DLAT: (k + 1) * DLAT]) for k in range(8)],
               bias_rhs=bdn_b.ap()[:, :], n=DLAT, sem="stn")
        W("vector", "stn", tl)
        vq = ev("vector", lambda e: e.tensor_copy(latb[:, :], pss[2].ap()[:, 0:DLAT]), sem="sv")
        ps_free[2] = ("sv", vq)
        W("scalar", "sv", vq)
        for k in range(2):
            tv = dma("scalar", "slq2", latT[:, k * 128:(k + 1) * 128],
                     latb[:, k * 128:(k + 1) * 128], transpose=True)
        W("tensor", "slq2", tv)
        kvslot = (2, 3, 4, 5)
        for ch in range(4):
            tk2 = mm_acc(kvslot[ch], [(latT[:, k * 128:(k + 1) * 128],
                             wup_b[:, k * 2 * D + ch * 512: k * 2 * D + ch * 512 + 512]) for k in range(2)],
                   bias_rhs=bup_b.ap()[:, ch * 512: ch * 512 + 512], sem="stn")
        kv_pend[0] = (t, tk2)
        q_pend[0] = (t, tq)

    def emit_qkv_evac():
        if kv_pend[0] is None:
            return
        tn, tk2 = kv_pend[0]
        _, tq = q_pend[0]
        qb = qbP[tn % 2]
        kvb = kvbP[tn % 2]
        kvslot = (2, 3, 4, 5)
        W("vector", "stn", tq)
        ev("vector", lambda e, qb=qb: e.tensor_copy(qb[:, 0:512], pss[0].ap()[:, 0:512]))
        ev("vector", lambda e, qb=qb: e.tensor_copy(qb[:, 512:1024], pss[1].ap()[:, 0:512]))
        W("vector", "stn", tk2)
        for ch in range(4):
            ev("vector", lambda e, ch=ch, kvb=kvb, kvslot=kvslot: e.tensor_copy(
                kvb[:, ch * 512:(ch + 1) * 512], pss[kvslot[ch]].ap()[:, 0:512]))
        vkv = ev("vector", lambda e: e.drain(), sem="sv")
        ps_free[0] = ("sv", vkv); ps_free[1] = ("sv", vkv)
        for ch in range(4):
            ps_free[kvslot[ch]] = ("sv", vkv)
        kv_pend[0] = None

    def emit_Y(t):
        cur = xs[t % 3]
        qb = qbP[t % 2]
        kvb = kvbP[t % 2]
        W("vector", "sst")
        q3 = qb.rearrange("p (h d) -> p h d", d=DH)
        p3 = ptmp.ap().rearrange("p (h d) -> p h d", d=DH)
        a3 = pacc.ap().rearrange("p (h d) -> p h d", d=DH)
        sc3 = scf.rearrange("p (h g) -> p h g", g=H)
        for g in range(H):
            ksl = kvb[:, g * 128: g * 128 + DH].unsqueeze(1).to_broadcast([128, H, DH])
            ev("vector", lambda e, ksl=ksl, q3=q3: e.tensor_tensor(p3, q3, ksl, op=ALU.mult))
            drain("vector")
            ev("vector", lambda e, g=g: e.tensor_reduce(sc3[:, :, g:g + 1], p3, axis=AX, op=ALU.add))
            drain("vector")
        ev("vector", lambda e: e.reduce_max(stats[:, 8:9], scf, axis=AX))
        vmx = ev("vector", lambda e: e.drain(), sem="sv")
        W("scalar", "sv", vmx)
        ev("scalar", lambda e: e.mul(stats[:, 9:10], stats[:, 8:9], -0.125))
        drain("scalar")
        sexp = ev("scalar", lambda e: e.activation(Eb, scf, AF.Exp, bias=stats[:, 9:10], scale=0.125), sem="ss")
        W("vector", "ss", sexp)
        ev("vector", lambda e: e.tensor_reduce(stats2[:, 0:H], Eb.rearrange("p (h g) -> p h g", g=H), axis=AX, op=ALU.add))
        drain("vector")
        ev("vector", lambda e: e.reciprocal(stats2[:, H:2 * H], stats2[:, 0:H]))
        drain("vector")
        E3 = Eb.rearrange("p (h g) -> p h g", g=H)
        for g in range(H):
            esl = E3[:, :, g:g + 1].to_broadcast([128, H, DH])
            vsl = kvb[:, g * 128 + DH: g * 128 + 128].unsqueeze(1).to_broadcast([128, H, DH])
            if g == 0:
                ev("vector", lambda e, esl=esl, vsl=vsl: e.tensor_tensor(a3, esl, vsl, op=ALU.mult))
            else:
                ev("vector", lambda e, esl=esl, vsl=vsl: e.tensor_tensor(p3, esl, vsl, op=ALU.mult))
                drain("vector")
                ev("vector", lambda e: e.tensor_add(pacc.ap(), pacc.ap(), ptmp.ap()))
            drain("vector")
        ev("vector", lambda e: e.tensor_tensor(
            ob.rearrange("p (h d) -> p h d", d=DH), a3,
            stats2[:, H:2 * H].unsqueeze(2).to_broadcast([128, H, DH]), op=ALU.mult))
        vo = ev("vector", lambda e: e.drain(), sem="sv")
        emit_qkv_evac()
        W("scalar", "sv", vo)
        for k in range(8):
            tv = dma("scalar", "slq", obT[:, k * 128:(k + 1) * 128],
                     ob[:, k * 128:(k + 1) * 128], transpose=True)
        W("tensor", "slq", tv)
        for hh in range(2):
            tao = mm_acc(hh, [(obT[:, k * 128:(k + 1) * 128],
                         wo_b[:, k * D + hh * 512: k * D + hh * 512 + 512]) for k in range(8)],
                   bias_rhs=bo_b.ap()[:, hh * 512: hh * 512 + 512], sem="stn")
        W("vector", "stn", tao)
        ev("vector", lambda e: e.tensor_copy(aof[:, 0:512], pss[0].ap()[:, 0:512]))
        vao = ev("vector", lambda e: e.tensor_copy(aof[:, 512:1024], pss[1].ap()[:, 0:512]), sem="sv")
        ps_free[0] = ("sv", vao); ps_free[1] = ("sv", vao)
        ln_norm(aof, ao2.ap(), ptmp.ap())
        ev("vector", lambda e, cur=cur: e.tensor_add(xpf.ap(), cur, ao2.ap()))
        vxp = ev("vector", lambda e: e.drain(), sem="sv")
        W("sync", "sv", vxp)
        dma("sync", "sst", xp_dram[t * 128:(t + 1) * 128, :], xpf.ap())
        if d_xp is not None:
            dma("sync", "sst", d_xp[t * 128:(t + 1) * 128, :], xpf.ap())
        ln_norm(xpf.ap(), yhb, ptmp.ap())
        vyh = ev("vector", lambda e: e.engine_nop(), sem="sv")
        W("sync", "sv", vyh)
        dma("sync", "sst", ag1y_i[t * 128:(t + 1) * 128, :], yhb)
        if d_yh is not None:
            dma("sync", "sst", d_yh[t * 128:(t + 1) * 128, :], yhb)
        W("scalar", "sv", vyh)
        for k in range(8):
            tv = dma("scalar", "slq", yhT[:, k * 128:(k + 1) * 128],
                     yhb[:, k * 128:(k + 1) * 128], transpose=True)
        W("sync", "slq", tv)
        for k in range(8):
            dma("sync", "sst", yTd[k * 128:(k + 1) * 128, t * 128:(t + 1) * 128],
                yhT[:, k * 128:(k + 1) * 128])
        W("tensor", "slq", tv)
        tlg = mm_acc(2, [(yhT[:, k * 128:(k + 1) * 128],
                    wg8_b.ap()[:, k * E:(k + 1) * E]) for k in range(8)], n=E, sem="stn")
        W("vector", "stn", tlg)
        vlg = ev("vector", lambda e: e.tensor_copy(lgf.ap()[:, 0:E], pss[2].ap()[:, 0:E]), sem="sv")
        ps_free[2] = ("sv", vlg)
        drain("vector")
        ev("vector", lambda e: e.reduce_max(stats[:, 10:11], lgf.ap()[:, 0:E], axis=AX, negate=True))
        vng = ev("vector", lambda e: e.drain(), sem="sv")
        W("scalar", "sv", vng)
        sle = ev("scalar", lambda e: e.activation(lgf.ap()[:, 16:16 + E], lgf.ap()[:, 0:E], AF.Exp, bias=stats[:, 10:11]), sem="ss")
        W("vector", "ss", sle)
        ev("vector", lambda e: e.reduce_sum(stats[:, 11:12], lgf.ap()[:, 16:16 + E], axis=AX))
        drain("vector")
        ev("vector", lambda e: e.reciprocal(stats[:, 12:13], stats[:, 11:12]))
        drain("vector")
        ev("vector", lambda e: e.tensor_scalar(lgf.ap()[:, 0:E], lgf.ap()[:, 16:16 + E], stats[:, 12:13], None, op0=ALU.mult))
        vsc = ev("vector", lambda e: e.drain(), sem="sv")
        W("sync", "sv", vsc)
        dma("sync", "sst", scd[t * 128:(t + 1) * 128, :], lgf.ap()[:, 0:E])
        if t + 3 < TILES:
            xld[(t + 3) % 3] = dma("sync", SXS[(t + 3) % 3], xs[(t + 3) % 3][:, :],
                                   x_in[(t + 3) * 128:(t + 4) * 128, :])

    if "p0" in phases:
        emit_X(0)
        emit_qkv_evac()
        for t in range(TILES):
            if t + 1 < TILES:
                emit_X(t + 1)
            emit_Y(t)

    if "p1" in phases:
        W("gpsimd", "sst")
        ev("gpsimd", lambda e: e.collective_compute(
            "AllGather", ALU.bypass, replica_groups=[list(range(NC_N))],
            ins=[scd.opt()], outs=[sc_all.opt()]), sem="scc")
        ev("gpsimd", lambda e: e.collective_compute(
            "AllGather", ALU.bypass, replica_groups=[list(range(NC_N))],
            ins=[ag1y_i.opt()], outs=[y_all.opt()]), sem="scc")
        ag_sc, ag_yt = 1, 2

        last_veo = [None]

        def emit_glu_pass(ngroups, scale_cols, accum, yt_src, out_dram, wait_scc):
            prev_mm = [None, None]
            gtv = [None, None]
            for g in range(ngroups):
                pg = g % 2
                tok0 = g * GRP
                sytk = "syt0" if pg == 0 else "syt1"
                if prev_mm[pg] is not None:
                    W("scalar", "stn", prev_mm[pg])
                rnk = tok0 // RNK
                toff = tok0 % RNK
                if wait_scc and g < 2:
                    W("scalar", "scc", ag_yt)
                vy = dma("scalar", sytk,
                         yt_g[pg].rearrange("p (c t) -> p c t", c=8),
                         yt_src[rnk * D:(rnk + 1) * D, toff:toff + GRP].rearrange("(c p) t -> p c t", p=128))
                W("tensor", sytk, vy)
                for fc in range(16):
                    sA, sB = fc % 2, 2 + fc % 2
                    free_ps(sA); free_ps(sB)
                    for dc in range(8):
                        ev("tensor", (lambda sA=sA, dc=dc, fc=fc, pg=pg: lambda e: e.matmul(
                            pss[sA].ap()[:, 0:GRP],
                            wgt_b[:, dc * FF + fc * 128: dc * FF + (fc + 1) * 128],
                            yt_g[pg][:, dc * GRP:(dc + 1) * GRP],
                            start=(dc == 0), stop=(dc == 7)))())
                    tga = None
                    for dc in range(8):
                        fn = (lambda sB=sB, dc=dc, fc=fc, pg=pg: lambda e: e.matmul(
                            pss[sB].ap()[:, 0:GRP],
                            wut_b[:, dc * FF + fc * 128: dc * FF + (fc + 1) * 128],
                            yt_g[pg][:, dc * GRP:(dc + 1) * GRP],
                            start=(dc == 0), stop=(dc == 7)))()
                        if dc == 7:
                            tga = ev("tensor", fn, sem="stn")
                        else:
                            ev("tensor", fn)
                    if fc == 15:
                        prev_mm[pg] = tga
                    W("scalar", "stn", tga)
                    if gtv[fc % 2] is not None:
                        W("scalar", "sv", gtv[fc % 2])
                    sg1 = ev("scalar", (lambda fc=fc, sA=sA: lambda e: e.activation(
                        gt_b[fc % 2].ap(), pss[sA].ap()[:, 0:GRP], AF.Gelu))(), sem="ss")
                    W("vector", "ss", sg1)
                    vh = ev("vector", (lambda fc=fc, sB=sB: lambda e: e.tensor_tensor(
                        hb_g[:, fc * GRP:(fc + 1) * GRP], gt_b[fc % 2].ap(), pss[sB].ap()[:, 0:GRP], op=ALU.mult))(), sem="sv")
                    gtv[fc % 2] = vh
                    ps_free[sA] = ("sv", C["sv"])
                    ps_free[sB] = ("sv", C["sv"])
                W("tensor", "sv", C["sv"])
                assert not accum
                first_evac = True
                for ch in range(4):
                    marks = {}
                    for half in range(2):
                        slot = (ch * 2 + half) % 6
                        free_ps(slot)
                        for fc in range(16):
                            fn = (lambda slot=slot, fc=fc, ch=ch, half=half: lambda e: e.matmul(
                                pss[slot].ap()[:, 0:512],
                                hb_g[:, fc * GRP + ch * 128: fc * GRP + (ch + 1) * 128],
                                wdt_b[:, fc * D + half * 512: fc * D + (half + 1) * 512],
                                start=(fc == 0), stop=(fc == 15)))()
                            if fc == 15:
                                marks[half] = ev("tensor", fn, sem="stn")
                            else:
                                ev("tensor", fn)
                    W("vector", "stn", marks[1])
                    if first_evac:
                        W("vector", "sac")
                        first_evac = False
                    for half in range(2):
                        slot = (ch * 2 + half) % 6
                        dst = eo_b.ap()[:, ch * D + half * 512: ch * D + (half + 1) * 512]
                        if scale_cols is not None:
                            scol = scale_cols(g, ch)
                            if accum:
                                ev("vector", (lambda dst=dst, slot=slot, scol=scol, ch=ch, half=half: lambda e: e.scalar_tensor_tensor(
                                    dst, pss[slot].ap()[:, 0:512], scol,
                                    acb.ap()[:, ch * D + half * 512: ch * D + (half + 1) * 512],
                                    op0=ALU.mult, op1=ALU.add))())
                            else:
                                ev("vector", (lambda dst=dst, slot=slot, scol=scol: lambda e: e.tensor_scalar(
                                    dst, pss[slot].ap()[:, 0:512], scol, None, op0=ALU.mult))())
                        else:
                            ev("vector", (lambda dst=dst, slot=slot: lambda e: e.tensor_copy(
                                dst, pss[slot].ap()[:, 0:512]))())
                        drain("vector")
                    vch = ev("vector", lambda e: e.drain(), sem="sv")
                    ps_free[(ch * 2) % 6] = ("sv", vch)
                    ps_free[(ch * 2 + 1) % 6] = ("sv", vch)
                veo = vch
                last_veo[0] = veo
                W("sync", "sv", veo)
                dma("sync", "sac", out_dram[tok0:tok0 + GRP, :].rearrange("(c p) d -> p c d", p=128),
                    eo_b.ap().rearrange("p (c d) -> p c d", d=D))

        W("vector", "stn")
        W("vector", "slq")
        W("vector", "sst")
        W("vector", "sx0"); W("vector", "sx1"); W("vector", "sx2")
        W("scalar", "stn"); W("scalar", "sst"); W("scalar", "slq"); W("scalar", "slq2")
        W("scalar", "sx0"); W("scalar", "sx1"); W("scalar", "sx2")
        load_cast(WsgT, wgt_b)
        load_cast(WsuT, wut_b)
        load_cast(WsdT, wdt_b)
        wsh = ev("scalar", lambda e: e.copy(stats[0:1, 14:15], stg_f32[0:1, 0:1]), sem="ss")
        W("tensor", "ss", wsh)
        W("sync", "sst")
        W("scalar", "sst")
        emit_glu_pass(NGS, None, False, yTd, acc_sh, wait_scc=False)

        W("sync", "scc", ag_sc)
        vsl = None
        for bb in range(B):
            vsl = dma("sync", "ssc", sc_b[bb].rearrange("p (c e) -> p c e", e=E),
                      sc_all[bb * SB:(bb + 1) * SB, :].rearrange("(c p) e -> p c e", p=128))
        W("vector", "ssc", vsl)
        W("vector", "sw"); W("vector", "sw1"); W("vector", "ss")
        for j in range(4):
            bb = j // 2
            sc3v = sc_b[bb].rearrange("p (c e) -> p c e", e=E)
            ev("vector", (lambda j=j, sc3v=sc3v: lambda e: e.tensor_tensor(
                stg_f32.rearrange("p (c e) -> p c e", e=E)[:, 0:CHB, :], sc3v,
                eh_s.ap()[:, j * E:(j + 1) * E].unsqueeze(1).to_broadcast([128, CHB, E]), op=ALU.mult))())
            drain("vector")
            ev("vector", (lambda j=j: lambda e: e.tensor_reduce(
                scj.ap()[:, j * CHB:(j + 1) * CHB],
                stg_f32.rearrange("p (c e) -> p c e", e=E)[:, 0:CHB, :], axis=AX, op=ALU.add))())
            drain("vector")
        ev("vector", lambda e: e.drain(), sem="sv")

        tau_sl = [0, 16]
        tcm = None
        for it in range(NBIS):
            cur, nxt = tau_sl
            lo_c = tt.ap()[0:1, cur:cur + 4]
            hi_c = tt.ap()[0:1, cur + 4:cur + 8]
            lo_n = tt.ap()[0:1, nxt:nxt + 4]
            hi_n = tt.ap()[0:1, nxt + 4:nxt + 8]
            mid = tt.ap()[0:1, 8:12]
            cnt_f = tt.ap()[0:1, 12:16]
            ev("vector", (lambda mid=mid, lo_c=lo_c, hi_c=hi_c: lambda e: e.tensor_tensor(
                mid, lo_c, hi_c, op=ALU.add))())
            drain("vector")
            ev("vector", (lambda mid=mid: lambda e: e.tensor_scalar(
                mid, mid, 0.5, None, op0=ALU.mult))())
            drain("vector")
            ev("vector", (lambda mid=mid: lambda e: e.tensor_copy(
                bc1.ap().rearrange("o (p j) -> o p j", j=4),
                mid.unsqueeze(1).to_broadcast([1, 128, 4])))())
            vb = ev("vector", lambda e: e.drain(), sem="sv")
            W("sync", "sv", vb)
            vst = dma("sync", "stx", taux_d, bc1.ap()[0:1, :])
            W("sync", "stx", vst)
            vld = dma("sync", "stx", cand128.ap(), taux_d.rearrange("(p j) -> p j", p=128))
            W("vector", "stx", vld)
            if tcm is not None:
                W("vector", "stn", tcm)
            for j in range(4):
                ev("vector", (lambda j=j: lambda e: e.tensor_single_scalar(
                    mk_b.ap()[:, j * CHB:(j + 1) * CHB], scj.ap()[:, j * CHB:(j + 1) * CHB],
                    cand128.ap()[:, j:j + 1], op=ALU.is_ge))())
            vmk = ev("vector", lambda e: e.drain(), sem="sv")
            W("tensor", "sv", vmk)
            if pcc_free[0] is not None:
                W("tensor", pcc_free[0][0], pcc_free[0][1])
                pcc_free[0] = None
            tcm = ev("tensor", lambda e: e.matmul(pcc.ap()[0:1, 0:4 * CHB], ones_col.ap(),
                                                  mk_b.ap(), start=True, stop=True), sem="stn")
            W("vector", "stn", tcm)
            ev("vector", (lambda cnt_f=cnt_f: lambda e: e.tensor_reduce(
                cnt_f.rearrange("o (j x) -> o j x", x=1),
                pcc.ap()[0:1, 0:4 * CHB].rearrange("o (j c) -> o j c", c=CHB), axis=AX, op=ALU.add))())
            vrd = ev("vector", lambda e: e.drain(), sem="sv")
            pcc_free[0] = ("sv", C["sv"])
            ev("vector", (lambda cnt_f=cnt_f: lambda e: e.tensor_scalar(
                stg_f32[0:1, 0:4], cnt_f, float(CAP), None, op0=ALU.is_ge))())
            drain("vector")
            ev("vector", lambda e: e.tensor_copy(mk16.ap(), stg_f32[0:1, 0:4]))
            drain("vector")
            ev("vector", (lambda lo_n=lo_n, mid=mid, lo_c=lo_c: lambda e: e.select(
                lo_n, mk16.ap(), mid, lo_c, add_drain=True))())
            drain("vector")
            ev("vector", (lambda hi_n=hi_n, hi_c=hi_c, mid=mid: lambda e: e.select(
                hi_n, mk16.ap(), hi_c, mid, add_drain=True))())
            drain("vector")
            tau_sl = [nxt, cur]
        fin = tau_sl[0]
        ev("vector", (lambda fin=fin: lambda e: e.tensor_copy(
            bc1.ap().rearrange("o (p j) -> o p j", j=4),
            tt.ap()[0:1, fin:fin + 4].unsqueeze(1).to_broadcast([1, 128, 4])))())
        vb = ev("vector", lambda e: e.drain(), sem="sv")
        W("sync", "sv", vb)
        vst = dma("sync", "stx", taux_d, bc1.ap()[0:1, :])
        W("sync", "stx", vst)
        vld = dma("sync", "stx", tau128.ap(), taux_d.rearrange("(p j) -> p j", p=128))
        W("vector", "stx", vld)
        if d_tau is not None:
            vtt = ev("vector", lambda e: e.engine_nop(), sem="sv")
            W("sync", "sv", vtt)
            dma("sync", "sst", d_tau[:, :], tt.ap())
        for j in range(4):
            ev("vector", (lambda j=j: lambda e: e.tensor_single_scalar(
                wsc.ap()[:, j * CHB:(j + 1) * CHB], scj.ap()[:, j * CHB:(j + 1) * CHB],
                tau128.ap()[:, j:j + 1], op=ALU.is_ge))())
            drain("vector")
            ev("vector", (lambda j=j: lambda e: e.tensor_tensor(
                wsc.ap()[:, j * CHB:(j + 1) * CHB], wsc.ap()[:, j * CHB:(j + 1) * CHB],
                scj.ap()[:, j * CHB:(j + 1) * CHB], op=ALU.mult))())
            drain("vector")
        for j in range(4):
            ev("vector", (lambda j=j: lambda e: e.tensor_single_scalar(
                mk_b.ap()[:, j * CHB:(j + 1) * CHB], scj.ap()[:, j * CHB:(j + 1) * CHB],
                tau128.ap()[:, j:j + 1], op=ALU.is_ge))())
            drain("vector")
        vwsc = ev("vector", lambda e: e.drain(), sem="sv")
        if d_mkb is not None:
            W("sync", "sv", vwsc)
            dma("sync", "sst", d_mkb[:, :], mk_b.ap())
        if d_wsc is not None:
            W("sync", "sv", vwsc)
            dma("sync", "sst", d_wsc[:, :], wsc.ap())
        if d_scj is not None:
            W("sync", "sv", vwsc)
            dma("sync", "sst", d_scj[:, :], scj.ap())
        W("sync", "sv", vwsc)

        stgbf = stg.ap().bitcast(BF16)

        NCG = SB // 1024
        SLT = NCG * 256
        MGN = SLT // 512

        def emit_routed(le):
            for bb in range(B):
                j = 2 * bb + le
                mkj = mk_b.ap()[:, j * CHB:(j + 1) * CHB]
                free_ps(4)
                if pcc_free[0] is not None:
                    W("tensor", pcc_free[0][0], pcc_free[0][1])
                    pcc_free[0] = None
                W("tensor", "sv", C["sv"])
                t1 = ev("tensor", (lambda mkj=mkj: lambda e: e.matmul(pss[4].ap()[:, 0:CHB], ltb.ap(),
                                                     mkj, start=True, stop=True))(), sem="stn")
                t2 = ev("tensor", (lambda mkj=mkj: lambda e: e.matmul(pcc.ap()[0:1, 0:CHB], ones_col.ap(),
                                                     mkj, start=True, stop=True))(), sem="stn")
                W("vector", "stn", t2)
                W("vector", "ss")
                if d_poss is not None:
                    W("vector", "sst")
                ev("vector", (lambda mkj=mkj: lambda e: e.tensor_sub(posc.ap()[:, 0:CHB], pss[4].ap()[:, 0:CHB], mkj))())
                drain("vector")
                ra = stg_f32[0:1, 0:CHB]
                rb = stg_f32[0:1, 64:64 + CHB]
                rc = stg_f32[0:1, 128:128 + CHB]
                ev("vector", lambda e: e.tensor_copy(ra, pcc.ap()[0:1, 0:CHB]))
                drain("vector")
                ev("vector", lambda e: e.memset(rb, 0.0))
                drain("vector")
                ev("vector", lambda e: e.tensor_copy(rb[:, 1:CHB], ra[:, 0:CHB - 1]))
                drain("vector")
                cur, oth = rb, ra
                for sh in (1, 2, 4, 8, 16, 32):
                    if sh >= CHB:
                        break
                    ev("vector", (lambda oth=oth, cur=cur, sh=sh: lambda e: e.tensor_copy(oth[:, 0:sh], cur[:, 0:sh]))())
                    drain("vector")
                    ev("vector", (lambda oth=oth, cur=cur, sh=sh: lambda e: e.tensor_add(
                        oth[:, sh:CHB], cur[:, sh:CHB], cur[:, 0:CHB - sh]))())
                    drain("vector")
                    cur, oth = oth, cur
                gv = cur.rearrange("o (g k) -> o g k", k=8)
                gs = gv[:, :, 0:1].to_broadcast([1, CHB // 8, 8])
                ev("vector", (lambda cur=cur, gv=gv, gs=gs: lambda e: e.tensor_sub(
                    rc.rearrange("o (g k) -> o g k", k=8), gv, gs))())
                vps = ev("vector", lambda e: e.drain(), sem="sv")
                ps_free[4] = ("sv", vps)
                pcc_free[0] = ("sv", vps)
                W("sync", "sv", vps)
                vst = dma("sync", "stx", pcb_d.rearrange("(p c) -> p c", c=CHB).unsqueeze(0),
                          rc.unsqueeze(1).to_broadcast([1, 128, CHB]))
                W("sync", "stx", vst)
                vld = dma("sync", "stx", cb128.ap()[:, 0:CHB], pcb_d.rearrange("(p c) -> p c", c=CHB))
                W("vector", "stx", vld)
                ev("vector", lambda e: e.tensor_add(posc.ap()[:, 0:CHB], posc.ap()[:, 0:CHB], cb128.ap()[:, 0:CHB]))
                drain("vector")
                ev("vector", lambda e: e.tensor_scalar(cb128.ap()[:, 0:CHB], posc.ap()[:, 0:CHB], 256.0, 2048.0, op0=ALU.is_ge, op1=ALU.mult))
                drain("vector")
                ev("vector", lambda e: e.tensor_add(posc.ap()[:, 0:CHB], posc.ap()[:, 0:CHB], cb128.ap()[:, 0:CHB]))
                drain("vector")
                ev("vector", (lambda mkj=mkj: lambda e: e.tensor_scalar(cb128.ap()[:, 0:CHB], mkj, -4096.0, 4096.0, op0=ALU.mult, op1=ALU.add))())
                drain("vector")
                ev("vector", lambda e: e.tensor_add(poss.ap()[:, 0:CHB], posc.ap()[:, 0:CHB], cb128.ap()[:, 0:CHB]))
                vposs = ev("vector", lambda e: e.drain(), sem="sv")
                if d_poss is not None:
                    W("sync", "sv", vposs)
                    dma("sync", "sst", d_poss[j * 128:(j + 1) * 128, 0:CHB], poss.ap()[:, 0:CHB])
                    dma("sync", "sst", d_poss[j * 128:(j + 1) * 128, 64:64 + CHB], posc.ap()[:, 0:CHB])
                    dma("sync", "sst", d_poss[j * 128:(j + 1) * 128, 128:128 + CHB], cb128.ap()[:, 0:CHB])

                W("sync", "sv", vposs)
                W("sync", "ss")
                W("sync", "scc", ag_yt)
                W("sync", "stn", C["stn"])
                W("scalar", "stn", C["stn"])
                W("vector", "sac")
                pstrip = stgbf
                pbrd = [None]
                gytr = [None, None]
                for cg in range(NCG):
                    for ss in range(4):
                        free_ps(ss)
                    if cg > 0 or (bb > 0 or le > 0):
                        W("sync", "stn", C["stn"])
                    vyb = None
                    for sb in range(8):
                        c8 = cg * 8 + sb
                        vyb = dma("sync", "syt0", hb2[:, sb * 1024:(sb + 1) * 1024],
                                  y_all[bb * SB + c8 * 128: bb * SB + (c8 + 1) * 128, :])
                    if pbrd[0] is not None:
                        W("vector", "stn", pbrd[0])
                    W("vector", "sw"); W("vector", "sw1"); W("vector", "ss")
                    for sb in range(8):
                        for ob in range(2):
                            ev("vector", (lambda sb=sb, ob=ob, c=cg * 8 + sb: lambda e: e.tensor_single_scalar(
                                pstrip[:, (sb * 2 + ob) * 128:(sb * 2 + ob + 1) * 128],
                                iotar_s.ap()[:, ob * 128:(ob + 1) * 128],
                                poss.ap()[:, c:c + 1], op=ALU.is_equal))())
                    vpb = ev("vector", lambda e: e.drain(), sem="sv")
                    W("tensor", "sv", vpb)
                    W("tensor", "syt0", vyb)
                    for sb in range(8):
                        for ob in range(2):
                            for half in range(2):
                                fn = (lambda ob=ob, half=half, sb=sb: lambda e: e.matmul(
                                    pss[ob * 2 + half].ap()[:, 0:512],
                                    pstrip[:, (sb * 2 + ob) * 128:(sb * 2 + ob + 1) * 128],
                                    hb2[:, sb * 1024 + half * 512: sb * 1024 + (half + 1) * 512],
                                    start=(sb == 0), stop=(sb == 7)))()
                                if sb == 7 and ob == 1 and half == 1:
                                    pbrd[0] = ev("tensor", fn, sem="stn")
                                else:
                                    ev("tensor", fn)
                    for ob in range(2):
                        po = cg * 2 + ob
                        W("vector", "stn", pbrd[0])
                        if gytr[ob] is not None:
                            W("vector", "slq", C["slq"])
                        ev("vector", (lambda ob=ob: lambda e: e.tensor_copy(
                            eo_b.ap()[:, ob * 1024: ob * 1024 + 512], pss[ob * 2].ap()[:, 0:512]))())
                        drain("vector")
                        vg = ev("vector", (lambda ob=ob: lambda e: e.tensor_copy(
                            eo_b.ap()[:, ob * 1024 + 512: (ob + 1) * 1024], pss[ob * 2 + 1].ap()[:, 0:512]))(), sem="sv")
                        ps_free[ob * 2] = ("sv", vg)
                        ps_free[ob * 2 + 1] = ("sv", vg)
                        W("scalar", "sv", vg)
                        for dc in range(8):
                            tvx = dma("scalar", "slq", xTs[:, dc * SLT + po * 128: dc * SLT + (po + 1) * 128],
                                      eo_b.ap()[:, ob * 1024 + dc * 128: ob * 1024 + (dc + 1) * 128], transpose=True)
                        gytr[ob] = tvx

                W("tensor", "slq", C["slq"])
                gtv2 = [None, None]
                pb2rd = [None]
                pb2rd_d = {}
                ptt_ev = {}
                last_sc_ev = [None]
                for mg in range(MGN):
                    for fc in range(16):
                        sA, sB2 = fc % 2, 2 + fc % 2
                        free_ps(sA); free_ps(sB2)
                        for dc in range(8):
                            ev("tensor", (lambda sA=sA, dc=dc, fc=fc, mg=mg: lambda e: e.matmul(
                                pss[sA].ap()[:, 0:GRP],
                                wgt_b[:, dc * FF + fc * 128: dc * FF + (fc + 1) * 128],
                                xTs[:, dc * SLT + mg * 512: dc * SLT + (mg + 1) * 512],
                                start=(dc == 0), stop=(dc == 7)))())
                        tga = None
                        for dc in range(8):
                            fn = (lambda sB2=sB2, dc=dc, fc=fc, mg=mg: lambda e: e.matmul(
                                pss[sB2].ap()[:, 0:GRP],
                                wut_b[:, dc * FF + fc * 128: dc * FF + (fc + 1) * 128],
                                xTs[:, dc * SLT + mg * 512: dc * SLT + (mg + 1) * 512],
                                start=(dc == 0), stop=(dc == 7)))()
                            if dc == 7:
                                tga = ev("tensor", fn, sem="stn")
                            else:
                                ev("tensor", fn)
                        W("scalar", "stn", tga)
                        if gtv2[fc % 2] is not None:
                            W("scalar", "sv", gtv2[fc % 2])
                        sg1 = ev("scalar", (lambda fc=fc, sA=sA: lambda e: e.activation(
                            gt_b[fc % 2].ap(), pss[sA].ap()[:, 0:GRP], AF.Gelu))(), sem="ss")
                        W("vector", "ss", sg1)
                        vh = ev("vector", (lambda fc=fc, sB2=sB2: lambda e: e.tensor_tensor(
                            hb2[:, fc * GRP:(fc + 1) * GRP], gt_b[fc % 2].ap(), pss[sB2].ap()[:, 0:GRP], op=ALU.mult))(), sem="sv")
                        gtv2[fc % 2] = vh
                        ps_free[sA] = ("sv", C["sv"])
                        ps_free[sB2] = ("sv", C["sv"])
                    W("tensor", "sv", C["sv"])
                    for ch in range(4):
                        marks = {}
                        for half in range(2):
                            slot = (ch * 2 + half) % 6
                            free_ps(slot)
                            for fc in range(16):
                                fn = (lambda slot=slot, fc=fc, ch=ch, half=half: lambda e: e.matmul(
                                    pss[slot].ap()[:, 0:512],
                                    hb2[:, fc * GRP + ch * 128: fc * GRP + (ch + 1) * 128],
                                    wdt_b[:, fc * D + half * 512: fc * D + (half + 1) * 512],
                                    start=(fc == 0), stop=(fc == 15)))()
                                if fc == 15:
                                    marks[half] = ev("tensor", fn, sem="stn")
                                else:
                                    ev("tensor", fn)
                        W("vector", "stn", marks[1])
                        for half in range(2):
                            slot = (ch * 2 + half) % 6
                            ev("vector", (lambda ch=ch, half=half, slot=slot: lambda e: e.tensor_copy(
                                eoI[:, ch * 1024 + half * 512: ch * 1024 + (half + 1) * 512],
                                pss[slot].ap()[:, 0:512]))())
                            drain("vector")
                        vch = ev("vector", lambda e: e.drain(), sem="sv")
                        ps_free[(ch * 2) % 6] = ("sv", vch)
                        ps_free[(ch * 2 + 1) % 6] = ("sv", vch)
                    for cgl in range(2):
                        cg = 2 * mg + cgl
                        if pb2rd[0] is not None:
                            W("vector", "stn", C["stn"])
                        W("vector", "sw"); W("vector", "sw1"); W("vector", "ss")
                        for tbl in range(8):
                            for ob in range(2):
                                ev("vector", (lambda tbl=tbl, ob=ob, c=cg * 8 + tbl: lambda e: e.tensor_single_scalar(
                                    pstrip[:, (tbl * 2 + ob) * 128:(tbl * 2 + ob + 1) * 128],
                                    iotar_s.ap()[:, ob * 128:(ob + 1) * 128],
                                    poss.ap()[:, c:c + 1], op=ALU.is_equal))())
                        vps1 = ev("vector", lambda e: e.drain(), sem="sv")
                        tmark = {}
                        emark = {}
                        for tbl in range(8):
                            c = cg * 8 + tbl
                            free_ps(4); free_ps(5)
                            for ob in range(2):
                                k = tbl * 2 + ob
                                p2 = k % 4
                                if k == 0:
                                    W("tensor", "sv", vps1)
                                if ptt_ev.get(0) is not None:
                                    W("tensor", "sv", ptt_ev[0])
                                tmark[k] = ev("tensor", (lambda k=k: lambda e: e.transpose(
                                    ptt.ap(), pstrip[:, k * 128:(k + 1) * 128], idm_b.ap()))(), sem="stn")
                                W("vector", "stn", tmark[k])
                                if pb2rd_d.get(p2) is not None:
                                    W("vector", "stn", pb2rd_d[p2])
                                emark[k] = ev("vector", (lambda p2=p2: lambda e: e.tensor_copy(
                                    Pb2[p2].ap(), ptt.ap()))(), sem="sv")
                                ptt_ev[0] = emark[k]
                            for ob in range(2):
                                k = tbl * 2 + ob
                                p2 = k % 4
                                W("tensor", "sv", emark[k])
                                for half in range(2):
                                    fn = (lambda p2=p2, half=half, cgl=cgl, ob=ob: lambda e: e.matmul(
                                        pss[4 + half].ap()[:, 0:512],
                                        Pb2[p2].ap(),
                                        eoI[:, (cgl * 2 + ob) * 1024 + half * 512: (cgl * 2 + ob) * 1024 + (half + 1) * 512],
                                        start=(ob == 0), stop=(ob == 1)))()
                                    if half == 1:
                                        smark = ev("tensor", fn, sem="stn")
                                    else:
                                        ev("tensor", fn)
                                pb2rd_d[p2] = smark
                            pb2rd[0] = smark
                            if le == 1:
                                if last_sc_ev[0] is not None:
                                    W("sync", "sv", last_sc_ev[0])
                                va = dma("sync", "sar", acb.ap()[:, 0:D],
                                         acc_d[bb * SB + c * 128: bb * SB + (c + 1) * 128, :])
                            W("vector", "stn", smark)
                            W("vector", "sac")
                            if le == 1:
                                W("vector", "sar", va)
                            scol = wsc.ap()[:, j * CHB + c: j * CHB + c + 1]
                            for half in range(2):
                                dst = eo_b.ap()[:, 2048 + half * 512: 2048 + (half + 1) * 512]
                                if le == 1:
                                    ev("vector", (lambda dst=dst, half=half, scol=scol: lambda e: e.scalar_tensor_tensor(
                                        dst, pss[4 + half].ap()[:, 0:512], scol,
                                        acb.ap()[:, half * 512:(half + 1) * 512],
                                        op0=ALU.mult, op1=ALU.add))())
                                else:
                                    ev("vector", (lambda dst=dst, half=half, scol=scol: lambda e: e.tensor_scalar(
                                        dst, pss[4 + half].ap()[:, 0:512], scol, None, op0=ALU.mult))())
                                drain("vector")
                            veo = ev("vector", lambda e: e.drain(), sem="sv")
                            last_sc_ev[0] = veo
                            ps_free[4] = ("sv", veo)
                            ps_free[5] = ("sv", veo)
                            W("sync", "sv", veo)
                            dma("sync", "sac", acc_d[bb * SB + c * 128: bb * SB + (c + 1) * 128, :],
                                eo_b.ap()[:, 2048:3072])

        for le in range(2):
            W("vector", "stn")
            W("scalar", "stn")
            W("sync", "stn", C["stn"])
            load_cast(WgT[le], wgt_b)
            load_cast(WuT[le], wut_b)
            load_cast(WdT[le], wdt_b)
            wexp = ev("scalar", lambda e: e.copy(stats[0:1, 14:15], stg_f32[0:1, 0:1]), sem="ss")
            W("tensor", "ss", wexp)
            emit_routed(le)

        W("gpsimd", "sac")
        ev("gpsimd", lambda e: e.collective_compute(
            "ReduceScatter", ALU.add, replica_groups=[list(range(NC_N))],
            ins=[acc_d.opt()], outs=[rs_out.opt()]), sem="scc")

    if "p2" in phases:
        W("sync", "scc", C["scc"])
        for t in range(TILES):
            base = t * 128
            W("vector", "sfin")
            v0 = dma("sync", "sar", at[0], rs_out[base:base + 128, :])
            v1 = dma("sync", "sar", at[1], acc_sh[base:base + 128, :])
            vxp2 = dma("sync", "sar", xpf.ap(), xp_dram[base:base + 128, :])
            W("vector", "sar", vxp2)
            W("vector", "sac")
            ev("vector", lambda e: e.tensor_add(ot2, at[0], at[1]))
            ln_norm(ot2, ao2.ap(), pacc.ap())
            ev("vector", lambda e: e.tensor_add(ao2.ap(), ao2.ap(), xpf.ap()))
            vfin = ev("vector", lambda e: e.drain(), sem="sv")
            W("sync", "sv", vfin)
            dma("sync", "sfin", out[base:base + 128, :], ao2.ap())
        W("sync", "sfin")
        W("gpsimd", "sfin")

    blk = es.enter_context(nc.Block())
    for eng in ("sync", "scalar", "vector", "tensor", "gpsimd"):
        fns = streams[eng]
        getattr(blk, eng)(lambda e, fns=fns: [f(e) for f in fns])
    es.close()
    return nc, dbg


def _prep_inputs(cfg, inputs):
    SB, B, TC = cfg["SB"], cfg["B"], cfg["TC"]
    L = "L0_"
    f32 = lambda a: np.ascontiguousarray(np.asarray(a, dtype=np.float32))
    x = f32(inputs[L + "attn_x"]).reshape(B * SB, D)
    Wq = f32(inputs[L + "attn_Wq"]); bq = f32(inputs[L + "attn_bq"]).reshape(1, D)
    Wdn = f32(inputs[L + "attn_Wdown"]); bdn = f32(inputs[L + "attn_bdown"]).reshape(1, DLAT)
    Wup = f32(inputs[L + "attn_Wup"]); bup = f32(inputs[L + "attn_bup"]).reshape(1, 2 * D)
    Wo = f32(inputs[L + "attn_Wout"]); bo = f32(inputs[L + "attn_bout"]).reshape(1, D)
    gateT = np.ascontiguousarray(f32(inputs[L + "ffn_gate_w"]).T)
    WsgT = np.ascontiguousarray(f32(inputs[L + "ffn_Ws_g"]).T)
    WsuT = np.ascontiguousarray(f32(inputs[L + "ffn_Ws_u"]).T)
    WsdT = np.ascontiguousarray(f32(inputs[L + "ffn_Ws_d"]).T)
    Wg = f32(inputs[L + "ffn_Wg"])
    Wu = f32(inputs[L + "ffn_Wu"])
    Wd = f32(inputs[L + "ffn_Wd"])
    LT = np.tril(np.ones((128, 128), np.float32)).T
    iotar = np.tile(np.arange(256, dtype=np.float32), (128, 1))
    idm = np.eye(128, dtype=np.float32)
    maps = []
    for c in range(NC_N):
        eh = np.zeros((128, 4 * E), np.float32)
        for j in range(4):
            eh[:, j * E + 2 * c + (j % 2)] = 1.0
        m = {
            "x": x[c * TC:(c + 1) * TC],
            "Wq": Wq, "bq": bq, "Wdn": Wdn, "bdn": bdn,
            "Wup": Wup, "bup": bup, "Wo": Wo, "bo": bo,
            "gateT": gateT, "WsgT": WsgT, "WsuT": WsuT, "WsdT": WsdT,
            "eh_in": eh, "LT_in": LT, "iotar_in": iotar, "idm_in": idm,
        }
        for le in range(2):
            eg = 2 * c + le
            m[f"Wg{le}T"] = np.ascontiguousarray(Wg[eg].T)
            m[f"Wu{le}T"] = np.ascontiguousarray(Wu[eg].T)
            m[f"Wd{le}T"] = np.ascontiguousarray(Wd[eg].T)
        maps.append(m)
    return maps


def kernel(**inputs):
    cfg = make_cfg()
    nc, _dbg = build_nc(cfg)
    in_maps = _prep_inputs(cfg, inputs)
    res = run_bass_kernel_spmd(nc, in_maps, core_ids=list(range(NC_N)))
    outs = [np.asarray(res.results[c]["out"]) for c in range(NC_N)]
    full = np.concatenate(outs, axis=0)
    return full.reshape(cfg["B"], cfg["SB"], D)


# version 12
# speedup vs baseline: 1.0518x; 1.0216x over previous
import numpy as np
import ml_dtypes
from contextlib import ExitStack

import concourse.bass as bass
import concourse.mybir as mybir
from concourse.bass_utils import run_bass_kernel_spmd

F32 = mybir.dt.float32
BF16 = mybir.dt.bfloat16
I16 = mybir.dt.int16
I32 = mybir.dt.int32
AF = mybir.ActivationFunctionType
ALU = mybir.AluOpType
AX = mybir.AxisListType.X

NC_N = 8
D = 1024
H = 16
DH = 64
DLAT = 256
E = 16
FF = 2048
LN_EPS = 1e-5
NBIS = 46


def make_cfg(SB=8192, B=2, phases=("p0", "p1", "p2")):
    c = {"SB": SB, "B": B}
    c["TC"] = SB * B // NC_N
    c["CAP"] = SB // 8
    c["TILES"] = c["TC"] // 128
    c["phases"] = phases
    return c


def build_nc(cfg, debug=()):
    SB, B, TC, CAP, TILES = (cfg[k] for k in ("SB", "B", "TC", "CAP", "TILES"))
    phases = cfg["phases"]
    TT = B * SB
    GRP = 512
    NG = TT // GRP
    NGS = TC // GRP
    CHB = SB // 128
    RNK = TC

    nc = bass.Bass("TRN2", target_bir_lowering=False, debug=False)

    def din(name, shape, dt=F32):
        return nc.dram_tensor(name, shape, dt, kind="ExternalInput").ap()

    x_in = din("x", [TC, D])
    Wq = din("Wq", [D, D]); bq = din("bq", [1, D])
    Wdn = din("Wdn", [D, DLAT]); bdn = din("bdn", [1, DLAT])
    Wup = din("Wup", [DLAT, 2 * D]); bup = din("bup", [1, 2 * D])
    Wo = din("Wo", [D, D]); bo = din("bo", [1, D])
    gateT = din("gateT", [D, E])
    WsgT = din("WsgT", [D, FF]); WsuT = din("WsuT", [D, FF]); WsdT = din("WsdT", [FF, D])
    WgT = [din(f"Wg{i}T", [D, FF]) for i in range(2)]
    WuT = [din(f"Wu{i}T", [D, FF]) for i in range(2)]
    WdT = [din(f"Wd{i}T", [FF, D]) for i in range(2)]
    eh_in = din("eh_in", [128, 4 * E])
    LT_in = din("LT_in", [128, 128])
    iotar_in = din("iotar_in", [128, 256])
    idm_in = din("idm_in", [128, 128])

    out = nc.dram_tensor("out", [TC, D], F32, kind="ExternalOutput").ap()
    dbg = {}
    def dout(name, shape, dt=F32):
        if name in debug:
            dbg[name] = nc.dram_tensor("d_" + name, shape, dt, kind="ExternalOutput").ap()
        return dbg.get(name)

    d_xp = dout("xp", [TC, D])
    d_yh = dout("yh", [TC, D], BF16)
    d_tau = dout("tau", [1, 24])
    d_wsc = dout("wsc", [128, 4 * CHB])
    d_scj = dout("scj", [128, 4 * CHB])
    d_poss = dout("poss", [4 * 128, 192])
    d_mkb = dout("mkb", [128, 4 * CHB], BF16)

    xp_dram = nc.dram_tensor("xp_dram", [TC, D], F32).ap()
    yTd = nc.dram_tensor("yTd", [D, TC], BF16).ap()
    ag1y_i = nc.dram_tensor("ag1y_i", [TC, D], BF16).ap()
    y_all = nc.dram_tensor("y_all", [TT, D], BF16).ap()
    pcb_d = nc.dram_tensor("pcb_d", [128 * (SB // 128)], F32).ap()
    scd = nc.dram_tensor("scd", [TC, E], F32).ap()
    sc_all = nc.dram_tensor("sc_all", [NC_N * TC, E], F32).ap()
    taux_d = nc.dram_tensor("taux_d", [512], F32).ap()
    acc_d = nc.dram_tensor("acc_d", [TT, D], BF16).ap()
    rs_out = nc.dram_tensor("rs_out", [TC, D], BF16).ap()
    acc_sh = nc.dram_tensor("acc_sh", [TC, D], BF16).ap()

    es = ExitStack()
    sb = lambda n, s, dt: es.enter_context(nc.sbuf_tensor(n, s, dt))

    BUF_A = sb("BUF_A", [128, 49152], BF16)
    BUF_B = sb("BUF_B", [128, 28672], BF16)
    A_, B_ = BUF_A.ap(), BUF_B.ap()
    wq_b = A_[:, 0:8 * D]
    wo_b = A_[:, 8 * D:16 * D]
    wup_b = A_[:, 16 * D:20 * D]
    wdn_b = A_[:, 20 * D:22 * D]
    REG = A_[:, 22528:22528 + 12 * D]
    EXT = A_[:, 34816:49152]
    xs = [REG[:, 0:2 * D].bitcast(F32), REG[:, 2 * D:4 * D].bitcast(F32),
          EXT[:, 0:2 * D].bitcast(F32)]
    aof = REG[:, 4 * D:6 * D].bitcast(F32)
    kvbP = [REG[:, 6 * D:8 * D], EXT[:, 2 * D:4 * D]]
    xhat = REG[:, 8 * D:9 * D]
    xhatT = REG[:, 9 * D:10 * D]
    qbP = [REG[:, 10 * D:11 * D], EXT[:, 4 * D:5 * D]]
    obT = EXT[:, 5 * D:6 * D]
    yhb = REG[:, 11 * D:12 * D]
    latb = B_[:, 0:DLAT]
    latT = B_[:, DLAT:2 * DLAT]
    scf = B_[:, 512:1024].bitcast(F32)
    Eb = B_[:, 1024:1280]
    ob = B_[:, 1280:2304]
    yhT = B_[:, 2304:3328]
    wgt_b = A_[:, 0:16384]
    wut_b = A_[:, 16384:32768]
    wdt_b = A_[:, 32768:49152]
    yt_g = [B_[:, 0:4096], B_[:, 4096:8192]]
    hb_g = B_[:, 8192:16384]
    xTs = B_[:, 0:16384]
    hb2 = B_[:, 16384:24576]
    eoI = B_[:, 24576:28672]
    at = [B_[:, 0:D], B_[:, D:2 * D]]
    ot2 = B_[:, 2 * D:4 * D].bitcast(F32)

    wg8_b = sb("wg8_b", [128, 8 * E], BF16)
    bq_b = sb("bq_b", [1, D], BF16)
    bdn_b = sb("bdn_b", [1, DLAT], BF16)
    bup_b = sb("bup_b", [1, 2 * D], BF16)
    bo_b = sb("bo_b", [1, D], BF16)
    ones = sb("ones", [1, 128], BF16)
    ones_col = sb("ones_col", [128, 1], BF16)
    stg = sb("stg0", [128, 1024], F32)
    stg_f32 = stg.ap()
    gt_b = [sb(f"gt_b{i}", [128, GRP], BF16) for i in range(2)]
    eo_b = sb("eo_b", [128, 4 * D], BF16)
    acb = sb("acb", [128, D], BF16)
    pacc = sb("pacc", [128, D], F32)
    ptmp = sb("ptmp", [128, D], F32)
    ao2 = sb("ao2", [128, D], F32)
    xpf = sb("xpf", [128, D], F32)
    lgf = sb("lgf", [128, 32], F32)
    stats = sb("stats", [128, 16], F32)
    stats2 = sb("stats2", [128, 2 * H], F32)
    epsb = sb("epsb", [128, 1], F32)
    sc_b = [B_[:, 16384 + i * 4096: 16384 + (i + 1) * 4096].bitcast(F32)[:, 0:CHB * E] for i in range(B)]
    scj = sb("scj", [128, 4 * CHB], F32)
    wsc = sb("wsc", [128, 4 * CHB], F32)
    mk_b = sb("mk_b", [128, 4 * CHB], BF16)
    eh_s = sb("eh_s", [128, 4 * E], F32)
    cand128 = sb("cand128", [128, 4], F32)
    iotar_s = sb("iotar_s", [128, 256], F32)
    ltb = sb("ltb", [128, 128], BF16)
    idm_b = sb("idm_b", [128, 128], BF16)
    posc = sb("posc", [128, 64], F32)
    poss = sb("poss", [128, 64], F32)
    cb128 = sb("cb128", [128, 64], F32)
    Pb4 = sb("Pb4", [128, 512], BF16)
    Pb2 = [sb(f"Pb2_{i}", [128, 128], BF16) for i in range(4)]
    tau128 = sb("tau128", [128, 4], F32)
    tt = sb("tt", [1, 24], F32)
    mk16 = sb("mk16", [1, 4], I16)
    bc1 = sb("bc1", [1, 512], F32)

    pss = [es.enter_context(nc.psum_tensor(f"ps{i}", [128, 512], F32, bank=i)) for i in range(6)]
    ptt = es.enter_context(nc.psum_tensor("ptt", [128, 128], BF16, bank=6))
    pcc = es.enter_context(nc.psum_tensor("pcc", [1, 384], F32, bank=7))

    sems = {}
    for s in ("sw", "sw1", "sx0", "sx1", "sx2", "sst", "slq", "slq2", "syt0", "syt1", "ssc", "stx",
              "sac", "sar", "sfin", "sv", "ss", "stn", "scc"):
        sems[s] = es.enter_context(nc.semaphore(s))
    C = {s: 0 for s in sems}
    ps_free = [None] * 6
    pcc_free = [None]
    ptt_free = [None]
    streams = {k: [] for k in ("sync", "scalar", "vector", "tensor", "gpsimd")}

    def W(eng, sem, val=None):
        v = C[sem] if val is None else val
        streams[eng].append(lambda e: e.wait_ge(sems[sem], v))

    def INC(sem, n=1):
        C[sem] += n
        return sems[sem], C[sem]

    def dma(q, sem, out_ap, in_ap, **kw):
        s, v = INC(sem, 16)
        streams[q].append(lambda e: e.dma_start(out=out_ap, in_=in_ap, **kw).then_inc(s, 16))
        return v

    def ev(eng, fn, sem=None):
        if sem is None:
            streams[eng].append(fn)
            return None
        s, v = INC(sem, 1)
        streams[eng].append(lambda e: fn(e).then_inc(s, 1))
        return v

    def drain(eng):
        streams[eng].append(lambda e: e.drain())

    def ln_norm(src, dst, junk):
        st = lambda i: stats[:, i:i + 1]
        drain("vector")
        ev("vector", lambda e: e.reduce_sum(st(0), src, axis=AX))
        ev("vector", lambda e: e.scalar_tensor_tensor(junk, src, 1.0, src, op0=ALU.mult, op1=ALU.mult, accum_out=st(1)))
        drain("vector")
        ev("vector", lambda e: e.tensor_scalar(st(2), st(0), 1.0 / D, None, op0=ALU.mult))
        ev("vector", lambda e: e.tensor_scalar(st(3), st(1), 1.0 / D, None, op0=ALU.mult))
        drain("vector")
        ev("vector", lambda e: e.tensor_tensor(st(4), st(2), st(2), op=ALU.mult))
        drain("vector")
        vs = ev("vector", lambda e: e.tensor_sub(st(5), st(3), st(4)), sem="sv")
        W("scalar", "sv", vs)
        sq = ev("scalar", lambda e: e.activation(st(6), st(5), AF.Sqrt, bias=epsb[:, :]), sem="ss")
        W("vector", "ss", sq)
        ev("vector", lambda e: e.reciprocal(st(7), st(6)))
        drain("vector")
        ev("vector", lambda e: e.tensor_scalar(dst, src, st(2), st(7), op0=ALU.subtract, op1=ALU.mult))
        drain("vector")

    def free_ps(i):
        if ps_free[i] is not None:
            s, v = ps_free[i]
            W("tensor", s, v)
            ps_free[i] = None

    def mm_acc(slot, pairs, bias_rhs=None, n=512, sem=None):
        free_ps(slot)
        p = pss[slot].ap()[:, 0:n]
        ret = None
        for i, (l, r) in enumerate(pairs):
            first = (i == 0)
            last = (i == len(pairs) - 1) and bias_rhs is None
            fn = (lambda l=l, r=r, f=first, la=last: (lambda e: e.matmul(p, l, r, start=f, stop=la)))()
            if last and sem is not None:
                ret = ev("tensor", fn, sem=sem)
            else:
                ev("tensor", fn)
        if bias_rhs is not None:
            fn = (lambda br=bias_rhs: (lambda e: e.matmul(p, ones.ap()[:, 0:128], br, start=False, stop=True)))()
            if sem is not None:
                ret = ev("tensor", fn, sem=sem)
            else:
                ev("tensor", fn)
        return ret

    ev("vector", lambda e: e.memset(ones.ap(), 1.0))
    ev("vector", lambda e: e.memset(ones_col.ap(), 1.0))
    ev("vector", lambda e: e.memset(epsb.ap(), LN_EPS))
    ev("vector", lambda e: e.memset(tt.ap(), 0.0))
    drain("vector")
    ev("vector", lambda e: e.memset(tt.ap()[0:1, 4:8], 1.0))
    vz = ev("vector", lambda e: e.drain(), sem="sv")
    W("sync", "sv", vz)

    CHW = 1024

    lc_state = {"k": 0, "dv": {0: None, 1: None}}

    def load_cast(dram, dst):
        R, Ccols = dram.shape
        nchunks = R // 128
        src_r = dram.rearrange("(c p) o -> p c o", p=128)
        HW2 = 512
        for c0 in range(nchunks):
            o0 = 0
            while o0 < Ccols:
                take = min(HW2, Ccols - o0)
                pb = lc_state["k"] % 2
                lc_state["k"] += 1
                if lc_state["dv"][pb] is not None:
                    W("sync", "ss", lc_state["dv"][pb])
                v = dma("sync", "sw" if pb == 0 else "sw1",
                        stg_f32[:, pb * 512: pb * 512 + take], src_r[:, c0, o0:o0 + take])
                W("scalar", "sw" if pb == 0 else "sw1", v)
                off = c0 * Ccols + o0
                dv = ev("scalar", (lambda take=take, off=off, dst=dst, pb=pb: lambda e: e.copy(
                    dst[:, off:off + take], stg_f32[:, pb * 512: pb * 512 + take]))(), sem="ss")
                lc_state["dv"][pb] = dv
                o0 += take

    load_cast(Wq, wq_b)
    load_cast(Wdn, wdn_b)
    load_cast(Wup, wup_b)
    load_cast(Wo, wo_b)
    load_cast(gateT, wg8_b.ap())
    W("sync", "ss")
    for (bsrc, bdst) in ((bq, bq_b), (bdn, bdn_b), (bup, bup_b), (bo, bo_b)):
        BW = bsrc.shape[1]
        for o0 in range(0, BW, 1024):
            tk = min(1024, BW - o0)
            v = dma("sync", "sw", stg_f32[0:1, 0:tk], bsrc[:, o0:o0 + tk])
            W("vector", "sw", v)
            ev("vector", lambda e, bdst=bdst, o0=o0, tk=tk: e.tensor_copy(bdst.ap()[:, o0:o0 + tk], stg_f32[0:1, 0:tk]))
            dv = ev("vector", lambda e: e.drain(), sem="sv")
            W("sync", "sv", dv)
    dma("sync", "sw", eh_s.ap(), eh_in)
    dma("sync", "sw", iotar_s.ap(), iotar_in)
    load_cast(LT_in, ltb.ap())
    load_cast(idm_in, idm_b.ap())
    wcast_s = ev("scalar", lambda e: e.copy(stats[0:1, 14:15], stg_f32[0:1, 0:1]), sem="ss")
    wcast = ev("vector", lambda e: e.drain(), sem="sv")

    SXS = ("sx0", "sx1", "sx2")
    xld = [None, None, None]
    for i in range(min(3, TILES)):
        xld[i] = dma("sync", SXS[i], xs[i][:, :], x_in[i * 128:(i + 1) * 128, :])
    kv_pend = [None]
    q_pend = [None]

    def emit_X(t):
        cur = xs[t % 3]
        qb = qbP[t % 2]
        W("vector", SXS[t % 3], xld[t % 3])
        W("vector", "sst")
        ln_norm(cur, xhat, pacc.ap())
        vx = ev("vector", lambda e: e.engine_nop(), sem="sv")
        W("scalar", "sv", vx)
        W("scalar", "sst")
        W("scalar", "stn", C["stn"])
        for k in range(8):
            tv = dma("scalar", "slq2", xhatT[:, k * 128:(k + 1) * 128],
                     xhat[:, k * 128:(k + 1) * 128], transpose=True)
        W("tensor", "slq2", tv)
        W("tensor", "sv", wcast)
        W("tensor", "ss", wcast_s)
        for hh in range(2):
            tq = mm_acc(hh, [(xhatT[:, k * 128:(k + 1) * 128],
                         wq_b[:, k * D + hh * 512: k * D + hh * 512 + 512]) for k in range(8)],
                   bias_rhs=bq_b.ap()[:, hh * 512: hh * 512 + 512], sem="stn")
        tl = mm_acc(2, [(xhatT[:, k * 128:(k + 1) * 128],
                    wdn_b[:, k * DLAT: (k + 1) * DLAT]) for k in range(8)],
               bias_rhs=bdn_b.ap()[:, :], n=DLAT, sem="stn")
        W("vector", "stn", tl)
        vq = ev("vector", lambda e: e.tensor_copy(latb[:, :], pss[2].ap()[:, 0:DLAT]), sem="sv")
        ps_free[2] = ("sv", vq)
        W("scalar", "sv", vq)
        for k in range(2):
            tv = dma("scalar", "slq2", latT[:, k * 128:(k + 1) * 128],
                     latb[:, k * 128:(k + 1) * 128], transpose=True)
        W("tensor", "slq2", tv)
        kvslot = (2, 3, 4, 5)
        for ch in range(4):
            tk2 = mm_acc(kvslot[ch], [(latT[:, k * 128:(k + 1) * 128],
                             wup_b[:, k * 2 * D + ch * 512: k * 2 * D + ch * 512 + 512]) for k in range(2)],
                   bias_rhs=bup_b.ap()[:, ch * 512: ch * 512 + 512], sem="stn")
        kv_pend[0] = (t, tk2)
        q_pend[0] = (t, tq)

    def emit_qkv_evac():
        if kv_pend[0] is None:
            return
        tn, tk2 = kv_pend[0]
        _, tq = q_pend[0]
        qb = qbP[tn % 2]
        kvb = kvbP[tn % 2]
        kvslot = (2, 3, 4, 5)
        W("vector", "stn", tq)
        ev("vector", lambda e, qb=qb: e.tensor_copy(qb[:, 0:512], pss[0].ap()[:, 0:512]))
        ev("vector", lambda e, qb=qb: e.tensor_copy(qb[:, 512:1024], pss[1].ap()[:, 0:512]))
        W("vector", "stn", tk2)
        for ch in range(4):
            ev("vector", lambda e, ch=ch, kvb=kvb, kvslot=kvslot: e.tensor_copy(
                kvb[:, ch * 512:(ch + 1) * 512], pss[kvslot[ch]].ap()[:, 0:512]))
        vkv = ev("vector", lambda e: e.drain(), sem="sv")
        ps_free[0] = ("sv", vkv); ps_free[1] = ("sv", vkv)
        for ch in range(4):
            ps_free[kvslot[ch]] = ("sv", vkv)
        kv_pend[0] = None

    def emit_Y(t):
        cur = xs[t % 3]
        qb = qbP[t % 2]
        kvb = kvbP[t % 2]
        W("vector", "sst")
        q3 = qb.rearrange("p (h d) -> p h d", d=DH)
        p3 = ptmp.ap().rearrange("p (h d) -> p h d", d=DH)
        a3 = pacc.ap().rearrange("p (h d) -> p h d", d=DH)
        sc3 = scf.rearrange("p (h g) -> p h g", g=H)
        for g in range(H):
            ksl = kvb[:, g * 128: g * 128 + DH].unsqueeze(1).to_broadcast([128, H, DH])
            ev("vector", lambda e, ksl=ksl, q3=q3: e.tensor_tensor(p3, q3, ksl, op=ALU.mult))
            drain("vector")
            ev("vector", lambda e, g=g: e.tensor_reduce(sc3[:, :, g:g + 1], p3, axis=AX, op=ALU.add))
            drain("vector")
        ev("vector", lambda e: e.reduce_max(stats[:, 8:9], scf, axis=AX))
        vmx = ev("vector", lambda e: e.drain(), sem="sv")
        W("scalar", "sv", vmx)
        ev("scalar", lambda e: e.mul(stats[:, 9:10], stats[:, 8:9], -0.125))
        drain("scalar")
        sexp = ev("scalar", lambda e: e.activation(Eb, scf, AF.Exp, bias=stats[:, 9:10], scale=0.125), sem="ss")
        W("vector", "ss", sexp)
        ev("vector", lambda e: e.tensor_reduce(stats2[:, 0:H], Eb.rearrange("p (h g) -> p h g", g=H), axis=AX, op=ALU.add))
        drain("vector")
        ev("vector", lambda e: e.reciprocal(stats2[:, H:2 * H], stats2[:, 0:H]))
        drain("vector")
        E3 = Eb.rearrange("p (h g) -> p h g", g=H)
        for g in range(H):
            esl = E3[:, :, g:g + 1].to_broadcast([128, H, DH])
            vsl = kvb[:, g * 128 + DH: g * 128 + 128].unsqueeze(1).to_broadcast([128, H, DH])
            if g == 0:
                ev("vector", lambda e, esl=esl, vsl=vsl: e.tensor_tensor(a3, esl, vsl, op=ALU.mult))
            else:
                ev("vector", lambda e, esl=esl, vsl=vsl: e.tensor_tensor(p3, esl, vsl, op=ALU.mult))
                drain("vector")
                ev("vector", lambda e: e.tensor_add(pacc.ap(), pacc.ap(), ptmp.ap()))
            drain("vector")
        ev("vector", lambda e: e.tensor_tensor(
            ob.rearrange("p (h d) -> p h d", d=DH), a3,
            stats2[:, H:2 * H].unsqueeze(2).to_broadcast([128, H, DH]), op=ALU.mult))
        vo = ev("vector", lambda e: e.drain(), sem="sv")
        emit_qkv_evac()
        W("scalar", "sv", vo)
        for k in range(8):
            tv = dma("scalar", "slq", obT[:, k * 128:(k + 1) * 128],
                     ob[:, k * 128:(k + 1) * 128], transpose=True)
        W("tensor", "slq", tv)
        for hh in range(2):
            tao = mm_acc(hh, [(obT[:, k * 128:(k + 1) * 128],
                         wo_b[:, k * D + hh * 512: k * D + hh * 512 + 512]) for k in range(8)],
                   bias_rhs=bo_b.ap()[:, hh * 512: hh * 512 + 512], sem="stn")
        W("vector", "stn", tao)
        ev("vector", lambda e: e.tensor_copy(aof[:, 0:512], pss[0].ap()[:, 0:512]))
        vao = ev("vector", lambda e: e.tensor_copy(aof[:, 512:1024], pss[1].ap()[:, 0:512]), sem="sv")
        ps_free[0] = ("sv", vao); ps_free[1] = ("sv", vao)
        ln_norm(aof, ao2.ap(), ptmp.ap())
        ev("vector", lambda e, cur=cur: e.tensor_add(xpf.ap(), cur, ao2.ap()))
        vxp = ev("vector", lambda e: e.drain(), sem="sv")
        W("sync", "sv", vxp)
        dma("sync", "sst", xp_dram[t * 128:(t + 1) * 128, :], xpf.ap())
        if d_xp is not None:
            dma("sync", "sst", d_xp[t * 128:(t + 1) * 128, :], xpf.ap())
        ln_norm(xpf.ap(), yhb, ptmp.ap())
        vyh = ev("vector", lambda e: e.engine_nop(), sem="sv")
        W("sync", "sv", vyh)
        dma("sync", "sst", ag1y_i[t * 128:(t + 1) * 128, :], yhb)
        if d_yh is not None:
            dma("sync", "sst", d_yh[t * 128:(t + 1) * 128, :], yhb)
        W("scalar", "sv", vyh)
        for k in range(8):
            tv = dma("scalar", "slq", yhT[:, k * 128:(k + 1) * 128],
                     yhb[:, k * 128:(k + 1) * 128], transpose=True)
        W("sync", "slq", tv)
        for k in range(8):
            dma("sync", "sst", yTd[k * 128:(k + 1) * 128, t * 128:(t + 1) * 128],
                yhT[:, k * 128:(k + 1) * 128])
        W("tensor", "slq", tv)
        tlg = mm_acc(2, [(yhT[:, k * 128:(k + 1) * 128],
                    wg8_b.ap()[:, k * E:(k + 1) * E]) for k in range(8)], n=E, sem="stn")
        W("vector", "stn", tlg)
        vlg = ev("vector", lambda e: e.tensor_copy(lgf.ap()[:, 0:E], pss[2].ap()[:, 0:E]), sem="sv")
        ps_free[2] = ("sv", vlg)
        drain("vector")
        ev("vector", lambda e: e.reduce_max(stats[:, 10:11], lgf.ap()[:, 0:E], axis=AX, negate=True))
        vng = ev("vector", lambda e: e.drain(), sem="sv")
        W("scalar", "sv", vng)
        sle = ev("scalar", lambda e: e.activation(lgf.ap()[:, 16:16 + E], lgf.ap()[:, 0:E], AF.Exp, bias=stats[:, 10:11]), sem="ss")
        W("vector", "ss", sle)
        ev("vector", lambda e: e.reduce_sum(stats[:, 11:12], lgf.ap()[:, 16:16 + E], axis=AX))
        drain("vector")
        ev("vector", lambda e: e.reciprocal(stats[:, 12:13], stats[:, 11:12]))
        drain("vector")
        ev("vector", lambda e: e.tensor_scalar(lgf.ap()[:, 0:E], lgf.ap()[:, 16:16 + E], stats[:, 12:13], None, op0=ALU.mult))
        vsc = ev("vector", lambda e: e.drain(), sem="sv")
        W("sync", "sv", vsc)
        dma("sync", "sst", scd[t * 128:(t + 1) * 128, :], lgf.ap()[:, 0:E])
        if t + 3 < TILES:
            xld[(t + 3) % 3] = dma("sync", SXS[(t + 3) % 3], xs[(t + 3) % 3][:, :],
                                   x_in[(t + 3) * 128:(t + 4) * 128, :])

    if "p0" in phases:
        emit_X(0)
        emit_qkv_evac()
        for t in range(TILES):
            if t + 1 < TILES:
                emit_X(t + 1)
            emit_Y(t)

    if "p1" in phases:
        W("gpsimd", "sst")
        ev("gpsimd", lambda e: e.collective_compute(
            "AllGather", ALU.bypass, replica_groups=[list(range(NC_N))],
            ins=[scd.opt()], outs=[sc_all.opt()]), sem="scc")
        ev("gpsimd", lambda e: e.collective_compute(
            "AllGather", ALU.bypass, replica_groups=[list(range(NC_N))],
            ins=[ag1y_i.opt()], outs=[y_all.opt()]), sem="scc")
        ag_sc, ag_yt = 1, 2

        last_veo = [None]

        def emit_glu_pass(ngroups, scale_cols, accum, yt_src, out_dram, wait_scc):
            prev_mm = [None, None]
            gtv = [None, None]
            for g in range(ngroups):
                pg = g % 2
                tok0 = g * GRP
                sytk = "syt0" if pg == 0 else "syt1"
                if prev_mm[pg] is not None:
                    W("scalar", "stn", prev_mm[pg])
                rnk = tok0 // RNK
                toff = tok0 % RNK
                if wait_scc and g < 2:
                    W("scalar", "scc", ag_yt)
                vy = dma("scalar", sytk,
                         yt_g[pg].rearrange("p (c t) -> p c t", c=8),
                         yt_src[rnk * D:(rnk + 1) * D, toff:toff + GRP].rearrange("(c p) t -> p c t", p=128))
                W("tensor", sytk, vy)
                for fc in range(16):
                    sA, sB = fc % 2, 2 + fc % 2
                    free_ps(sA); free_ps(sB)
                    for dc in range(8):
                        ev("tensor", (lambda sA=sA, dc=dc, fc=fc, pg=pg: lambda e: e.matmul(
                            pss[sA].ap()[:, 0:GRP],
                            wgt_b[:, dc * FF + fc * 128: dc * FF + (fc + 1) * 128],
                            yt_g[pg][:, dc * GRP:(dc + 1) * GRP],
                            start=(dc == 0), stop=(dc == 7)))())
                    tga = None
                    for dc in range(8):
                        fn = (lambda sB=sB, dc=dc, fc=fc, pg=pg: lambda e: e.matmul(
                            pss[sB].ap()[:, 0:GRP],
                            wut_b[:, dc * FF + fc * 128: dc * FF + (fc + 1) * 128],
                            yt_g[pg][:, dc * GRP:(dc + 1) * GRP],
                            start=(dc == 0), stop=(dc == 7)))()
                        if dc == 7:
                            tga = ev("tensor", fn, sem="stn")
                        else:
                            ev("tensor", fn)
                    if fc == 15:
                        prev_mm[pg] = tga
                    W("scalar", "stn", tga)
                    if gtv[fc % 2] is not None:
                        W("scalar", "sv", gtv[fc % 2])
                    sg1 = ev("scalar", (lambda fc=fc, sA=sA: lambda e: e.activation(
                        gt_b[fc % 2].ap(), pss[sA].ap()[:, 0:GRP], AF.Gelu))(), sem="ss")
                    W("vector", "ss", sg1)
                    vh = ev("vector", (lambda fc=fc, sB=sB: lambda e: e.tensor_tensor(
                        hb_g[:, fc * GRP:(fc + 1) * GRP], gt_b[fc % 2].ap(), pss[sB].ap()[:, 0:GRP], op=ALU.mult))(), sem="sv")
                    gtv[fc % 2] = vh
                    ps_free[sA] = ("sv", C["sv"])
                    ps_free[sB] = ("sv", C["sv"])
                W("tensor", "sv", C["sv"])
                assert not accum
                first_evac = True
                for ch in range(4):
                    marks = {}
                    for half in range(2):
                        slot = (ch * 2 + half) % 6
                        free_ps(slot)
                        for fc in range(16):
                            fn = (lambda slot=slot, fc=fc, ch=ch, half=half: lambda e: e.matmul(
                                pss[slot].ap()[:, 0:512],
                                hb_g[:, fc * GRP + ch * 128: fc * GRP + (ch + 1) * 128],
                                wdt_b[:, fc * D + half * 512: fc * D + (half + 1) * 512],
                                start=(fc == 0), stop=(fc == 15)))()
                            if fc == 15:
                                marks[half] = ev("tensor", fn, sem="stn")
                            else:
                                ev("tensor", fn)
                    W("vector", "stn", marks[1])
                    if first_evac:
                        W("vector", "sac")
                        first_evac = False
                    for half in range(2):
                        slot = (ch * 2 + half) % 6
                        dst = eo_b.ap()[:, ch * D + half * 512: ch * D + (half + 1) * 512]
                        if scale_cols is not None:
                            scol = scale_cols(g, ch)
                            if accum:
                                ev("vector", (lambda dst=dst, slot=slot, scol=scol, ch=ch, half=half: lambda e: e.scalar_tensor_tensor(
                                    dst, pss[slot].ap()[:, 0:512], scol,
                                    acb.ap()[:, ch * D + half * 512: ch * D + (half + 1) * 512],
                                    op0=ALU.mult, op1=ALU.add))())
                            else:
                                ev("vector", (lambda dst=dst, slot=slot, scol=scol: lambda e: e.tensor_scalar(
                                    dst, pss[slot].ap()[:, 0:512], scol, None, op0=ALU.mult))())
                        else:
                            ev("vector", (lambda dst=dst, slot=slot: lambda e: e.tensor_copy(
                                dst, pss[slot].ap()[:, 0:512]))())
                        drain("vector")
                    vch = ev("vector", lambda e: e.drain(), sem="sv")
                    ps_free[(ch * 2) % 6] = ("sv", vch)
                    ps_free[(ch * 2 + 1) % 6] = ("sv", vch)
                veo = vch
                last_veo[0] = veo
                W("sync", "sv", veo)
                dma("sync", "sac", out_dram[tok0:tok0 + GRP, :].rearrange("(c p) d -> p c d", p=128),
                    eo_b.ap().rearrange("p (c d) -> p c d", d=D))

        W("vector", "stn")
        W("vector", "slq")
        W("vector", "sst")
        W("vector", "sx0"); W("vector", "sx1"); W("vector", "sx2")
        W("scalar", "stn"); W("scalar", "sst"); W("scalar", "slq"); W("scalar", "slq2")
        W("scalar", "sx0"); W("scalar", "sx1"); W("scalar", "sx2")
        load_cast(WsgT, wgt_b)
        load_cast(WsuT, wut_b)
        load_cast(WsdT, wdt_b)
        wsh = ev("scalar", lambda e: e.copy(stats[0:1, 14:15], stg_f32[0:1, 0:1]), sem="ss")
        W("tensor", "ss", wsh)
        W("sync", "sst")
        W("scalar", "sst")
        emit_glu_pass(NGS, None, False, yTd, acc_sh, wait_scc=False)

        W("sync", "scc", ag_sc)
        vsl = None
        for bb in range(B):
            vsl = dma("sync", "ssc", sc_b[bb].rearrange("p (c e) -> p c e", e=E),
                      sc_all[bb * SB:(bb + 1) * SB, :].rearrange("(c p) e -> p c e", p=128))
        W("vector", "ssc", vsl)
        W("vector", "sw"); W("vector", "sw1"); W("vector", "ss")
        for j in range(4):
            bb = j // 2
            sc3v = sc_b[bb].rearrange("p (c e) -> p c e", e=E)
            ev("vector", (lambda j=j, sc3v=sc3v: lambda e: e.tensor_tensor(
                stg_f32.rearrange("p (c e) -> p c e", e=E)[:, 0:CHB, :], sc3v,
                eh_s.ap()[:, j * E:(j + 1) * E].unsqueeze(1).to_broadcast([128, CHB, E]), op=ALU.mult))())
            drain("vector")
            ev("vector", (lambda j=j: lambda e: e.tensor_reduce(
                scj.ap()[:, j * CHB:(j + 1) * CHB],
                stg_f32.rearrange("p (c e) -> p c e", e=E)[:, 0:CHB, :], axis=AX, op=ALU.add))())
            drain("vector")
        ev("vector", lambda e: e.drain(), sem="sv")

        tau_sl = [0, 16]
        tcm = None
        for it in range(NBIS):
            cur, nxt = tau_sl
            lo_c = tt.ap()[0:1, cur:cur + 4]
            hi_c = tt.ap()[0:1, cur + 4:cur + 8]
            lo_n = tt.ap()[0:1, nxt:nxt + 4]
            hi_n = tt.ap()[0:1, nxt + 4:nxt + 8]
            mid = tt.ap()[0:1, 8:12]
            cnt_f = tt.ap()[0:1, 12:16]
            ev("vector", (lambda mid=mid, lo_c=lo_c, hi_c=hi_c: lambda e: e.tensor_tensor(
                mid, lo_c, hi_c, op=ALU.add))())
            drain("vector")
            ev("vector", (lambda mid=mid: lambda e: e.tensor_scalar(
                mid, mid, 0.5, None, op0=ALU.mult))())
            drain("vector")
            ev("vector", (lambda mid=mid: lambda e: e.tensor_copy(
                bc1.ap().rearrange("o (p j) -> o p j", j=4),
                mid.unsqueeze(1).to_broadcast([1, 128, 4])))())
            vb = ev("vector", lambda e: e.drain(), sem="sv")
            W("sync", "sv", vb)
            vst = dma("sync", "stx", taux_d, bc1.ap()[0:1, :])
            W("sync", "stx", vst)
            vld = dma("sync", "stx", cand128.ap(), taux_d.rearrange("(p j) -> p j", p=128))
            W("vector", "stx", vld)
            if tcm is not None:
                W("vector", "stn", tcm)
            for j in range(4):
                ev("vector", (lambda j=j: lambda e: e.tensor_single_scalar(
                    mk_b.ap()[:, j * CHB:(j + 1) * CHB], scj.ap()[:, j * CHB:(j + 1) * CHB],
                    cand128.ap()[:, j:j + 1], op=ALU.is_ge))())
            vmk = ev("vector", lambda e: e.drain(), sem="sv")
            W("tensor", "sv", vmk)
            if pcc_free[0] is not None:
                W("tensor", pcc_free[0][0], pcc_free[0][1])
                pcc_free[0] = None
            tcm = ev("tensor", lambda e: e.matmul(pcc.ap()[0:1, 0:4 * CHB], ones_col.ap(),
                                                  mk_b.ap(), start=True, stop=True), sem="stn")
            W("vector", "stn", tcm)
            ev("vector", (lambda cnt_f=cnt_f: lambda e: e.tensor_reduce(
                cnt_f.rearrange("o (j x) -> o j x", x=1),
                pcc.ap()[0:1, 0:4 * CHB].rearrange("o (j c) -> o j c", c=CHB), axis=AX, op=ALU.add))())
            vrd = ev("vector", lambda e: e.drain(), sem="sv")
            pcc_free[0] = ("sv", C["sv"])
            ev("vector", (lambda cnt_f=cnt_f: lambda e: e.tensor_scalar(
                stg_f32[0:1, 0:4], cnt_f, float(CAP), None, op0=ALU.is_ge))())
            drain("vector")
            ev("vector", lambda e: e.tensor_copy(mk16.ap(), stg_f32[0:1, 0:4]))
            drain("vector")
            ev("vector", (lambda lo_n=lo_n, mid=mid, lo_c=lo_c: lambda e: e.select(
                lo_n, mk16.ap(), mid, lo_c, add_drain=True))())
            drain("vector")
            ev("vector", (lambda hi_n=hi_n, hi_c=hi_c, mid=mid: lambda e: e.select(
                hi_n, mk16.ap(), hi_c, mid, add_drain=True))())
            drain("vector")
            tau_sl = [nxt, cur]
        fin = tau_sl[0]
        ev("vector", (lambda fin=fin: lambda e: e.tensor_copy(
            bc1.ap().rearrange("o (p j) -> o p j", j=4),
            tt.ap()[0:1, fin:fin + 4].unsqueeze(1).to_broadcast([1, 128, 4])))())
        vb = ev("vector", lambda e: e.drain(), sem="sv")
        W("sync", "sv", vb)
        vst = dma("sync", "stx", taux_d, bc1.ap()[0:1, :])
        W("sync", "stx", vst)
        vld = dma("sync", "stx", tau128.ap(), taux_d.rearrange("(p j) -> p j", p=128))
        W("vector", "stx", vld)
        if d_tau is not None:
            vtt = ev("vector", lambda e: e.engine_nop(), sem="sv")
            W("sync", "sv", vtt)
            dma("sync", "sst", d_tau[:, :], tt.ap())
        for j in range(4):
            ev("vector", (lambda j=j: lambda e: e.tensor_single_scalar(
                wsc.ap()[:, j * CHB:(j + 1) * CHB], scj.ap()[:, j * CHB:(j + 1) * CHB],
                tau128.ap()[:, j:j + 1], op=ALU.is_ge))())
            drain("vector")
            ev("vector", (lambda j=j: lambda e: e.tensor_tensor(
                wsc.ap()[:, j * CHB:(j + 1) * CHB], wsc.ap()[:, j * CHB:(j + 1) * CHB],
                scj.ap()[:, j * CHB:(j + 1) * CHB], op=ALU.mult))())
            drain("vector")
        for j in range(4):
            ev("vector", (lambda j=j: lambda e: e.tensor_single_scalar(
                mk_b.ap()[:, j * CHB:(j + 1) * CHB], scj.ap()[:, j * CHB:(j + 1) * CHB],
                tau128.ap()[:, j:j + 1], op=ALU.is_ge))())
            drain("vector")
        vwsc = ev("vector", lambda e: e.drain(), sem="sv")
        if d_mkb is not None:
            W("sync", "sv", vwsc)
            dma("sync", "sst", d_mkb[:, :], mk_b.ap())
        if d_wsc is not None:
            W("sync", "sv", vwsc)
            dma("sync", "sst", d_wsc[:, :], wsc.ap())
        if d_scj is not None:
            W("sync", "sv", vwsc)
            dma("sync", "sst", d_scj[:, :], scj.ap())
        W("sync", "sv", vwsc)

        stgbf = stg.ap().bitcast(BF16)

        NCG = SB // 1024
        SLT = NCG * 256
        MGN = SLT // 512

        def emit_routed(le):
            for bb in range(B):
                j = 2 * bb + le
                mkj = mk_b.ap()[:, j * CHB:(j + 1) * CHB]
                free_ps(4)
                if pcc_free[0] is not None:
                    W("tensor", pcc_free[0][0], pcc_free[0][1])
                    pcc_free[0] = None
                W("tensor", "sv", C["sv"])
                t1 = ev("tensor", (lambda mkj=mkj: lambda e: e.matmul(pss[4].ap()[:, 0:CHB], ltb.ap(),
                                                     mkj, start=True, stop=True))(), sem="stn")
                t2 = ev("tensor", (lambda mkj=mkj: lambda e: e.matmul(pcc.ap()[0:1, 0:CHB], ones_col.ap(),
                                                     mkj, start=True, stop=True))(), sem="stn")
                W("vector", "stn", t2)
                W("vector", "ss")
                if d_poss is not None:
                    W("vector", "sst")
                ev("vector", (lambda mkj=mkj: lambda e: e.tensor_sub(posc.ap()[:, 0:CHB], pss[4].ap()[:, 0:CHB], mkj))())
                drain("vector")
                ra = stg_f32[0:1, 0:CHB]
                rb = stg_f32[0:1, 64:64 + CHB]
                rc = stg_f32[0:1, 128:128 + CHB]
                ev("vector", lambda e: e.tensor_copy(ra, pcc.ap()[0:1, 0:CHB]))
                drain("vector")
                ev("vector", lambda e: e.memset(rb, 0.0))
                drain("vector")
                ev("vector", lambda e: e.tensor_copy(rb[:, 1:CHB], ra[:, 0:CHB - 1]))
                drain("vector")
                cur, oth = rb, ra
                for sh in (1, 2, 4, 8, 16, 32):
                    if sh >= CHB:
                        break
                    ev("vector", (lambda oth=oth, cur=cur, sh=sh: lambda e: e.tensor_copy(oth[:, 0:sh], cur[:, 0:sh]))())
                    drain("vector")
                    ev("vector", (lambda oth=oth, cur=cur, sh=sh: lambda e: e.tensor_add(
                        oth[:, sh:CHB], cur[:, sh:CHB], cur[:, 0:CHB - sh]))())
                    drain("vector")
                    cur, oth = oth, cur
                gv = cur.rearrange("o (g k) -> o g k", k=8)
                gs = gv[:, :, 0:1].to_broadcast([1, CHB // 8, 8])
                ev("vector", (lambda cur=cur, gv=gv, gs=gs: lambda e: e.tensor_sub(
                    rc.rearrange("o (g k) -> o g k", k=8), gv, gs))())
                vps = ev("vector", lambda e: e.drain(), sem="sv")
                ps_free[4] = ("sv", vps)
                pcc_free[0] = ("sv", vps)
                W("sync", "sv", vps)
                vst = dma("sync", "stx", pcb_d.rearrange("(p c) -> p c", c=CHB).unsqueeze(0),
                          rc.unsqueeze(1).to_broadcast([1, 128, CHB]))
                W("sync", "stx", vst)
                vld = dma("sync", "stx", cb128.ap()[:, 0:CHB], pcb_d.rearrange("(p c) -> p c", c=CHB))
                W("vector", "stx", vld)
                ev("vector", lambda e: e.tensor_add(posc.ap()[:, 0:CHB], posc.ap()[:, 0:CHB], cb128.ap()[:, 0:CHB]))
                drain("vector")
                ev("vector", lambda e: e.tensor_scalar(cb128.ap()[:, 0:CHB], posc.ap()[:, 0:CHB], 256.0, 2048.0, op0=ALU.is_ge, op1=ALU.mult))
                drain("vector")
                ev("vector", lambda e: e.tensor_add(posc.ap()[:, 0:CHB], posc.ap()[:, 0:CHB], cb128.ap()[:, 0:CHB]))
                drain("vector")
                ev("vector", (lambda mkj=mkj: lambda e: e.tensor_scalar(cb128.ap()[:, 0:CHB], mkj, -4096.0, 4096.0, op0=ALU.mult, op1=ALU.add))())
                drain("vector")
                ev("vector", lambda e: e.tensor_add(poss.ap()[:, 0:CHB], posc.ap()[:, 0:CHB], cb128.ap()[:, 0:CHB]))
                vposs = ev("vector", lambda e: e.drain(), sem="sv")
                if d_poss is not None:
                    W("sync", "sv", vposs)
                    dma("sync", "sst", d_poss[j * 128:(j + 1) * 128, 0:CHB], poss.ap()[:, 0:CHB])
                    dma("sync", "sst", d_poss[j * 128:(j + 1) * 128, 64:64 + CHB], posc.ap()[:, 0:CHB])
                    dma("sync", "sst", d_poss[j * 128:(j + 1) * 128, 128:128 + CHB], cb128.ap()[:, 0:CHB])

                W("sync", "sv", vposs)
                W("sync", "ss")
                W("sync", "scc", ag_yt)
                W("sync", "stn", C["stn"])
                W("scalar", "stn", C["stn"])
                W("vector", "sac")
                pstrip = stgbf
                pbrd = [None]
                gytr = [None, None]
                for cg in range(NCG):
                    for ss in range(4):
                        free_ps(ss)
                    if cg > 0 or (bb > 0 or le > 0):
                        W("sync", "stn", C["stn"])
                    vyb = None
                    for sb in range(8):
                        c8 = cg * 8 + sb
                        vyb = dma("sync", "syt0", hb2[:, sb * 1024:(sb + 1) * 1024],
                                  y_all[bb * SB + c8 * 128: bb * SB + (c8 + 1) * 128, :])
                    if pbrd[0] is not None:
                        W("vector", "stn", pbrd[0])
                    W("vector", "sw"); W("vector", "sw1"); W("vector", "ss")
                    for sb in range(8):
                        for ob in range(2):
                            ev("vector", (lambda sb=sb, ob=ob, c=cg * 8 + sb: lambda e: e.tensor_single_scalar(
                                pstrip[:, (sb * 2 + ob) * 128:(sb * 2 + ob + 1) * 128],
                                iotar_s.ap()[:, ob * 128:(ob + 1) * 128],
                                poss.ap()[:, c:c + 1], op=ALU.is_equal))())
                    vpb = ev("vector", lambda e: e.drain(), sem="sv")
                    W("tensor", "sv", vpb)
                    W("tensor", "syt0", vyb)
                    for sb in range(8):
                        for ob in range(2):
                            for half in range(2):
                                fn = (lambda ob=ob, half=half, sb=sb: lambda e: e.matmul(
                                    pss[ob * 2 + half].ap()[:, 0:512],
                                    pstrip[:, (sb * 2 + ob) * 128:(sb * 2 + ob + 1) * 128],
                                    hb2[:, sb * 1024 + half * 512: sb * 1024 + (half + 1) * 512],
                                    start=(sb == 0), stop=(sb == 7)))()
                                if sb == 7 and ob == 1 and half == 1:
                                    pbrd[0] = ev("tensor", fn, sem="stn")
                                else:
                                    ev("tensor", fn)
                    for ob in range(2):
                        po = cg * 2 + ob
                        W("vector", "stn", pbrd[0])
                        if gytr[ob] is not None:
                            W("vector", "slq", C["slq"])
                        ev("vector", (lambda ob=ob: lambda e: e.tensor_copy(
                            eo_b.ap()[:, ob * 1024: ob * 1024 + 512], pss[ob * 2].ap()[:, 0:512]))())
                        drain("vector")
                        vg = ev("vector", (lambda ob=ob: lambda e: e.tensor_copy(
                            eo_b.ap()[:, ob * 1024 + 512: (ob + 1) * 1024], pss[ob * 2 + 1].ap()[:, 0:512]))(), sem="sv")
                        ps_free[ob * 2] = ("sv", vg)
                        ps_free[ob * 2 + 1] = ("sv", vg)
                        W("scalar", "sv", vg)
                        for dc in range(8):
                            tvx = dma("scalar", "slq", xTs[:, dc * SLT + po * 128: dc * SLT + (po + 1) * 128],
                                      eo_b.ap()[:, ob * 1024 + dc * 128: ob * 1024 + (dc + 1) * 128], transpose=True)
                        gytr[ob] = tvx

                W("tensor", "slq", C["slq"])
                gtv2 = [None, None]
                pb2rd = [None]
                pb2rd_d = {}
                ptt_ev = {}
                last_sc_ev = [None]

                def emit_A_fc(mg, fc):
                    sA, sB2 = fc % 2, 2 + fc % 2
                    free_ps(sA); free_ps(sB2)
                    for dc in range(8):
                        ev("tensor", (lambda sA=sA, dc=dc, fc=fc, mg=mg: lambda e: e.matmul(
                            pss[sA].ap()[:, 0:GRP],
                            wgt_b[:, dc * FF + fc * 128: dc * FF + (fc + 1) * 128],
                            xTs[:, dc * SLT + mg * 512: dc * SLT + (mg + 1) * 512],
                            start=(dc == 0), stop=(dc == 7)))())
                    tga = None
                    for dc in range(8):
                        fn = (lambda sB2=sB2, dc=dc, fc=fc, mg=mg: lambda e: e.matmul(
                            pss[sB2].ap()[:, 0:GRP],
                            wut_b[:, dc * FF + fc * 128: dc * FF + (fc + 1) * 128],
                            xTs[:, dc * SLT + mg * 512: dc * SLT + (mg + 1) * 512],
                            start=(dc == 0), stop=(dc == 7)))()
                        if dc == 7:
                            tga = ev("tensor", fn, sem="stn")
                        else:
                            ev("tensor", fn)
                    W("scalar", "stn", tga)
                    if gtv2[fc % 2] is not None:
                        W("scalar", "sv", gtv2[fc % 2])
                    sg1 = ev("scalar", (lambda fc=fc, sA=sA: lambda e: e.activation(
                        gt_b[fc % 2].ap(), pss[sA].ap()[:, 0:GRP], AF.Gelu))(), sem="ss")
                    W("vector", "ss", sg1)
                    vh = ev("vector", (lambda fc=fc, sB2=sB2: lambda e: e.tensor_tensor(
                        hb2[:, fc * GRP:(fc + 1) * GRP], gt_b[fc % 2].ap(), pss[sB2].ap()[:, 0:GRP], op=ALU.mult))(), sem="sv")
                    gtv2[fc % 2] = vh
                    ps_free[sA] = ("sv", C["sv"])
                    ps_free[sB2] = ("sv", C["sv"])

                def emit_B(mg):
                    W("tensor", "sv", C["sv"])
                    for ch in range(4):
                        marks = {}
                        for half in range(2):
                            slot = (ch * 2 + half) % 6
                            free_ps(slot)
                            for fc in range(16):
                                fn = (lambda slot=slot, fc=fc, ch=ch, half=half: lambda e: e.matmul(
                                    pss[slot].ap()[:, 0:512],
                                    hb2[:, fc * GRP + ch * 128: fc * GRP + (ch + 1) * 128],
                                    wdt_b[:, fc * D + half * 512: fc * D + (half + 1) * 512],
                                    start=(fc == 0), stop=(fc == 15)))()
                                if fc == 15:
                                    marks[half] = ev("tensor", fn, sem="stn")
                                else:
                                    ev("tensor", fn)
                        W("vector", "stn", marks[1])
                        for half in range(2):
                            slot = (ch * 2 + half) % 6
                            ev("vector", (lambda ch=ch, half=half, slot=slot: lambda e: e.tensor_copy(
                                eoI[:, ch * 1024 + half * 512: ch * 1024 + (half + 1) * 512],
                                pss[slot].ap()[:, 0:512]))())
                            drain("vector")
                        vch = ev("vector", lambda e: e.drain(), sem="sv")
                        ps_free[(ch * 2) % 6] = ("sv", vch)
                        ps_free[(ch * 2 + 1) % 6] = ("sv", vch)

                strip_v = [None]

                def emit_strip(cg):
                    if pb2rd[0] is not None:
                        W("vector", "stn", C["stn"])
                    W("vector", "sw"); W("vector", "sw1"); W("vector", "ss")
                    for tbl in range(8):
                        for ob in range(2):
                            ev("vector", (lambda tbl=tbl, ob=ob, c=cg * 8 + tbl: lambda e: e.tensor_single_scalar(
                                pstrip[:, (tbl * 2 + ob) * 128:(tbl * 2 + ob + 1) * 128],
                                iotar_s.ap()[:, ob * 128:(ob + 1) * 128],
                                poss.ap()[:, c:c + 1], op=ALU.is_equal))())
                    strip_v[0] = ev("vector", lambda e: e.drain(), sem="sv")

                def emit_scatter_tb(mg, cgl, tbl):
                    cg = 2 * mg + cgl
                    c = cg * 8 + tbl
                    free_ps(4); free_ps(5)
                    emark = {}
                    for ob in range(2):
                        k = tbl * 2 + ob
                        p2 = k % 4
                        if k == 0:
                            W("tensor", "sv", strip_v[0])
                        if ptt_ev.get(0) is not None:
                            W("tensor", "sv", ptt_ev[0])
                        tmk = ev("tensor", (lambda k=k: lambda e: e.transpose(
                            ptt.ap(), pstrip[:, k * 128:(k + 1) * 128], idm_b.ap()))(), sem="stn")
                        W("vector", "stn", tmk)
                        if pb2rd_d.get(p2) is not None:
                            W("vector", "stn", pb2rd_d[p2])
                        emark[k] = ev("vector", (lambda p2=p2: lambda e: e.tensor_copy(
                            Pb2[p2].ap(), ptt.ap()))(), sem="sv")
                        ptt_ev[0] = emark[k]
                    for ob in range(2):
                        k = tbl * 2 + ob
                        p2 = k % 4
                        W("tensor", "sv", emark[k])
                        for half in range(2):
                            fn = (lambda p2=p2, half=half, cgl=cgl, ob=ob: lambda e: e.matmul(
                                pss[4 + half].ap()[:, 0:512],
                                Pb2[p2].ap(),
                                eoI[:, (cgl * 2 + ob) * 1024 + half * 512: (cgl * 2 + ob) * 1024 + (half + 1) * 512],
                                start=(ob == 0), stop=(ob == 1)))()
                            if half == 1:
                                smark = ev("tensor", fn, sem="stn")
                            else:
                                ev("tensor", fn)
                        pb2rd_d[p2] = smark
                    pb2rd[0] = smark
                    if le == 1:
                        if last_sc_ev[0] is not None:
                            W("sync", "sv", last_sc_ev[0])
                        va = dma("sync", "sar", acb.ap()[:, 0:D],
                                 acc_d[bb * SB + c * 128: bb * SB + (c + 1) * 128, :])
                    W("vector", "stn", smark)
                    W("vector", "sac")
                    if le == 1:
                        W("vector", "sar", va)
                    scol = wsc.ap()[:, j * CHB + c: j * CHB + c + 1]
                    for half in range(2):
                        dst = eo_b.ap()[:, 2048 + half * 512: 2048 + (half + 1) * 512]
                        if le == 1:
                            ev("vector", (lambda dst=dst, half=half, scol=scol: lambda e: e.scalar_tensor_tensor(
                                dst, pss[4 + half].ap()[:, 0:512], scol,
                                acb.ap()[:, half * 512:(half + 1) * 512],
                                op0=ALU.mult, op1=ALU.add))())
                        else:
                            ev("vector", (lambda dst=dst, half=half, scol=scol: lambda e: e.tensor_scalar(
                                dst, pss[4 + half].ap()[:, 0:512], scol, None, op0=ALU.mult))())
                        drain("vector")
                    veo = ev("vector", lambda e: e.drain(), sem="sv")
                    last_sc_ev[0] = veo
                    ps_free[4] = ("sv", veo)
                    ps_free[5] = ("sv", veo)
                    W("sync", "sv", veo)
                    dma("sync", "sac", acc_d[bb * SB + c * 128: bb * SB + (c + 1) * 128, :],
                        eo_b.ap()[:, 2048:3072])

                for fc in range(16):
                    emit_A_fc(0, fc)
                for mg in range(MGN):
                    emit_B(mg)
                    nxt = mg + 1 if mg + 1 < MGN else None
                    i = 0
                    for cgl in range(2):
                        emit_strip(2 * mg + cgl)
                        for tbl in range(8):
                            emit_scatter_tb(mg, cgl, tbl)
                            if nxt is not None and i < 16:
                                emit_A_fc(nxt, i)
                            i += 1

        for le in range(2):
            W("vector", "stn")
            W("scalar", "stn")
            W("sync", "stn", C["stn"])
            load_cast(WgT[le], wgt_b)
            load_cast(WuT[le], wut_b)
            load_cast(WdT[le], wdt_b)
            wexp = ev("scalar", lambda e: e.copy(stats[0:1, 14:15], stg_f32[0:1, 0:1]), sem="ss")
            W("tensor", "ss", wexp)
            emit_routed(le)

        W("gpsimd", "sac")
        ev("gpsimd", lambda e: e.collective_compute(
            "ReduceScatter", ALU.add, replica_groups=[list(range(NC_N))],
            ins=[acc_d.opt()], outs=[rs_out.opt()]), sem="scc")

    if "p2" in phases:
        W("sync", "scc", C["scc"])
        for t in range(TILES):
            base = t * 128
            W("vector", "sfin")
            v0 = dma("sync", "sar", at[0], rs_out[base:base + 128, :])
            v1 = dma("sync", "sar", at[1], acc_sh[base:base + 128, :])
            vxp2 = dma("sync", "sar", xpf.ap(), xp_dram[base:base + 128, :])
            W("vector", "sar", vxp2)
            W("vector", "sac")
            ev("vector", lambda e: e.tensor_add(ot2, at[0], at[1]))
            ln_norm(ot2, ao2.ap(), pacc.ap())
            ev("vector", lambda e: e.tensor_add(ao2.ap(), ao2.ap(), xpf.ap()))
            vfin = ev("vector", lambda e: e.drain(), sem="sv")
            W("sync", "sv", vfin)
            dma("sync", "sfin", out[base:base + 128, :], ao2.ap())
        W("sync", "sfin")
        W("gpsimd", "sfin")

    blk = es.enter_context(nc.Block())
    for eng in ("sync", "scalar", "vector", "tensor", "gpsimd"):
        fns = streams[eng]
        getattr(blk, eng)(lambda e, fns=fns: [f(e) for f in fns])
    es.close()
    return nc, dbg


def _prep_inputs(cfg, inputs):
    SB, B, TC = cfg["SB"], cfg["B"], cfg["TC"]
    L = "L0_"
    f32 = lambda a: np.ascontiguousarray(np.asarray(a, dtype=np.float32))
    x = f32(inputs[L + "attn_x"]).reshape(B * SB, D)
    Wq = f32(inputs[L + "attn_Wq"]); bq = f32(inputs[L + "attn_bq"]).reshape(1, D)
    Wdn = f32(inputs[L + "attn_Wdown"]); bdn = f32(inputs[L + "attn_bdown"]).reshape(1, DLAT)
    Wup = f32(inputs[L + "attn_Wup"]); bup = f32(inputs[L + "attn_bup"]).reshape(1, 2 * D)
    Wo = f32(inputs[L + "attn_Wout"]); bo = f32(inputs[L + "attn_bout"]).reshape(1, D)
    gateT = np.ascontiguousarray(f32(inputs[L + "ffn_gate_w"]).T)
    WsgT = np.ascontiguousarray(f32(inputs[L + "ffn_Ws_g"]).T)
    WsuT = np.ascontiguousarray(f32(inputs[L + "ffn_Ws_u"]).T)
    WsdT = np.ascontiguousarray(f32(inputs[L + "ffn_Ws_d"]).T)
    Wg = f32(inputs[L + "ffn_Wg"])
    Wu = f32(inputs[L + "ffn_Wu"])
    Wd = f32(inputs[L + "ffn_Wd"])
    LT = np.tril(np.ones((128, 128), np.float32)).T
    iotar = np.tile(np.arange(256, dtype=np.float32), (128, 1))
    idm = np.eye(128, dtype=np.float32)
    maps = []
    for c in range(NC_N):
        eh = np.zeros((128, 4 * E), np.float32)
        for j in range(4):
            eh[:, j * E + 2 * c + (j % 2)] = 1.0
        m = {
            "x": x[c * TC:(c + 1) * TC],
            "Wq": Wq, "bq": bq, "Wdn": Wdn, "bdn": bdn,
            "Wup": Wup, "bup": bup, "Wo": Wo, "bo": bo,
            "gateT": gateT, "WsgT": WsgT, "WsuT": WsuT, "WsdT": WsdT,
            "eh_in": eh, "LT_in": LT, "iotar_in": iotar, "idm_in": idm,
        }
        for le in range(2):
            eg = 2 * c + le
            m[f"Wg{le}T"] = np.ascontiguousarray(Wg[eg].T)
            m[f"Wu{le}T"] = np.ascontiguousarray(Wu[eg].T)
            m[f"Wd{le}T"] = np.ascontiguousarray(Wd[eg].T)
        maps.append(m)
    return maps


def kernel(**inputs):
    cfg = make_cfg()
    nc, _dbg = build_nc(cfg)
    in_maps = _prep_inputs(cfg, inputs)
    res = run_bass_kernel_spmd(nc, in_maps, core_ids=list(range(NC_N)))
    outs = [np.asarray(res.results[c]["out"]) for c in range(NC_N)]
    full = np.concatenate(outs, axis=0)
    return full.reshape(cfg["B"], cfg["SB"], D)


# version 13
# speedup vs baseline: 1.1208x; 1.0657x over previous
import numpy as np
import ml_dtypes
from contextlib import ExitStack

import concourse.bass as bass
import concourse.mybir as mybir
from concourse.bass_utils import run_bass_kernel_spmd

F32 = mybir.dt.float32
BF16 = mybir.dt.bfloat16
I16 = mybir.dt.int16
I32 = mybir.dt.int32
AF = mybir.ActivationFunctionType
ALU = mybir.AluOpType
AX = mybir.AxisListType.X

NC_N = 8
D = 1024
H = 16
DH = 64
DLAT = 256
E = 16
FF = 2048
LN_EPS = 1e-5
NBIS = 46


def make_cfg(SB=8192, B=2, phases=("p0", "p1", "p2")):
    c = {"SB": SB, "B": B}
    c["TC"] = SB * B // NC_N
    c["CAP"] = SB // 8
    c["TILES"] = c["TC"] // 128
    c["phases"] = phases
    return c


def build_nc(cfg, debug=()):
    SB, B, TC, CAP, TILES = (cfg[k] for k in ("SB", "B", "TC", "CAP", "TILES"))
    phases = cfg["phases"]
    TT = B * SB
    GRP = 512
    NG = TT // GRP
    NGS = TC // GRP
    CHB = SB // 128
    RNK = TC

    nc = bass.Bass("TRN2", target_bir_lowering=False, debug=False)

    def din(name, shape, dt=F32):
        return nc.dram_tensor(name, shape, dt, kind="ExternalInput").ap()

    x_in = din("x", [TC, D])
    Wq = din("Wq", [D, D]); bq = din("bq", [1, D])
    Wdn = din("Wdn", [D, DLAT]); bdn = din("bdn", [1, DLAT])
    Wup = din("Wup", [DLAT, 2 * D]); bup = din("bup", [1, 2 * D])
    Wo = din("Wo", [D, D]); bo = din("bo", [1, D])
    gateT = din("gateT", [D, E])
    WsgT = din("WsgT", [D, FF]); WsuT = din("WsuT", [D, FF]); WsdT = din("WsdT", [FF, D])
    WgT = [din(f"Wg{i}T", [D, FF]) for i in range(2)]
    WuT = [din(f"Wu{i}T", [D, FF]) for i in range(2)]
    WdT = [din(f"Wd{i}T", [FF, D]) for i in range(2)]
    eh_in = din("eh_in", [128, 4 * E])
    LT_in = din("LT_in", [128, 128])
    iotar_in = din("iotar_in", [128, 256])
    idm_in = din("idm_in", [128, 128])

    out = nc.dram_tensor("out", [TC, D], F32, kind="ExternalOutput").ap()
    dbg = {}
    def dout(name, shape, dt=F32):
        if name in debug:
            dbg[name] = nc.dram_tensor("d_" + name, shape, dt, kind="ExternalOutput").ap()
        return dbg.get(name)

    d_xp = dout("xp", [TC, D])
    d_yh = dout("yh", [TC, D], BF16)
    d_tau = dout("tau", [1, 24])
    d_wsc = dout("wsc", [128, 4 * CHB])
    d_scj = dout("scj", [128, 4 * CHB])
    d_poss = dout("poss", [4 * 128, 192])
    d_mkb = dout("mkb", [128, 4 * CHB], BF16)

    xp_dram = nc.dram_tensor("xp_dram", [TC, D], F32).ap()
    yTd = nc.dram_tensor("yTd", [D, TC], BF16).ap()
    ag1y_i = nc.dram_tensor("ag1y_i", [TC, D], BF16).ap()
    y_all = nc.dram_tensor("y_all", [TT, D], BF16).ap()
    pcb_d = nc.dram_tensor("pcb_d", [128 * (SB // 128)], F32).ap()
    scd = nc.dram_tensor("scd", [TC, E], F32).ap()
    sc_all = nc.dram_tensor("sc_all", [NC_N * TC, E], F32).ap()
    taux_d = nc.dram_tensor("taux_d", [512], F32).ap()
    acc_d = nc.dram_tensor("acc_d", [TT, D], BF16).ap()
    rs_out = nc.dram_tensor("rs_out", [TC, D], BF16).ap()
    acc_sh = nc.dram_tensor("acc_sh", [TC, D], BF16).ap()
    wsg_c = nc.dram_tensor("wsg_c", [D, FF], BF16).ap()
    wsu_c = nc.dram_tensor("wsu_c", [D, FF], BF16).ap()
    wsd_c = nc.dram_tensor("wsd_c", [FF, D], BF16).ap()
    wg_c = [nc.dram_tensor(f"wg_c{i}", [D, FF], BF16).ap() for i in range(2)]
    wu_c = [nc.dram_tensor(f"wu_c{i}", [D, FF], BF16).ap() for i in range(2)]
    wd_c = [nc.dram_tensor(f"wd_c{i}", [FF, D], BF16).ap() for i in range(2)]

    es = ExitStack()
    sb = lambda n, s, dt: es.enter_context(nc.sbuf_tensor(n, s, dt))

    BUF_A = sb("BUF_A", [128, 49152], BF16)
    BUF_B = sb("BUF_B", [128, 28672], BF16)
    A_, B_ = BUF_A.ap(), BUF_B.ap()
    wq_b = A_[:, 0:8 * D]
    wo_b = A_[:, 8 * D:16 * D]
    wup_b = A_[:, 16 * D:20 * D]
    wdn_b = A_[:, 20 * D:22 * D]
    REG = A_[:, 22528:22528 + 12 * D]
    EXT = A_[:, 34816:49152]
    xs = [REG[:, 0:2 * D].bitcast(F32), REG[:, 2 * D:4 * D].bitcast(F32),
          EXT[:, 0:2 * D].bitcast(F32)]
    aof = REG[:, 4 * D:6 * D].bitcast(F32)
    kvbP = [REG[:, 6 * D:8 * D], EXT[:, 2 * D:4 * D]]
    xhat = REG[:, 8 * D:9 * D]
    xhatT = REG[:, 9 * D:10 * D]
    qbP = [REG[:, 10 * D:11 * D], EXT[:, 4 * D:5 * D]]
    obT = EXT[:, 5 * D:6 * D]
    yhb = REG[:, 11 * D:12 * D]
    latb = B_[:, 0:DLAT]
    latT = B_[:, DLAT:2 * DLAT]
    scf = B_[:, 512:1024].bitcast(F32)
    Eb = B_[:, 1024:1280]
    ob = B_[:, 1280:2304]
    yhT = B_[:, 2304:3328]
    wgt_b = A_[:, 0:16384]
    wut_b = A_[:, 16384:32768]
    wdt_b = A_[:, 32768:49152]
    yt_g = [B_[:, 0:4096], B_[:, 4096:8192]]
    hb_g = B_[:, 8192:16384]
    xTs = B_[:, 0:16384]
    hb2 = B_[:, 16384:24576]
    eoI = B_[:, 24576:28672]
    at = [B_[:, 0:D], B_[:, D:2 * D]]
    ot2 = B_[:, 2 * D:4 * D].bitcast(F32)

    wg8_b = sb("wg8_b", [128, 8 * E], BF16)
    bq_b = sb("bq_b", [1, D], BF16)
    bdn_b = sb("bdn_b", [1, DLAT], BF16)
    bup_b = sb("bup_b", [1, 2 * D], BF16)
    bo_b = sb("bo_b", [1, D], BF16)
    ones = sb("ones", [1, 128], BF16)
    ones_col = sb("ones_col", [128, 1], BF16)
    stg = sb("stg0", [128, 1024], F32)
    stg_f32 = stg.ap()
    gt_b = [sb(f"gt_b{i}", [128, GRP], BF16) for i in range(2)]
    eo_b = sb("eo_b", [128, 4 * D], BF16)
    acb = sb("acb", [128, D], BF16)
    pacc = sb("pacc", [128, D], F32)
    ptmp = sb("ptmp", [128, D], F32)
    ao2 = sb("ao2", [128, D], F32)
    xpf = sb("xpf", [128, D], F32)
    lgf = sb("lgf", [128, 32], F32)
    stats = sb("stats", [128, 16], F32)
    stats2 = sb("stats2", [128, 2 * H], F32)
    epsb = sb("epsb", [128, 1], F32)
    sc_b = [B_[:, 16384 + i * 4096: 16384 + (i + 1) * 4096].bitcast(F32)[:, 0:CHB * E] for i in range(B)]
    scj = sb("scj", [128, 4 * CHB], F32)
    wsc = sb("wsc", [128, 4 * CHB], F32)
    mk_b = sb("mk_b", [128, 4 * CHB], BF16)
    eh_s = sb("eh_s", [128, 4 * E], F32)
    cand128 = sb("cand128", [128, 4], F32)
    iotar_s = sb("iotar_s", [128, 256], F32)
    ltb = sb("ltb", [128, 128], BF16)
    idm_b = sb("idm_b", [128, 128], BF16)
    posc = sb("posc", [128, 64], F32)
    poss = sb("poss", [128, 64], F32)
    cb128 = sb("cb128", [128, 64], F32)
    Pb4 = sb("Pb4", [128, 512], BF16)
    Pb2 = [sb(f"Pb2_{i}", [128, 128], BF16) for i in range(4)]
    tau128 = sb("tau128", [128, 4], F32)
    tt = sb("tt", [1, 24], F32)
    mk16 = sb("mk16", [1, 4], I16)
    bc1 = sb("bc1", [1, 512], F32)

    pss = [es.enter_context(nc.psum_tensor(f"ps{i}", [128, 512], F32, bank=i)) for i in range(6)]
    ptt = es.enter_context(nc.psum_tensor("ptt", [128, 128], BF16, bank=6))
    pcc = es.enter_context(nc.psum_tensor("pcc", [1, 384], F32, bank=7))

    sems = {}
    for s in ("sw", "sw1", "scg", "sx0", "sx1", "sx2", "sst", "slq", "slq2", "syt0", "syt1", "ssc", "stx",
              "sac", "sar", "sfin", "sv", "ss", "stn", "scc"):
        sems[s] = es.enter_context(nc.semaphore(s))
    C = {s: 0 for s in sems}
    ps_free = [None] * 6
    pcc_free = [None]
    ptt_free = [None]
    streams = {k: [] for k in ("sync", "scalar", "vector", "tensor", "gpsimd")}

    def W(eng, sem, val=None):
        v = C[sem] if val is None else val
        streams[eng].append(lambda e: e.wait_ge(sems[sem], v))

    def INC(sem, n=1):
        C[sem] += n
        return sems[sem], C[sem]

    def dma(q, sem, out_ap, in_ap, **kw):
        s, v = INC(sem, 16)
        streams[q].append(lambda e: e.dma_start(out=out_ap, in_=in_ap, **kw).then_inc(s, 16))
        return v

    def ev(eng, fn, sem=None):
        if sem is None:
            streams[eng].append(fn)
            return None
        s, v = INC(sem, 1)
        streams[eng].append(lambda e: fn(e).then_inc(s, 1))
        return v

    def drain(eng):
        streams[eng].append(lambda e: e.drain())

    def ln_norm(src, dst, junk):
        st = lambda i: stats[:, i:i + 1]
        drain("vector")
        ev("vector", lambda e: e.reduce_sum(st(0), src, axis=AX))
        ev("vector", lambda e: e.scalar_tensor_tensor(junk, src, 1.0, src, op0=ALU.mult, op1=ALU.mult, accum_out=st(1)))
        drain("vector")
        ev("vector", lambda e: e.tensor_scalar(st(2), st(0), 1.0 / D, None, op0=ALU.mult))
        ev("vector", lambda e: e.tensor_scalar(st(3), st(1), 1.0 / D, None, op0=ALU.mult))
        drain("vector")
        ev("vector", lambda e: e.tensor_tensor(st(4), st(2), st(2), op=ALU.mult))
        drain("vector")
        vs = ev("vector", lambda e: e.tensor_sub(st(5), st(3), st(4)), sem="sv")
        W("scalar", "sv", vs)
        sq = ev("scalar", lambda e: e.activation(st(6), st(5), AF.Sqrt, bias=epsb[:, :]), sem="ss")
        W("vector", "ss", sq)
        ev("vector", lambda e: e.reciprocal(st(7), st(6)))
        drain("vector")
        ev("vector", lambda e: e.tensor_scalar(dst, src, st(2), st(7), op0=ALU.subtract, op1=ALU.mult))
        drain("vector")

    def free_ps(i):
        if ps_free[i] is not None:
            s, v = ps_free[i]
            W("tensor", s, v)
            ps_free[i] = None

    def mm_acc(slot, pairs, bias_rhs=None, n=512, sem=None):
        free_ps(slot)
        p = pss[slot].ap()[:, 0:n]
        ret = None
        for i, (l, r) in enumerate(pairs):
            first = (i == 0)
            last = (i == len(pairs) - 1) and bias_rhs is None
            fn = (lambda l=l, r=r, f=first, la=last: (lambda e: e.matmul(p, l, r, start=f, stop=la)))()
            if last and sem is not None:
                ret = ev("tensor", fn, sem=sem)
            else:
                ev("tensor", fn)
        if bias_rhs is not None:
            fn = (lambda br=bias_rhs: (lambda e: e.matmul(p, ones.ap()[:, 0:128], br, start=False, stop=True)))()
            if sem is not None:
                ret = ev("tensor", fn, sem=sem)
            else:
                ev("tensor", fn)
        return ret

    ev("vector", lambda e: e.memset(ones.ap(), 1.0))
    ev("vector", lambda e: e.memset(ones_col.ap(), 1.0))
    ev("vector", lambda e: e.memset(epsb.ap(), LN_EPS))
    ev("vector", lambda e: e.memset(tt.ap(), 0.0))
    drain("vector")
    ev("vector", lambda e: e.memset(tt.ap()[0:1, 4:8], 1.0))
    vz = ev("vector", lambda e: e.drain(), sem="sv")
    W("sync", "sv", vz)

    CHW = 1024

    def load_bf(dram, dst):
        R, Ccols = dram.shape
        return dma("sync", "sw", dst.rearrange("p (c o) -> p c o", o=Ccols),
                   dram.rearrange("(c p) o -> p c o", p=128))

    lc_state = {"k": 0, "dv": {0: None, 1: None}}

    def load_cast(dram, dst):
        R, Ccols = dram.shape
        nchunks = R // 128
        src_r = dram.rearrange("(c p) o -> p c o", p=128)
        HW2 = 512
        for c0 in range(nchunks):
            o0 = 0
            while o0 < Ccols:
                take = min(HW2, Ccols - o0)
                pb = lc_state["k"] % 2
                lc_state["k"] += 1
                if lc_state["dv"][pb] is not None:
                    W("sync", "ss", lc_state["dv"][pb])
                v = dma("sync", "sw" if pb == 0 else "sw1",
                        stg_f32[:, pb * 512: pb * 512 + take], src_r[:, c0, o0:o0 + take])
                W("scalar", "sw" if pb == 0 else "sw1", v)
                off = c0 * Ccols + o0
                dv = ev("scalar", (lambda take=take, off=off, dst=dst, pb=pb: lambda e: e.copy(
                    dst[:, off:off + take], stg_f32[:, pb * 512: pb * 512 + take]))(), sem="ss")
                lc_state["dv"][pb] = dv
                o0 += take

    load_cast(Wq, wq_b)
    load_cast(Wdn, wdn_b)
    load_cast(Wup, wup_b)
    load_cast(Wo, wo_b)
    load_cast(gateT, wg8_b.ap())
    W("sync", "ss")
    for (bsrc, bdst) in ((bq, bq_b), (bdn, bdn_b), (bup, bup_b), (bo, bo_b)):
        BW = bsrc.shape[1]
        for o0 in range(0, BW, 1024):
            tk = min(1024, BW - o0)
            v = dma("sync", "sw", stg_f32[0:1, 0:tk], bsrc[:, o0:o0 + tk])
            W("vector", "sw", v)
            ev("vector", lambda e, bdst=bdst, o0=o0, tk=tk: e.tensor_copy(bdst.ap()[:, o0:o0 + tk], stg_f32[0:1, 0:tk]))
            dv = ev("vector", lambda e: e.drain(), sem="sv")
            W("sync", "sv", dv)
    dma("sync", "sw", eh_s.ap(), eh_in)
    dma("sync", "sw", iotar_s.ap(), iotar_in)
    load_cast(LT_in, ltb.ap())
    load_cast(idm_in, idm_b.ap())
    wcast_s = ev("scalar", lambda e: e.copy(stats[0:1, 14:15], stg_f32[0:1, 0:1]), sem="ss")
    wcast = ev("vector", lambda e: e.drain(), sem="sv")
    for (fsrc, bdst) in ((WsgT, wsg_c), (WsuT, wsu_c), (WsdT, wsd_c),
                         (WgT[0], wg_c[0]), (WuT[0], wu_c[0]), (WdT[0], wd_c[0]),
                         (WgT[1], wg_c[1]), (WuT[1], wu_c[1]), (WdT[1], wd_c[1])):
        s_, v_ = INC("scg", 16)
        streams["gpsimd"].append((lambda fsrc=fsrc, bdst=bdst, s_=s_: lambda e: e.dma_start(
            out=bdst[:, :], in_=fsrc[:, :]).then_inc(s_, 16))())
    scg_all = C["scg"]

    SXS = ("sx0", "sx1", "sx2")
    xld = [None, None, None]
    for i in range(min(3, TILES)):
        xld[i] = dma("sync", SXS[i], xs[i][:, :], x_in[i * 128:(i + 1) * 128, :])
    kv_pend = [None]
    q_pend = [None]

    def emit_X(t):
        cur = xs[t % 3]
        qb = qbP[t % 2]
        W("vector", SXS[t % 3], xld[t % 3])
        W("vector", "sst")
        ln_norm(cur, xhat, pacc.ap())
        vx = ev("vector", lambda e: e.engine_nop(), sem="sv")
        W("scalar", "sv", vx)
        W("scalar", "sst")
        W("scalar", "stn", C["stn"])
        for k in range(8):
            tv = dma("scalar", "slq2", xhatT[:, k * 128:(k + 1) * 128],
                     xhat[:, k * 128:(k + 1) * 128], transpose=True)
        W("tensor", "slq2", tv)
        W("tensor", "sv", wcast)
        W("tensor", "ss", wcast_s)
        for hh in range(2):
            tq = mm_acc(hh, [(xhatT[:, k * 128:(k + 1) * 128],
                         wq_b[:, k * D + hh * 512: k * D + hh * 512 + 512]) for k in range(8)],
                   bias_rhs=bq_b.ap()[:, hh * 512: hh * 512 + 512], sem="stn")
        tl = mm_acc(2, [(xhatT[:, k * 128:(k + 1) * 128],
                    wdn_b[:, k * DLAT: (k + 1) * DLAT]) for k in range(8)],
               bias_rhs=bdn_b.ap()[:, :], n=DLAT, sem="stn")
        W("vector", "stn", tl)
        vq = ev("vector", lambda e: e.tensor_copy(latb[:, :], pss[2].ap()[:, 0:DLAT]), sem="sv")
        ps_free[2] = ("sv", vq)
        W("scalar", "sv", vq)
        for k in range(2):
            tv = dma("scalar", "slq2", latT[:, k * 128:(k + 1) * 128],
                     latb[:, k * 128:(k + 1) * 128], transpose=True)
        W("tensor", "slq2", tv)
        kvslot = (2, 3, 4, 5)
        for ch in range(4):
            tk2 = mm_acc(kvslot[ch], [(latT[:, k * 128:(k + 1) * 128],
                             wup_b[:, k * 2 * D + ch * 512: k * 2 * D + ch * 512 + 512]) for k in range(2)],
                   bias_rhs=bup_b.ap()[:, ch * 512: ch * 512 + 512], sem="stn")
        kv_pend[0] = (t, tk2)
        q_pend[0] = (t, tq)

    def emit_qkv_evac():
        if kv_pend[0] is None:
            return
        tn, tk2 = kv_pend[0]
        _, tq = q_pend[0]
        qb = qbP[tn % 2]
        kvb = kvbP[tn % 2]
        kvslot = (2, 3, 4, 5)
        W("vector", "stn", tq)
        ev("vector", lambda e, qb=qb: e.tensor_copy(qb[:, 0:512], pss[0].ap()[:, 0:512]))
        ev("vector", lambda e, qb=qb: e.tensor_copy(qb[:, 512:1024], pss[1].ap()[:, 0:512]))
        W("vector", "stn", tk2)
        for ch in range(4):
            ev("vector", lambda e, ch=ch, kvb=kvb, kvslot=kvslot: e.tensor_copy(
                kvb[:, ch * 512:(ch + 1) * 512], pss[kvslot[ch]].ap()[:, 0:512]))
        vkv = ev("vector", lambda e: e.drain(), sem="sv")
        ps_free[0] = ("sv", vkv); ps_free[1] = ("sv", vkv)
        for ch in range(4):
            ps_free[kvslot[ch]] = ("sv", vkv)
        kv_pend[0] = None

    def emit_Y(t):
        cur = xs[t % 3]
        qb = qbP[t % 2]
        kvb = kvbP[t % 2]
        W("vector", "sst")
        q3 = qb.rearrange("p (h d) -> p h d", d=DH)
        p3 = ptmp.ap().rearrange("p (h d) -> p h d", d=DH)
        a3 = pacc.ap().rearrange("p (h d) -> p h d", d=DH)
        sc3 = scf.rearrange("p (h g) -> p h g", g=H)
        for g in range(H):
            ksl = kvb[:, g * 128: g * 128 + DH].unsqueeze(1).to_broadcast([128, H, DH])
            ev("vector", lambda e, ksl=ksl, q3=q3: e.tensor_tensor(p3, q3, ksl, op=ALU.mult))
            drain("vector")
            ev("vector", lambda e, g=g: e.tensor_reduce(sc3[:, :, g:g + 1], p3, axis=AX, op=ALU.add))
            drain("vector")
        ev("vector", lambda e: e.reduce_max(stats[:, 8:9], scf, axis=AX))
        vmx = ev("vector", lambda e: e.drain(), sem="sv")
        W("scalar", "sv", vmx)
        ev("scalar", lambda e: e.mul(stats[:, 9:10], stats[:, 8:9], -0.125))
        drain("scalar")
        sexp = ev("scalar", lambda e: e.activation(Eb, scf, AF.Exp, bias=stats[:, 9:10], scale=0.125), sem="ss")
        W("vector", "ss", sexp)
        ev("vector", lambda e: e.tensor_reduce(stats2[:, 0:H], Eb.rearrange("p (h g) -> p h g", g=H), axis=AX, op=ALU.add))
        drain("vector")
        ev("vector", lambda e: e.reciprocal(stats2[:, H:2 * H], stats2[:, 0:H]))
        drain("vector")
        E3 = Eb.rearrange("p (h g) -> p h g", g=H)
        for g in range(H):
            esl = E3[:, :, g:g + 1].to_broadcast([128, H, DH])
            vsl = kvb[:, g * 128 + DH: g * 128 + 128].unsqueeze(1).to_broadcast([128, H, DH])
            if g == 0:
                ev("vector", lambda e, esl=esl, vsl=vsl: e.tensor_tensor(a3, esl, vsl, op=ALU.mult))
            else:
                ev("vector", lambda e, esl=esl, vsl=vsl: e.tensor_tensor(p3, esl, vsl, op=ALU.mult))
                drain("vector")
                ev("vector", lambda e: e.tensor_add(pacc.ap(), pacc.ap(), ptmp.ap()))
            drain("vector")
        ev("vector", lambda e: e.tensor_tensor(
            ob.rearrange("p (h d) -> p h d", d=DH), a3,
            stats2[:, H:2 * H].unsqueeze(2).to_broadcast([128, H, DH]), op=ALU.mult))
        vo = ev("vector", lambda e: e.drain(), sem="sv")
        emit_qkv_evac()
        W("scalar", "sv", vo)
        for k in range(8):
            tv = dma("scalar", "slq", obT[:, k * 128:(k + 1) * 128],
                     ob[:, k * 128:(k + 1) * 128], transpose=True)
        W("tensor", "slq", tv)
        for hh in range(2):
            tao = mm_acc(hh, [(obT[:, k * 128:(k + 1) * 128],
                         wo_b[:, k * D + hh * 512: k * D + hh * 512 + 512]) for k in range(8)],
                   bias_rhs=bo_b.ap()[:, hh * 512: hh * 512 + 512], sem="stn")
        W("vector", "stn", tao)
        ev("vector", lambda e: e.tensor_copy(aof[:, 0:512], pss[0].ap()[:, 0:512]))
        vao = ev("vector", lambda e: e.tensor_copy(aof[:, 512:1024], pss[1].ap()[:, 0:512]), sem="sv")
        ps_free[0] = ("sv", vao); ps_free[1] = ("sv", vao)
        ln_norm(aof, ao2.ap(), ptmp.ap())
        ev("vector", lambda e, cur=cur: e.tensor_add(xpf.ap(), cur, ao2.ap()))
        vxp = ev("vector", lambda e: e.drain(), sem="sv")
        W("sync", "sv", vxp)
        dma("sync", "sst", xp_dram[t * 128:(t + 1) * 128, :], xpf.ap())
        if d_xp is not None:
            dma("sync", "sst", d_xp[t * 128:(t + 1) * 128, :], xpf.ap())
        ln_norm(xpf.ap(), yhb, ptmp.ap())
        vyh = ev("vector", lambda e: e.engine_nop(), sem="sv")
        W("sync", "sv", vyh)
        dma("sync", "sst", ag1y_i[t * 128:(t + 1) * 128, :], yhb)
        if d_yh is not None:
            dma("sync", "sst", d_yh[t * 128:(t + 1) * 128, :], yhb)
        W("scalar", "sv", vyh)
        for k in range(8):
            tv = dma("scalar", "slq", yhT[:, k * 128:(k + 1) * 128],
                     yhb[:, k * 128:(k + 1) * 128], transpose=True)
        W("sync", "slq", tv)
        for k in range(8):
            dma("sync", "sst", yTd[k * 128:(k + 1) * 128, t * 128:(t + 1) * 128],
                yhT[:, k * 128:(k + 1) * 128])
        W("tensor", "slq", tv)
        tlg = mm_acc(2, [(yhT[:, k * 128:(k + 1) * 128],
                    wg8_b.ap()[:, k * E:(k + 1) * E]) for k in range(8)], n=E, sem="stn")
        W("vector", "stn", tlg)
        vlg = ev("vector", lambda e: e.tensor_copy(lgf.ap()[:, 0:E], pss[2].ap()[:, 0:E]), sem="sv")
        ps_free[2] = ("sv", vlg)
        drain("vector")
        ev("vector", lambda e: e.reduce_max(stats[:, 10:11], lgf.ap()[:, 0:E], axis=AX, negate=True))
        vng = ev("vector", lambda e: e.drain(), sem="sv")
        W("scalar", "sv", vng)
        sle = ev("scalar", lambda e: e.activation(lgf.ap()[:, 16:16 + E], lgf.ap()[:, 0:E], AF.Exp, bias=stats[:, 10:11]), sem="ss")
        W("vector", "ss", sle)
        ev("vector", lambda e: e.reduce_sum(stats[:, 11:12], lgf.ap()[:, 16:16 + E], axis=AX))
        drain("vector")
        ev("vector", lambda e: e.reciprocal(stats[:, 12:13], stats[:, 11:12]))
        drain("vector")
        ev("vector", lambda e: e.tensor_scalar(lgf.ap()[:, 0:E], lgf.ap()[:, 16:16 + E], stats[:, 12:13], None, op0=ALU.mult))
        vsc = ev("vector", lambda e: e.drain(), sem="sv")
        W("sync", "sv", vsc)
        dma("sync", "sst", scd[t * 128:(t + 1) * 128, :], lgf.ap()[:, 0:E])
        if t + 3 < TILES:
            xld[(t + 3) % 3] = dma("sync", SXS[(t + 3) % 3], xs[(t + 3) % 3][:, :],
                                   x_in[(t + 3) * 128:(t + 4) * 128, :])

    if "p0" in phases:
        emit_X(0)
        emit_qkv_evac()
        for t in range(TILES):
            if t + 1 < TILES:
                emit_X(t + 1)
            emit_Y(t)

    if "p1" in phases:
        W("gpsimd", "sst")
        ev("gpsimd", lambda e: e.collective_compute(
            "AllGather", ALU.bypass, replica_groups=[list(range(NC_N))],
            ins=[scd.opt()], outs=[sc_all.opt()]), sem="scc")
        ev("gpsimd", lambda e: e.collective_compute(
            "AllGather", ALU.bypass, replica_groups=[list(range(NC_N))],
            ins=[ag1y_i.opt()], outs=[y_all.opt()]), sem="scc")
        ag_sc, ag_yt = 1, 2

        last_veo = [None]

        def emit_glu_pass(ngroups, scale_cols, accum, yt_src, out_dram, wait_scc):
            prev_mm = [None, None]
            gtv = [None, None]
            for g in range(ngroups):
                pg = g % 2
                tok0 = g * GRP
                sytk = "syt0" if pg == 0 else "syt1"
                if prev_mm[pg] is not None:
                    W("scalar", "stn", prev_mm[pg])
                rnk = tok0 // RNK
                toff = tok0 % RNK
                if wait_scc and g < 2:
                    W("scalar", "scc", ag_yt)
                vy = dma("scalar", sytk,
                         yt_g[pg].rearrange("p (c t) -> p c t", c=8),
                         yt_src[rnk * D:(rnk + 1) * D, toff:toff + GRP].rearrange("(c p) t -> p c t", p=128))
                W("tensor", sytk, vy)
                for fc in range(16):
                    sA, sB = fc % 2, 2 + fc % 2
                    free_ps(sA); free_ps(sB)
                    for dc in range(8):
                        ev("tensor", (lambda sA=sA, dc=dc, fc=fc, pg=pg: lambda e: e.matmul(
                            pss[sA].ap()[:, 0:GRP],
                            wgt_b[:, dc * FF + fc * 128: dc * FF + (fc + 1) * 128],
                            yt_g[pg][:, dc * GRP:(dc + 1) * GRP],
                            start=(dc == 0), stop=(dc == 7)))())
                    tga = None
                    for dc in range(8):
                        fn = (lambda sB=sB, dc=dc, fc=fc, pg=pg: lambda e: e.matmul(
                            pss[sB].ap()[:, 0:GRP],
                            wut_b[:, dc * FF + fc * 128: dc * FF + (fc + 1) * 128],
                            yt_g[pg][:, dc * GRP:(dc + 1) * GRP],
                            start=(dc == 0), stop=(dc == 7)))()
                        if dc == 7:
                            tga = ev("tensor", fn, sem="stn")
                        else:
                            ev("tensor", fn)
                    if fc == 15:
                        prev_mm[pg] = tga
                    W("scalar", "stn", tga)
                    if gtv[fc % 2] is not None:
                        W("scalar", "sv", gtv[fc % 2])
                    sg1 = ev("scalar", (lambda fc=fc, sA=sA: lambda e: e.activation(
                        gt_b[fc % 2].ap(), pss[sA].ap()[:, 0:GRP], AF.Gelu))(), sem="ss")
                    W("vector", "ss", sg1)
                    vh = ev("vector", (lambda fc=fc, sB=sB: lambda e: e.tensor_tensor(
                        hb_g[:, fc * GRP:(fc + 1) * GRP], gt_b[fc % 2].ap(), pss[sB].ap()[:, 0:GRP], op=ALU.mult))(), sem="sv")
                    gtv[fc % 2] = vh
                    ps_free[sA] = ("sv", C["sv"])
                    ps_free[sB] = ("sv", C["sv"])
                W("tensor", "sv", C["sv"])
                assert not accum
                first_evac = True
                for ch in range(4):
                    marks = {}
                    for half in range(2):
                        slot = (ch * 2 + half) % 6
                        free_ps(slot)
                        for fc in range(16):
                            fn = (lambda slot=slot, fc=fc, ch=ch, half=half: lambda e: e.matmul(
                                pss[slot].ap()[:, 0:512],
                                hb_g[:, fc * GRP + ch * 128: fc * GRP + (ch + 1) * 128],
                                wdt_b[:, fc * D + half * 512: fc * D + (half + 1) * 512],
                                start=(fc == 0), stop=(fc == 15)))()
                            if fc == 15:
                                marks[half] = ev("tensor", fn, sem="stn")
                            else:
                                ev("tensor", fn)
                    W("vector", "stn", marks[1])
                    if first_evac:
                        W("vector", "sac")
                        first_evac = False
                    for half in range(2):
                        slot = (ch * 2 + half) % 6
                        dst = eo_b.ap()[:, ch * D + half * 512: ch * D + (half + 1) * 512]
                        if scale_cols is not None:
                            scol = scale_cols(g, ch)
                            if accum:
                                ev("vector", (lambda dst=dst, slot=slot, scol=scol, ch=ch, half=half: lambda e: e.scalar_tensor_tensor(
                                    dst, pss[slot].ap()[:, 0:512], scol,
                                    acb.ap()[:, ch * D + half * 512: ch * D + (half + 1) * 512],
                                    op0=ALU.mult, op1=ALU.add))())
                            else:
                                ev("vector", (lambda dst=dst, slot=slot, scol=scol: lambda e: e.tensor_scalar(
                                    dst, pss[slot].ap()[:, 0:512], scol, None, op0=ALU.mult))())
                        else:
                            ev("vector", (lambda dst=dst, slot=slot: lambda e: e.tensor_copy(
                                dst, pss[slot].ap()[:, 0:512]))())
                        drain("vector")
                    vch = ev("vector", lambda e: e.drain(), sem="sv")
                    ps_free[(ch * 2) % 6] = ("sv", vch)
                    ps_free[(ch * 2 + 1) % 6] = ("sv", vch)
                veo = vch
                last_veo[0] = veo
                W("sync", "sv", veo)
                dma("sync", "sac", out_dram[tok0:tok0 + GRP, :].rearrange("(c p) d -> p c d", p=128),
                    eo_b.ap().rearrange("p (c d) -> p c d", d=D))

        W("vector", "stn")
        W("vector", "slq")
        W("vector", "sst")
        W("vector", "sx0"); W("vector", "sx1"); W("vector", "sx2")
        W("sync", "scg", scg_all)
        W("sync", "stn", C["stn"])
        W("sync", "sv", C["sv"])
        W("sync", "ss", C["ss"])
        W("sync", "sst"); W("sync", "slq"); W("sync", "slq2")
        W("sync", "sx0"); W("sync", "sx1"); W("sync", "sx2")
        load_bf(wsg_c, wgt_b)
        load_bf(wsu_c, wut_b)
        wsh = load_bf(wsd_c, wdt_b)
        W("tensor", "sw", wsh)
        W("sync", "sst")
        W("scalar", "sst")
        emit_glu_pass(NGS, None, False, yTd, acc_sh, wait_scc=False)

        W("sync", "scc", ag_sc)
        vsl = None
        for bb in range(B):
            vsl = dma("sync", "ssc", sc_b[bb].rearrange("p (c e) -> p c e", e=E),
                      sc_all[bb * SB:(bb + 1) * SB, :].rearrange("(c p) e -> p c e", p=128))
        W("vector", "ssc", vsl)
        W("vector", "sw"); W("vector", "sw1"); W("vector", "ss")
        for j in range(4):
            bb = j // 2
            sc3v = sc_b[bb].rearrange("p (c e) -> p c e", e=E)
            ev("vector", (lambda j=j, sc3v=sc3v: lambda e: e.tensor_tensor(
                stg_f32.rearrange("p (c e) -> p c e", e=E)[:, 0:CHB, :], sc3v,
                eh_s.ap()[:, j * E:(j + 1) * E].unsqueeze(1).to_broadcast([128, CHB, E]), op=ALU.mult))())
            drain("vector")
            ev("vector", (lambda j=j: lambda e: e.tensor_reduce(
                scj.ap()[:, j * CHB:(j + 1) * CHB],
                stg_f32.rearrange("p (c e) -> p c e", e=E)[:, 0:CHB, :], axis=AX, op=ALU.add))())
            drain("vector")
        ev("vector", lambda e: e.drain(), sem="sv")

        tau_sl = [0, 16]
        tcm = None
        for it in range(NBIS):
            cur, nxt = tau_sl
            lo_c = tt.ap()[0:1, cur:cur + 4]
            hi_c = tt.ap()[0:1, cur + 4:cur + 8]
            lo_n = tt.ap()[0:1, nxt:nxt + 4]
            hi_n = tt.ap()[0:1, nxt + 4:nxt + 8]
            mid = tt.ap()[0:1, 8:12]
            cnt_f = tt.ap()[0:1, 12:16]
            ev("vector", (lambda mid=mid, lo_c=lo_c, hi_c=hi_c: lambda e: e.tensor_tensor(
                mid, lo_c, hi_c, op=ALU.add))())
            drain("vector")
            ev("vector", (lambda mid=mid: lambda e: e.tensor_scalar(
                mid, mid, 0.5, None, op0=ALU.mult))())
            drain("vector")
            ev("vector", (lambda mid=mid: lambda e: e.tensor_copy(
                bc1.ap().rearrange("o (p j) -> o p j", j=4),
                mid.unsqueeze(1).to_broadcast([1, 128, 4])))())
            vb = ev("vector", lambda e: e.drain(), sem="sv")
            W("sync", "sv", vb)
            vst = dma("sync", "stx", taux_d, bc1.ap()[0:1, :])
            W("sync", "stx", vst)
            vld = dma("sync", "stx", cand128.ap(), taux_d.rearrange("(p j) -> p j", p=128))
            W("vector", "stx", vld)
            if tcm is not None:
                W("vector", "stn", tcm)
            for j in range(4):
                ev("vector", (lambda j=j: lambda e: e.tensor_single_scalar(
                    mk_b.ap()[:, j * CHB:(j + 1) * CHB], scj.ap()[:, j * CHB:(j + 1) * CHB],
                    cand128.ap()[:, j:j + 1], op=ALU.is_ge))())
            vmk = ev("vector", lambda e: e.drain(), sem="sv")
            W("tensor", "sv", vmk)
            if pcc_free[0] is not None:
                W("tensor", pcc_free[0][0], pcc_free[0][1])
                pcc_free[0] = None
            tcm = ev("tensor", lambda e: e.matmul(pcc.ap()[0:1, 0:4 * CHB], ones_col.ap(),
                                                  mk_b.ap(), start=True, stop=True), sem="stn")
            W("vector", "stn", tcm)
            ev("vector", (lambda cnt_f=cnt_f: lambda e: e.tensor_reduce(
                cnt_f.rearrange("o (j x) -> o j x", x=1),
                pcc.ap()[0:1, 0:4 * CHB].rearrange("o (j c) -> o j c", c=CHB), axis=AX, op=ALU.add))())
            vrd = ev("vector", lambda e: e.drain(), sem="sv")
            pcc_free[0] = ("sv", C["sv"])
            ev("vector", (lambda cnt_f=cnt_f: lambda e: e.tensor_scalar(
                stg_f32[0:1, 0:4], cnt_f, float(CAP), None, op0=ALU.is_ge))())
            drain("vector")
            ev("vector", lambda e: e.tensor_copy(mk16.ap(), stg_f32[0:1, 0:4]))
            drain("vector")
            ev("vector", (lambda lo_n=lo_n, mid=mid, lo_c=lo_c: lambda e: e.select(
                lo_n, mk16.ap(), mid, lo_c, add_drain=True))())
            drain("vector")
            ev("vector", (lambda hi_n=hi_n, hi_c=hi_c, mid=mid: lambda e: e.select(
                hi_n, mk16.ap(), hi_c, mid, add_drain=True))())
            drain("vector")
            tau_sl = [nxt, cur]
        fin = tau_sl[0]
        ev("vector", (lambda fin=fin: lambda e: e.tensor_copy(
            bc1.ap().rearrange("o (p j) -> o p j", j=4),
            tt.ap()[0:1, fin:fin + 4].unsqueeze(1).to_broadcast([1, 128, 4])))())
        vb = ev("vector", lambda e: e.drain(), sem="sv")
        W("sync", "sv", vb)
        vst = dma("sync", "stx", taux_d, bc1.ap()[0:1, :])
        W("sync", "stx", vst)
        vld = dma("sync", "stx", tau128.ap(), taux_d.rearrange("(p j) -> p j", p=128))
        W("vector", "stx", vld)
        if d_tau is not None:
            vtt = ev("vector", lambda e: e.engine_nop(), sem="sv")
            W("sync", "sv", vtt)
            dma("sync", "sst", d_tau[:, :], tt.ap())
        for j in range(4):
            ev("vector", (lambda j=j: lambda e: e.tensor_single_scalar(
                wsc.ap()[:, j * CHB:(j + 1) * CHB], scj.ap()[:, j * CHB:(j + 1) * CHB],
                tau128.ap()[:, j:j + 1], op=ALU.is_ge))())
            drain("vector")
            ev("vector", (lambda j=j: lambda e: e.tensor_tensor(
                wsc.ap()[:, j * CHB:(j + 1) * CHB], wsc.ap()[:, j * CHB:(j + 1) * CHB],
                scj.ap()[:, j * CHB:(j + 1) * CHB], op=ALU.mult))())
            drain("vector")
        for j in range(4):
            ev("vector", (lambda j=j: lambda e: e.tensor_single_scalar(
                mk_b.ap()[:, j * CHB:(j + 1) * CHB], scj.ap()[:, j * CHB:(j + 1) * CHB],
                tau128.ap()[:, j:j + 1], op=ALU.is_ge))())
            drain("vector")
        vwsc = ev("vector", lambda e: e.drain(), sem="sv")
        if d_mkb is not None:
            W("sync", "sv", vwsc)
            dma("sync", "sst", d_mkb[:, :], mk_b.ap())
        if d_wsc is not None:
            W("sync", "sv", vwsc)
            dma("sync", "sst", d_wsc[:, :], wsc.ap())
        if d_scj is not None:
            W("sync", "sv", vwsc)
            dma("sync", "sst", d_scj[:, :], scj.ap())
        W("sync", "sv", vwsc)

        stgbf = stg.ap().bitcast(BF16)

        NCG = SB // 1024
        SLT = NCG * 256
        MGN = SLT // 512

        def emit_routed(le):
            for bb in range(B):
                j = 2 * bb + le
                mkj = mk_b.ap()[:, j * CHB:(j + 1) * CHB]
                free_ps(4)
                if pcc_free[0] is not None:
                    W("tensor", pcc_free[0][0], pcc_free[0][1])
                    pcc_free[0] = None
                W("tensor", "sv", C["sv"])
                t1 = ev("tensor", (lambda mkj=mkj: lambda e: e.matmul(pss[4].ap()[:, 0:CHB], ltb.ap(),
                                                     mkj, start=True, stop=True))(), sem="stn")
                t2 = ev("tensor", (lambda mkj=mkj: lambda e: e.matmul(pcc.ap()[0:1, 0:CHB], ones_col.ap(),
                                                     mkj, start=True, stop=True))(), sem="stn")
                W("vector", "stn", t2)
                W("vector", "ss")
                if d_poss is not None:
                    W("vector", "sst")
                ev("vector", (lambda mkj=mkj: lambda e: e.tensor_sub(posc.ap()[:, 0:CHB], pss[4].ap()[:, 0:CHB], mkj))())
                drain("vector")
                ra = stg_f32[0:1, 0:CHB]
                rb = stg_f32[0:1, 64:64 + CHB]
                rc = stg_f32[0:1, 128:128 + CHB]
                ev("vector", lambda e: e.tensor_copy(ra, pcc.ap()[0:1, 0:CHB]))
                drain("vector")
                ev("vector", lambda e: e.memset(rb, 0.0))
                drain("vector")
                ev("vector", lambda e: e.tensor_copy(rb[:, 1:CHB], ra[:, 0:CHB - 1]))
                drain("vector")
                cur, oth = rb, ra
                for sh in (1, 2, 4, 8, 16, 32):
                    if sh >= CHB:
                        break
                    ev("vector", (lambda oth=oth, cur=cur, sh=sh: lambda e: e.tensor_copy(oth[:, 0:sh], cur[:, 0:sh]))())
                    drain("vector")
                    ev("vector", (lambda oth=oth, cur=cur, sh=sh: lambda e: e.tensor_add(
                        oth[:, sh:CHB], cur[:, sh:CHB], cur[:, 0:CHB - sh]))())
                    drain("vector")
                    cur, oth = oth, cur
                gv = cur.rearrange("o (g k) -> o g k", k=8)
                gs = gv[:, :, 0:1].to_broadcast([1, CHB // 8, 8])
                ev("vector", (lambda cur=cur, gv=gv, gs=gs: lambda e: e.tensor_sub(
                    rc.rearrange("o (g k) -> o g k", k=8), gv, gs))())
                vps = ev("vector", lambda e: e.drain(), sem="sv")
                ps_free[4] = ("sv", vps)
                pcc_free[0] = ("sv", vps)
                W("sync", "sv", vps)
                vst = dma("sync", "stx", pcb_d.rearrange("(p c) -> p c", c=CHB).unsqueeze(0),
                          rc.unsqueeze(1).to_broadcast([1, 128, CHB]))
                W("sync", "stx", vst)
                vld = dma("sync", "stx", cb128.ap()[:, 0:CHB], pcb_d.rearrange("(p c) -> p c", c=CHB))
                W("vector", "stx", vld)
                ev("vector", lambda e: e.tensor_add(posc.ap()[:, 0:CHB], posc.ap()[:, 0:CHB], cb128.ap()[:, 0:CHB]))
                drain("vector")
                ev("vector", lambda e: e.tensor_scalar(cb128.ap()[:, 0:CHB], posc.ap()[:, 0:CHB], 256.0, 2048.0, op0=ALU.is_ge, op1=ALU.mult))
                drain("vector")
                ev("vector", lambda e: e.tensor_add(posc.ap()[:, 0:CHB], posc.ap()[:, 0:CHB], cb128.ap()[:, 0:CHB]))
                drain("vector")
                ev("vector", (lambda mkj=mkj: lambda e: e.tensor_scalar(cb128.ap()[:, 0:CHB], mkj, -4096.0, 4096.0, op0=ALU.mult, op1=ALU.add))())
                drain("vector")
                ev("vector", lambda e: e.tensor_add(poss.ap()[:, 0:CHB], posc.ap()[:, 0:CHB], cb128.ap()[:, 0:CHB]))
                vposs = ev("vector", lambda e: e.drain(), sem="sv")
                if d_poss is not None:
                    W("sync", "sv", vposs)
                    dma("sync", "sst", d_poss[j * 128:(j + 1) * 128, 0:CHB], poss.ap()[:, 0:CHB])
                    dma("sync", "sst", d_poss[j * 128:(j + 1) * 128, 64:64 + CHB], posc.ap()[:, 0:CHB])
                    dma("sync", "sst", d_poss[j * 128:(j + 1) * 128, 128:128 + CHB], cb128.ap()[:, 0:CHB])

                W("sync", "sv", vposs)
                W("sync", "ss")
                W("sync", "scc", ag_yt)
                W("sync", "stn", C["stn"])
                W("scalar", "stn", C["stn"])
                W("vector", "sac")
                pstrip = stgbf
                pbrd = [None]
                gytr = [None, None]
                for cg in range(NCG):
                    for ss in range(4):
                        free_ps(ss)
                    if cg > 0 or (bb > 0 or le > 0):
                        W("sync", "stn", C["stn"])
                    vyb = None
                    for sb in range(8):
                        c8 = cg * 8 + sb
                        vyb = dma("sync", "syt0", hb2[:, sb * 1024:(sb + 1) * 1024],
                                  y_all[bb * SB + c8 * 128: bb * SB + (c8 + 1) * 128, :])
                    if pbrd[0] is not None:
                        W("vector", "stn", pbrd[0])
                    W("vector", "sw"); W("vector", "sw1"); W("vector", "ss")
                    for sb in range(8):
                        for ob in range(2):
                            ev("vector", (lambda sb=sb, ob=ob, c=cg * 8 + sb: lambda e: e.tensor_single_scalar(
                                pstrip[:, (sb * 2 + ob) * 128:(sb * 2 + ob + 1) * 128],
                                iotar_s.ap()[:, ob * 128:(ob + 1) * 128],
                                poss.ap()[:, c:c + 1], op=ALU.is_equal))())
                    vpb = ev("vector", lambda e: e.drain(), sem="sv")
                    W("tensor", "sv", vpb)
                    W("tensor", "syt0", vyb)
                    for sb in range(8):
                        for ob in range(2):
                            for half in range(2):
                                fn = (lambda ob=ob, half=half, sb=sb: lambda e: e.matmul(
                                    pss[ob * 2 + half].ap()[:, 0:512],
                                    pstrip[:, (sb * 2 + ob) * 128:(sb * 2 + ob + 1) * 128],
                                    hb2[:, sb * 1024 + half * 512: sb * 1024 + (half + 1) * 512],
                                    start=(sb == 0), stop=(sb == 7)))()
                                if sb == 7 and ob == 1 and half == 1:
                                    pbrd[0] = ev("tensor", fn, sem="stn")
                                else:
                                    ev("tensor", fn)
                    for ob in range(2):
                        po = cg * 2 + ob
                        W("vector", "stn", pbrd[0])
                        if gytr[ob] is not None:
                            W("vector", "slq", C["slq"])
                        ev("vector", (lambda ob=ob: lambda e: e.tensor_copy(
                            eo_b.ap()[:, ob * 1024: ob * 1024 + 512], pss[ob * 2].ap()[:, 0:512]))())
                        drain("vector")
                        vg = ev("vector", (lambda ob=ob: lambda e: e.tensor_copy(
                            eo_b.ap()[:, ob * 1024 + 512: (ob + 1) * 1024], pss[ob * 2 + 1].ap()[:, 0:512]))(), sem="sv")
                        ps_free[ob * 2] = ("sv", vg)
                        ps_free[ob * 2 + 1] = ("sv", vg)
                        W("scalar", "sv", vg)
                        for dc in range(8):
                            tvx = dma("scalar", "slq", xTs[:, dc * SLT + po * 128: dc * SLT + (po + 1) * 128],
                                      eo_b.ap()[:, ob * 1024 + dc * 128: ob * 1024 + (dc + 1) * 128], transpose=True)
                        gytr[ob] = tvx

                W("tensor", "slq", C["slq"])
                gtv2 = [None, None]
                pb2rd = [None]
                pb2rd_d = {}
                ptt_ev = {}
                last_sc_ev = [None]

                def emit_A_fc(mg, fc):
                    sA, sB2 = fc % 2, 2 + fc % 2
                    free_ps(sA); free_ps(sB2)
                    for dc in range(8):
                        ev("tensor", (lambda sA=sA, dc=dc, fc=fc, mg=mg: lambda e: e.matmul(
                            pss[sA].ap()[:, 0:GRP],
                            wgt_b[:, dc * FF + fc * 128: dc * FF + (fc + 1) * 128],
                            xTs[:, dc * SLT + mg * 512: dc * SLT + (mg + 1) * 512],
                            start=(dc == 0), stop=(dc == 7)))())
                    tga = None
                    for dc in range(8):
                        fn = (lambda sB2=sB2, dc=dc, fc=fc, mg=mg: lambda e: e.matmul(
                            pss[sB2].ap()[:, 0:GRP],
                            wut_b[:, dc * FF + fc * 128: dc * FF + (fc + 1) * 128],
                            xTs[:, dc * SLT + mg * 512: dc * SLT + (mg + 1) * 512],
                            start=(dc == 0), stop=(dc == 7)))()
                        if dc == 7:
                            tga = ev("tensor", fn, sem="stn")
                        else:
                            ev("tensor", fn)
                    W("scalar", "stn", tga)
                    if gtv2[fc % 2] is not None:
                        W("scalar", "sv", gtv2[fc % 2])
                    sg1 = ev("scalar", (lambda fc=fc, sA=sA: lambda e: e.activation(
                        gt_b[fc % 2].ap(), pss[sA].ap()[:, 0:GRP], AF.Gelu))(), sem="ss")
                    W("vector", "ss", sg1)
                    vh = ev("vector", (lambda fc=fc, sB2=sB2: lambda e: e.tensor_tensor(
                        hb2[:, fc * GRP:(fc + 1) * GRP], gt_b[fc % 2].ap(), pss[sB2].ap()[:, 0:GRP], op=ALU.mult))(), sem="sv")
                    gtv2[fc % 2] = vh
                    ps_free[sA] = ("sv", C["sv"])
                    ps_free[sB2] = ("sv", C["sv"])

                def emit_B(mg):
                    W("tensor", "sv", C["sv"])
                    for ch in range(4):
                        marks = {}
                        for half in range(2):
                            slot = (ch * 2 + half) % 6
                            free_ps(slot)
                            for fc in range(16):
                                fn = (lambda slot=slot, fc=fc, ch=ch, half=half: lambda e: e.matmul(
                                    pss[slot].ap()[:, 0:512],
                                    hb2[:, fc * GRP + ch * 128: fc * GRP + (ch + 1) * 128],
                                    wdt_b[:, fc * D + half * 512: fc * D + (half + 1) * 512],
                                    start=(fc == 0), stop=(fc == 15)))()
                                if fc == 15:
                                    marks[half] = ev("tensor", fn, sem="stn")
                                else:
                                    ev("tensor", fn)
                        W("vector", "stn", marks[1])
                        for half in range(2):
                            slot = (ch * 2 + half) % 6
                            ev("vector", (lambda ch=ch, half=half, slot=slot: lambda e: e.tensor_copy(
                                eoI[:, ch * 1024 + half * 512: ch * 1024 + (half + 1) * 512],
                                pss[slot].ap()[:, 0:512]))())
                            drain("vector")
                        vch = ev("vector", lambda e: e.drain(), sem="sv")
                        ps_free[(ch * 2) % 6] = ("sv", vch)
                        ps_free[(ch * 2 + 1) % 6] = ("sv", vch)

                strip_v = [None]

                def emit_strip(cg):
                    if pb2rd[0] is not None:
                        W("vector", "stn", C["stn"])
                    W("vector", "sw"); W("vector", "sw1"); W("vector", "ss")
                    for tbl in range(8):
                        for ob in range(2):
                            ev("vector", (lambda tbl=tbl, ob=ob, c=cg * 8 + tbl: lambda e: e.tensor_single_scalar(
                                pstrip[:, (tbl * 2 + ob) * 128:(tbl * 2 + ob + 1) * 128],
                                iotar_s.ap()[:, ob * 128:(ob + 1) * 128],
                                poss.ap()[:, c:c + 1], op=ALU.is_equal))())
                    strip_v[0] = ev("vector", lambda e: e.drain(), sem="sv")

                def emit_scatter_tb(mg, cgl, tbl):
                    cg = 2 * mg + cgl
                    c = cg * 8 + tbl
                    free_ps(4); free_ps(5)
                    emark = {}
                    for ob in range(2):
                        k = tbl * 2 + ob
                        p2 = k % 4
                        if k == 0:
                            W("tensor", "sv", strip_v[0])
                        if ptt_ev.get(0) is not None:
                            W("tensor", "sv", ptt_ev[0])
                        tmk = ev("tensor", (lambda k=k: lambda e: e.transpose(
                            ptt.ap(), pstrip[:, k * 128:(k + 1) * 128], idm_b.ap()))(), sem="stn")
                        W("vector", "stn", tmk)
                        if pb2rd_d.get(p2) is not None:
                            W("vector", "stn", pb2rd_d[p2])
                        emark[k] = ev("vector", (lambda p2=p2: lambda e: e.tensor_copy(
                            Pb2[p2].ap(), ptt.ap()))(), sem="sv")
                        ptt_ev[0] = emark[k]
                    for ob in range(2):
                        k = tbl * 2 + ob
                        p2 = k % 4
                        W("tensor", "sv", emark[k])
                        for half in range(2):
                            fn = (lambda p2=p2, half=half, cgl=cgl, ob=ob: lambda e: e.matmul(
                                pss[4 + half].ap()[:, 0:512],
                                Pb2[p2].ap(),
                                eoI[:, (cgl * 2 + ob) * 1024 + half * 512: (cgl * 2 + ob) * 1024 + (half + 1) * 512],
                                start=(ob == 0), stop=(ob == 1)))()
                            if half == 1:
                                smark = ev("tensor", fn, sem="stn")
                            else:
                                ev("tensor", fn)
                        pb2rd_d[p2] = smark
                    pb2rd[0] = smark
                    if le == 1:
                        if last_sc_ev[0] is not None:
                            W("sync", "sv", last_sc_ev[0])
                        va = dma("sync", "sar", acb.ap()[:, 0:D],
                                 acc_d[bb * SB + c * 128: bb * SB + (c + 1) * 128, :])
                    W("vector", "stn", smark)
                    W("vector", "sac")
                    if le == 1:
                        W("vector", "sar", va)
                    scol = wsc.ap()[:, j * CHB + c: j * CHB + c + 1]
                    for half in range(2):
                        dst = eo_b.ap()[:, 2048 + half * 512: 2048 + (half + 1) * 512]
                        if le == 1:
                            ev("vector", (lambda dst=dst, half=half, scol=scol: lambda e: e.scalar_tensor_tensor(
                                dst, pss[4 + half].ap()[:, 0:512], scol,
                                acb.ap()[:, half * 512:(half + 1) * 512],
                                op0=ALU.mult, op1=ALU.add))())
                        else:
                            ev("vector", (lambda dst=dst, half=half, scol=scol: lambda e: e.tensor_scalar(
                                dst, pss[4 + half].ap()[:, 0:512], scol, None, op0=ALU.mult))())
                        drain("vector")
                    veo = ev("vector", lambda e: e.drain(), sem="sv")
                    last_sc_ev[0] = veo
                    ps_free[4] = ("sv", veo)
                    ps_free[5] = ("sv", veo)
                    W("sync", "sv", veo)
                    dma("sync", "sac", acc_d[bb * SB + c * 128: bb * SB + (c + 1) * 128, :],
                        eo_b.ap()[:, 2048:3072])

                for fc in range(16):
                    emit_A_fc(0, fc)
                for mg in range(MGN):
                    emit_B(mg)
                    nxt = mg + 1 if mg + 1 < MGN else None
                    i = 0
                    for cgl in range(2):
                        emit_strip(2 * mg + cgl)
                        for tbl in range(8):
                            emit_scatter_tb(mg, cgl, tbl)
                            if nxt is not None and i < 16:
                                emit_A_fc(nxt, i)
                            i += 1

        for le in range(2):
            W("sync", "stn", C["stn"])
            load_bf(wg_c[le], wgt_b)
            load_bf(wu_c[le], wut_b)
            wexp = load_bf(wd_c[le], wdt_b)
            W("tensor", "sw", wexp)
            emit_routed(le)

        W("gpsimd", "sac")
        ev("gpsimd", lambda e: e.collective_compute(
            "ReduceScatter", ALU.add, replica_groups=[list(range(NC_N))],
            ins=[acc_d.opt()], outs=[rs_out.opt()]), sem="scc")

    if "p2" in phases:
        W("sync", "scc", C["scc"])
        for t in range(TILES):
            base = t * 128
            W("vector", "sfin")
            v0 = dma("sync", "sar", at[0], rs_out[base:base + 128, :])
            v1 = dma("sync", "sar", at[1], acc_sh[base:base + 128, :])
            vxp2 = dma("sync", "sar", xpf.ap(), xp_dram[base:base + 128, :])
            W("vector", "sar", vxp2)
            W("vector", "sac")
            ev("vector", lambda e: e.tensor_add(ot2, at[0], at[1]))
            ln_norm(ot2, ao2.ap(), pacc.ap())
            ev("vector", lambda e: e.tensor_add(ao2.ap(), ao2.ap(), xpf.ap()))
            vfin = ev("vector", lambda e: e.drain(), sem="sv")
            W("sync", "sv", vfin)
            dma("sync", "sfin", out[base:base + 128, :], ao2.ap())
        W("sync", "sfin")
        W("gpsimd", "sfin")

    blk = es.enter_context(nc.Block())
    for eng in ("sync", "scalar", "vector", "tensor", "gpsimd"):
        fns = streams[eng]
        getattr(blk, eng)(lambda e, fns=fns: [f(e) for f in fns])
    es.close()
    return nc, dbg


def _prep_inputs(cfg, inputs):
    SB, B, TC = cfg["SB"], cfg["B"], cfg["TC"]
    L = "L0_"
    f32 = lambda a: np.ascontiguousarray(np.asarray(a, dtype=np.float32))
    x = f32(inputs[L + "attn_x"]).reshape(B * SB, D)
    Wq = f32(inputs[L + "attn_Wq"]); bq = f32(inputs[L + "attn_bq"]).reshape(1, D)
    Wdn = f32(inputs[L + "attn_Wdown"]); bdn = f32(inputs[L + "attn_bdown"]).reshape(1, DLAT)
    Wup = f32(inputs[L + "attn_Wup"]); bup = f32(inputs[L + "attn_bup"]).reshape(1, 2 * D)
    Wo = f32(inputs[L + "attn_Wout"]); bo = f32(inputs[L + "attn_bout"]).reshape(1, D)
    gateT = np.ascontiguousarray(f32(inputs[L + "ffn_gate_w"]).T)
    WsgT = np.ascontiguousarray(f32(inputs[L + "ffn_Ws_g"]).T)
    WsuT = np.ascontiguousarray(f32(inputs[L + "ffn_Ws_u"]).T)
    WsdT = np.ascontiguousarray(f32(inputs[L + "ffn_Ws_d"]).T)
    Wg = f32(inputs[L + "ffn_Wg"])
    Wu = f32(inputs[L + "ffn_Wu"])
    Wd = f32(inputs[L + "ffn_Wd"])
    LT = np.tril(np.ones((128, 128), np.float32)).T
    iotar = np.tile(np.arange(256, dtype=np.float32), (128, 1))
    idm = np.eye(128, dtype=np.float32)
    maps = []
    for c in range(NC_N):
        eh = np.zeros((128, 4 * E), np.float32)
        for j in range(4):
            eh[:, j * E + 2 * c + (j % 2)] = 1.0
        m = {
            "x": x[c * TC:(c + 1) * TC],
            "Wq": Wq, "bq": bq, "Wdn": Wdn, "bdn": bdn,
            "Wup": Wup, "bup": bup, "Wo": Wo, "bo": bo,
            "gateT": gateT, "WsgT": WsgT, "WsuT": WsuT, "WsdT": WsdT,
            "eh_in": eh, "LT_in": LT, "iotar_in": iotar, "idm_in": idm,
        }
        for le in range(2):
            eg = 2 * c + le
            m[f"Wg{le}T"] = np.ascontiguousarray(Wg[eg].T)
            m[f"Wu{le}T"] = np.ascontiguousarray(Wu[eg].T)
            m[f"Wd{le}T"] = np.ascontiguousarray(Wd[eg].T)
        maps.append(m)
    return maps


def kernel(**inputs):
    cfg = make_cfg()
    nc, _dbg = build_nc(cfg)
    in_maps = _prep_inputs(cfg, inputs)
    res = run_bass_kernel_spmd(nc, in_maps, core_ids=list(range(NC_N)))
    outs = [np.asarray(res.results[c]["out"]) for c in range(NC_N)]
    full = np.concatenate(outs, axis=0)
    return full.reshape(cfg["B"], cfg["SB"], D)
